# version 27
# baseline (speedup 1.0000x reference)
import contextlib
import numpy as np
import ml_dtypes
import concourse.bass as bass
import concourse.mybir as mybir
from concourse.bass_utils import run_bass_kernel_spmd

F32 = mybir.dt.float32
BF16 = mybir.dt.bfloat16
AF = mybir.ActivationFunctionType
ALU = mybir.AluOpType
AX = mybir.AxisListType

NCORE = 8
TOK = 2048
NCH = 16
E = 1024
NSL = 8
NH = 16
EPS_LN = 1e-5
EPS_ATT = 1e-6
C1 = 1572864.0
ENGS = ("sync", "tensor", "vector", "scalar", "gpsimd")


class B:

    def __init__(self, nc):
        self.nc = nc
        self.cnt = {}
        self.sems = {}
        self.items = None
        self.stack = contextlib.ExitStack()

    def sem(self, name):
        self.sems[name] = self.stack.enter_context(self.nc.semaphore(name))
        self.cnt[name] = 0

    def begin(self):
        self.items = {e: [] for e in ENGS}

    _SELF = {"vector": "sV", "scalar": "sA", "gpsimd": "sPL"}

    def emit(self, eng, op, *args, waits=(), inc=None, amt=1, **kwargs):
        ssem = self._SELF.get(eng)
        if ssem is not None and inc == ssem and op != "dma_start":
            waits = tuple(waits) + ((ssem, self.cnt[ssem]),)
        if inc is not None:
            self.cnt[inc] += amt
        self.items[eng].append(
            (op, args, kwargs, tuple((s, v) for (s, v) in waits if v > 0),
             inc, amt))
        return self.cnt[inc] if inc is not None else None

    def dma(self, eng, sem, out, in_, waits=(), **kw):
        return self.emit(eng, "dma_start", waits=waits, inc=sem, amt=16,
                         out=out, in_=in_, **kw)

    def end(self):
        items, sems = self.items, self.sems
        with self.nc.Block() as blk:
            for eng in ENGS:
                if not items[eng]:
                    continue

                def body(e, _its=tuple(items[eng])):
                    for (op, args, kwargs, waits, inc, amt) in _its:
                        for (s, v) in waits:
                            e.wait_ge(sems[s], v)
                        if op == "wait_ge":
                            e.wait_ge(sems[args[0]], args[1])
                            continue
                        ins = getattr(e, op)(*args, **kwargs)
                        if inc is not None:
                            ins.then_inc(sems[inc], amt)
                getattr(blk, eng)(body)
        self.items = None


def build(dbg=False, upto=99, s5n=16, s5parts='asny'):
    nc = bass.Bass("TRN2", target_bir_lowering=False)

    def din(name, shape, dt=F32):
        return nc.dram_tensor(name, list(shape), dt, kind="ExternalInput")

    xs = din("xs", (TOK, E))
    WqT = din("WqT", (E, E)); WkT = din("WkT", (E, E))
    WvT = din("WvT", (E, E)); WpT = din("WpT", (E, E))
    bWkT = din("bWkT", (E, E)); bWvT = din("bWvT", (E, E))
    bWrT = din("bWrT", (E, E)); bWoT = din("bWoT", (E, E))
    ffnw = din("ffnw", (E, E))
    bp_row = din("bp_row", (1, E)); fb_row = din("fb_row", (1, E))
    mu_k = din("mu_k_pf", (128, NSL)); mu_v = din("mu_v_pf", (128, NSL))
    mu_r = din("mu_r_pf", (128, NSL)); u_pf = din("u_pf", (128, NSL))
    mask_in = din("mask_sc", (128, 128))
    prevm = din("prevm", (128, NCORE))

    yout = nc.dram_tensor("yout", [TOK, E], F32, kind="ExternalOutput")

    xn1_d = nc.dram_tensor("xn1_d", [TOK, E], BF16)
    phk_d = nc.dram_tensor("phk_d", [TOK, E], BF16)
    phq_d = nc.dram_tensor("phq_d", [128, NSL * TOK], BF16)
    y_d = nc.dram_tensor("y_d", [TOK, E], BF16)
    xd_d = nc.dram_tensor("xd_d", [TOK, E], BF16)
    xn3_d = nc.dram_tensor("xn3_d", [TOK, E], BF16)
    wd_d = nc.dram_tensor("wd_d", [E, E], BF16)
    r2_d = nc.dram_tensor("r2_d", [TOK, E], F32)
    r3_d = nc.dram_tensor("r3_d", [TOK, E], F32)
    cc_in = nc.dram_tensor("cc_in", [128, 1040], BF16)
    cc_out = nc.dram_tensor("cc_out", [128 * NCORE, 1040], BF16)
    cc_in0 = nc.dram_tensor("cc_in0", [128, 520], BF16)
    cc_out0 = nc.dram_tensor("cc_out0", [128 * NCORE, 520], BF16)
    cc_in1 = nc.dram_tensor("cc_in1", [128, 520], BF16)
    cc_out1 = nc.dram_tensor("cc_out1", [128 * NCORE, 520], BF16)

    dbg_list = []
    if dbg:
        for nm, src in (("d_xn1", xn1_d), ("d_phk", phk_d), ("d_phq", phq_d),
                        ("d_y", y_d), ("d_xd", xd_d), ("d_xn3", xn3_d),
                        ("d_wd", wd_d), ("d_r2", r2_d), ("d_r3", r3_d)):
            dbg_list.append((nc.dram_tensor(nm, list(src.shape), src.dtype,
                                            kind="ExternalOutput"), src))

    b = B(nc)
    for s in ("sY", "sX0", "sX1", "sP0", "sP1", "sP2", "sST0", "sST1",
              "sSW0", "sSW1", "sQ0", "sQ1",
              "sK0", "sK1", "sK2", "sK3", "sCI", "sGA", "sGB", "sCC",
              "sT1", "sT2", "sT3", "sT4", "sT5", "sTW",
              "sPE", "sV", "sA", "sPL"):
        b.sem(s)
    E_ = b.emit

    def mk_sb(stack):
        def sb(name, cols, dt=BF16):
            return stack.enter_context(nc.sbuf_tensor(name, [128, cols], dt))
        return sb


    def finish():
        b.begin()
        for _s in list(b.sems):
            if _s in ("sPE", "sV", "sA", "sPL", "sCC"):
                continue
            E_("sync", "wait_ge", _s, b.cnt[_s])
        b.end()
        return nc
    stS = b.stack
    sbS = mk_sb(stS)

    mask_sb = sbS("mask_sb", 128, F32)
    muk_sb = sbS("muk_sb", NSL, F32); muv_sb = sbS("muv_sb", NSL, F32)
    mur_sb = sbS("mur_sb", NSL, F32); u_sb = sbS("u_sb", NSL, F32)
    eu_sb = sbS("eu_sb", NSL, F32)
    pm_sb = sbS("pm_sb", NCORE, F32)
    ones_t = stS.enter_context(nc.sbuf_tensor("ones_t", [1, 128], BF16))
    stt_s = sbS("stt_s", 128, F32)
    stt_q = sbS("stt_q", 64, F32)
    stm = sbS("stm", 64, F32)
    str_ = sbS("str_", 64, F32)
    sc_a = sbS("sc_a", E, F32)
    sc_b = sbS("sc_b", E, F32)
    sc_c = sbS("sc_c", E, F32)
    sc_d = sbS("sc_d", E, F32)
    sc_d2 = sbS("sc_d2", E, F32)
    sc_j = sbS("sc_j", E, F32)
    qs_t = sbS("qs_t", 32, F32)
    rs_t = sbS("rs_t", 32, F32)
    stb = sbS("stb", 64, F32)
    x_sb = [sbS("x_sb0", E, F32), sbS("x_sb1", E, F32),
            sbS("x_sb2", E, F32)]
    xq_sb = [sbS("xq0", E), sbS("xq1", E)]
    WA = sbS("WA", NSL * E)
    WB = sbS("WB", NSL * E)
    W3 = sbS("W3", NSL * E)

    PS = stS.enter_context(nc.psum_tensor("PS", [128, 4096], F32))
    P = [PS[:, 512 * i:512 * (i + 1)] for i in range(8)]

    def mm(out, lhsT, rhs, start, stop, waits=()):
        E_("tensor", "matmul", out, lhsT, rhs, start=start, stop=stop,
           skip_group_check=True, waits=waits, inc="sPE")

    def wslab(Wt):
        return Wt[:, :].rearrange("p (s o) -> p s o", o=E)

    def wsrc(dram):
        return dram.ap().rearrange("(s p) o -> p s o", p=128)

    lnq_xn = {0: 0, 1: 0}
    lnq_guard = {0: 0, 1: 0}
    lnq_pl = {0: 0, 1: 0}
    xq_half = {}
    pe_half = {}
    xq_store = {0: 0, 1: 0}
    out_store = {0: 0, 1: 0}
    xbp_done = {0: 0, 1: 0, 2: 0}
    srcrd = {0: ("sV", 0), 1: ("sV", 0)}
    qguard_v, qguard_pl, qtail = {}, {}, {}
    qguard_sb = [0]
    qguard_cx = [0]
    qst = {}

    def ln_pA(tc, p, qsets, ey=()):
        sa, sb_, scx, qs, rs = qsets[p][:5]
        gk = qsets[p][5]
        dxn = sc_d if p == 0 else sc_d2
        x3 = dxn[:, :].rearrange("p (k m) -> p k m", m=32)
        E_("vector", "tensor_reduce", qs[:, :], x3, AX.X, ALU.max,
           apply_absolute_value=True,
           waits=[("sA", lnq_xn[p])], inc="sV")
        E_("vector", "tensor_scalar", qs[:, :], qs[:, :], 1.0 / 7.5,
           1e-30, ALU.mult, ALU.max, inc="sV")
        E_("vector", "reciprocal", rs[:, :], qs[:, :], inc="sV")
        rdv = b.cnt["sV"]
        E_("gpsimd", "tensor_tensor",
           sa[:, :].rearrange("p (k m) -> p k m", m=32), x3,
           rs[:, :].unsqueeze(2).broadcast_to([128, 32, 32]), ALU.mult,
           waits=[("sV", rdv)] + list(ey),
           inc="sPL")
        yv = b.cnt["sPL"]
        lnq_guard[p] = rdv
        E_("scalar", "activation", sb_[:, :], sa[:, :], AF.Abs,
           waits=[("sPL", yv)], inc="sA")
        qst.setdefault(gk, {}).update(ya=b.cnt["sA"], yv=yv)

    def ln_pB(tc, p, qsets):
        sa, sb_, scx, qs, rs = qsets[p][:5]
        gk = qsets[p][5]
        st = qst[gk]
        E_("vector", "tensor_scalar", scx[:, :], sb_[:, :], 2.0, C1,
           ALU.is_ge, ALU.mult,
           waits=[("sA", st["ya"]), ("sPL", qguard_cx[0])], inc="sV")
        E_("vector", "tensor_scalar", sb_[:, :], sb_[:, :], 4.0, 2.0 * C1,
           ALU.is_ge, ALU.mult, inc="sV")
        E_("vector", "scalar_tensor_tensor", scx[:, :], scx[:, :], C1,
           sb_[:, :], ALU.add, ALU.add, inc="sV")
        E_("gpsimd", "tensor_tensor", sb_[:, :], sa[:, :], scx[:, :],
           ALU.add, waits=[("sV", b.cnt["sV"])], inc="sPL")
        E_("gpsimd", "tensor_tensor", sa[:, :], sb_[:, :], scx[:, :],
           ALU.subtract, inc="sPL")
        qguard_sb[0] = b.cnt["sPL"]
        qguard_cx[0] = b.cnt["sPL"]
        st["pl"] = b.cnt["sPL"]

    def ln_pC(tc, p, out_dram, qsets):
        sa, sb_, scx, qs, rs = qsets[p][:5]
        gk = qsets[p][5]
        st = qst[gk]
        E_("vector", "tensor_tensor",
           xq_sb[p][:, :].rearrange("p (k m) -> p k m", m=32),
           sa[:, :].rearrange("p (k m) -> p k m", m=32),
           qs[:, :].unsqueeze(2).broadcast_to([128, 32, 32]), ALU.mult,
           waits=[("sPL", st["pl"]),
                  ("sST%d" % p, xq_store.get(p, 0))], inc="sV")
        st["xdv"] = b.cnt["sV"]
        b.dma("scalar", "sST%d" % p,
              out_dram.ap()[128 * tc:128 * tc + 128, :],
              xq_sb[p][:, :], waits=[("sV", b.cnt["sV"])])
        xq_store[p] = b.cnt["sST%d" % p]


    def _quant(src_f32, dst_bf, xn_wait=None, st_wait=None, par=None,
               sset=None):
        sa, sb_, scx, qs, rs = (sc_a, sc_b, sc_c, qs_t, rs_t) if sset is None             else sset[:5]
        x3 = src_f32[:, :].rearrange("p (k m) -> p k m", m=32)
        if st_wait is None:
            entry = [("sST0", b.cnt["sST0"]), ("sST1", b.cnt["sST1"])]
        else:
            entry = [st_wait]
        gk = None if sset is None else sset[5]
        if gk is not None:
            entry += [("sV", qguard_v.get(gk, 0)),
                      ("sPL", qguard_pl.get(gk, 0))]
        else:
            entry += [("sPL", b.cnt["sPL"])]
        entry += [("sA", b.cnt["sA"] if xn_wait is None else xn_wait)]
        E_("vector", "tensor_reduce", qs[:, :], x3, AX.X, ALU.max,
           apply_absolute_value=True, waits=entry, inc="sV")
        E_("vector", "tensor_scalar", qs[:, :], qs[:, :], 1.0 / 7.5,
           1e-30, ALU.mult, ALU.max, inc="sV")
        E_("vector", "reciprocal", rs[:, :], qs[:, :], inc="sV")
        E_("vector", "tensor_tensor",
           sa[:, :].rearrange("p (k m) -> p k m", m=32), x3,
           rs[:, :].unsqueeze(2).broadcast_to([128, 32, 32]), ALU.mult,
           inc="sV")
        yv = ("sV", b.cnt["sV"])
        if par is not None:
            srcrd[par] = yv
        abs_w = [yv]
        if gk is not None:
            abs_w.append(("sPL", qguard_sb[0]))
        E_("scalar", "activation", sb_[:, :], sa[:, :], AF.Abs,
           waits=abs_w, inc="sA")
        ya = b.cnt["sA"]
        t1_w = [("sA", ya)]
        if gk is not None:
            t1_w.append(("sPL", abs_w[1][1]))
        E_("vector", "tensor_scalar", scx[:, :], sb_[:, :], 2.0, C1,
           ALU.is_ge, ALU.mult, waits=t1_w, inc="sV")
        E_("vector", "tensor_scalar", sb_[:, :], sb_[:, :], 4.0, 2.0 * C1,
           ALU.is_ge, ALU.mult, inc="sV")
        E_("vector", "scalar_tensor_tensor", scx[:, :], scx[:, :], C1,
           sb_[:, :], ALU.add, ALU.add, inc="sV")
        chv_ = b.cnt["sV"]
        if gk is not None:
            E_("gpsimd", "tensor_tensor", sb_[:, :], sa[:, :], scx[:, :],
               ALU.add, waits=[("sV", chv_)], inc="sPL")
            E_("gpsimd", "tensor_tensor", sa[:, :], sb_[:, :], scx[:, :],
               ALU.subtract, inc="sPL")
            qguard_sb[0] = b.cnt["sPL"]
            qtail[gk] = (b.cnt["sPL"], sa, qs, dst_bf)
            return
        E_("vector", "tensor_tensor", sb_[:, :], sa[:, :], scx[:, :],
           ALU.add, waits=[yv], inc="sV")
        E_("vector", "tensor_tensor", sa[:, :], sb_[:, :], scx[:, :],
           ALU.subtract, inc="sV")
        E_("vector", "tensor_tensor",
           dst_bf[:, :].rearrange("p (k m) -> p k m", m=32),
           sa[:, :].rearrange("p (k m) -> p k m", m=32),
           qs[:, :].unsqueeze(2).broadcast_to([128, 32, 32]), ALU.mult,
           inc="sV")

    def _quant_tail(gk):
        pl, sa, qs, dst_bf = qtail[gk]
        E_("vector", "tensor_tensor",
           dst_bf[:, :].rearrange("p (k m) -> p k m", m=32),
           sa[:, :].rearrange("p (k m) -> p k m", m=32),
           qs[:, :].unsqueeze(2).broadcast_to([128, 32, 32]), ALU.mult,
           waits=[("sPL", pl)], inc="sV")
        qguard_v[gk] = b.cnt["sV"]
        qguard_pl[gk] = qtail[gk][0]

    st1 = contextlib.ExitStack()
    sb1 = mk_sb(st1)
    A1 = sb1("A1", NSL * TOK)
    vbuf = sb1("vbuf", NCH * 1040)
    ring = sb1("ring", 3 * E)
    pq_out = [sb1("pq_out0", 512), sb1("pq_out1", 512)]
    pq_ring = [sb1("pq_ring%d" % _i, NSL * 256) for _i in range(3)]
    S_loc = sb1("S_loc", NCH * 520)
    s_base = sb1("s_base", 1040, F32)
    s_gt = sb1("s_gt", 1040)
    s_tot = sb1("s_tot", 1040)
    S_op = [sb1("S_op0", 520), sb1("S_op1", 520)]
    A_sb = [sb1("A_sb0", 512), sb1("A_sb1", 512)]

    y_sb = [sb1("y_sb0", E), sb1("y_sb1", E)]
    dn_sb = sb1("dn_sb", 16, F32)
    rd_sb = sb1("rd_sb", 16, F32)

    xnT = A1
    phikT = A1

    b.begin()
    for (dst, src) in ((mask_sb, mask_in), (muk_sb, mu_k), (muv_sb, mu_v),
                       (mur_sb, mu_r), (u_sb, u_pf), (pm_sb, prevm)):
        b.dma("sync", "sY", dst[:, :], src.ap())
    w0 = b.cnt["sY"]
    E_("scalar", "activation", eu_sb[:, :], u_sb[:, :], AF.Exp,
       waits=[("sY", w0)], inc="sA")

    E_("vector", "memset", vbuf[:, :], 1.0, inc="sV")
    v_ones = b.cnt["sV"]
    E_("vector", "memset", ones_t[0:1, :], 1.0, inc="sV")
    b.dma("gpsimd", "sGA", wslab(WA), wsrc(WqT))
    wq_ld = b.cnt["sGA"]
    b.dma("gpsimd", "sGB", wslab(WB), wsrc(WkT))
    wk_ld = b.cnt["sGB"]
    b.end()
    if upto <= 0:
        return finish()

    b.begin()
    xn1_st = {}
    s1_apply = {0: 0, 1: 0, 2: 0}
    s1sem = ["sX0", "sX1", "sP0"]
    for c in range(NCH):
        p = c % 3
        qn = "sync" if c % 2 == 0 else "scalar"
        b.dma(qn, s1sem[p], x_sb[p][:, :],
              xs.ap()[128 * c:128 * c + 128, :],
              waits=[("sA", s1_apply[p])] if c >= 3 else ())
        lw = [(s1sem[p], b.cnt[s1sem[p]])]
        E_("vector", "tensor_scalar", sc_a[:, :], x_sb[p][:, :], 1.0, None,
           ALU.mult, ALU.add, accum_out=stt_s[:, c:c + 1],
           waits=lw, inc="sV")
        E_("scalar", "activation", sc_j[:, :], x_sb[p][:, :], AF.Square,
           accum_out=stt_q[:, c:c + 1], waits=lw, inc="sA")
        wa = b.cnt["sA"]
        cc = slice(c, c + 1)
        E_("vector", "tensor_scalar_mul", stm[:, cc], stt_s[:, cc],
           1.0 / E, inc="sV")
        E_("vector", "tensor_scalar_mul", str_[:, cc], stt_q[:, cc],
           1.0 / E, waits=[("sA", wa)], inc="sV")
        E_("vector", "tensor_tensor", stb[:, cc], stm[:, cc],
           stm[:, cc], ALU.mult, inc="sV")
        E_("vector", "scalar_tensor_tensor", str_[:, cc], str_[:, cc],
           float(EPS_LN), stb[:, cc], ALU.add, ALU.subtract, inc="sV")
        E_("scalar", "activation", str_[:, cc], str_[:, cc],
           AF.Ln, waits=[("sV", b.cnt["sV"])], inc="sA")
        E_("scalar", "activation", str_[:, cc], str_[:, cc],
           AF.Exp, scale=-0.5, inc="sA")
        E_("vector", "scalar_tensor_tensor", stb[:, cc], stm[:, cc],
           -1.0, str_[:, cc], ALU.mult, ALU.mult,
           waits=[("sA", b.cnt["sA"])], inc="sV")
        pq = c % 2
        E_("scalar", "activation", xq_sb[pq][:, :], x_sb[p][:, :],
           AF.Identity,
           bias=stb[:, c:c + 1], scale=str_[:, c:c + 1],
           waits=[("sV", b.cnt["sV"]),
                  ("sSW%d" % pq, b.cnt["sSW%d" % pq])],
           inc="sA")
        s1_apply[p] = b.cnt["sA"]
        b.dma("gpsimd", "sSW%d" % pq, xn1_d.ap()[128 * c:128 * c + 128, :],
              xq_sb[pq][:, :], waits=[("sA", b.cnt["sA"])])
        xn1_st[c] = (b.cnt["sSW0"], b.cnt["sSW1"])
    b.end()
    if upto <= 1:
        return finish()

    b.begin()
    for h in range(2):
        for s in range(NSL):
            b.dma("sync", "sT1",
                  xnT[:, s * TOK + 1024 * h:s * TOK + 1024 * h + 1024],
                  xn1_d.ap()[1024 * h:1024 * h + 1024,
                             128 * s:128 * s + 128], transpose=True,
                  waits=[("sSW0", xn1_st[8 * h + 7][0]),
                         ("sSW1", xn1_st[8 * h + 7][1]),
                         ("sT1", b.cnt["sT1"])] if s == 0 else ())
    xnT_h1 = 8 * 16
    xnT_ok = b.cnt["sT1"]

    evacV, evacA = {}, {}
    prev_exp = 0
    prev_tt = ("sV", 0)
    for oc in range(NSL):
        for tt in range(4):
            bank = (oc * 4 + tt) % 4
            po = (oc * 4 + tt) % 2
            wl = [("sGA", wq_ld), ("sT1", xnT_h1 if tt < 2 else xnT_ok)]
            if bank in evacV:
                wl += [evacV[bank], ("sA", evacA[bank])]
            for s in range(NSL):
                mm(P[bank][:, :],
                   WA[:, s * E + 128 * oc: s * E + 128 * oc + 128],
                   xnT[:, s * TOK + 512 * tt: s * TOK + 512 * tt + 512],
                   start=(s == 0), stop=(s == NSL - 1),
                   waits=wl if s == 0 else ())
            pe_c = b.cnt["sPE"]
            E_("scalar", "activation", sc_a[:, 0:512], P[bank][:, :],
               AF.Relu, scale=-1.0,
               waits=[("sPE", pe_c), prev_tt], inc="sA")
            E_("scalar", "activation", sc_b[:, 0:512], sc_a[:, 0:512], AF.Exp,
               scale=-1.0, inc="sA")
            prev_exp = b.cnt["sA"]
            E_("scalar", "activation", sc_b[:, 512:1024], P[bank][:, :],
               AF.Relu, inc="sA")
            evacA[bank] = b.cnt["sA"]
            E_("vector", "tensor_tensor", pq_out[po][:, :], sc_b[:, 0:512],
               sc_b[:, 512:1024], ALU.add,
               waits=[("sA", evacA[bank]), ("sQ%d" % po, b.cnt["sQ%d" % po])],
               inc="sV")
            evacV[bank] = ("sV", b.cnt["sV"])
            prev_tt = ("sV", b.cnt["sV"])
            b.dma("sync", "sQ%d" % po,
                  phq_d.ap().rearrange("p (c s x) -> p c s x", s=NSL, x=128)[
                      :, 4 * tt:4 * tt + 4, oc, :],
                  pq_out[po][:, :].rearrange("p (c x) -> p c x", x=128),
                  waits=[prev_tt])
    q_pe_done = b.cnt["sPE"]
    phq_all_st = (b.cnt["sQ0"], b.cnt["sQ1"])

    b.dma("gpsimd", "sGA", wslab(WA), wsrc(WvT),
          waits=[("sPE", q_pe_done), ("sGA", b.cnt["sGA"])])
    wv_ld = b.cnt["sGA"]

    sS_first = set()
    phk_half = {}
    s2_pe_half = {}
    ring_pe = {}
    kevacV = {}
    vevacA = {}
    snapshots = {}
    phk_store = {}

    def kv_mms(c):
        bch, n = c // 8, c % 8
        for h in range(NH):
            j = 8 * bch + h // 2
            bank = 4 + j // 4
            col = 128 * (j % 4)
            half = 64 * (h % 2)
            first = (bank, half) not in sS_first
            if first:
                sS_first.add((bank, half))
            wlk = []
            if h == 0:
                wlk = [("sV", kevacV[c]), ("sA", vevacA[c])]
                if n > 0:
                    wlk.append(("sV", snapshots[(bch, n - 1)]))
            mm(P[4 + j // 4][half:half + 64, col:col + 65],
               ring[:, (c % 3) * E + 64 * h:(c % 3) * E + 64 * h + 64],
               vbuf[:, c * 1040 + 65 * h: c * 1040 + 65 * h + 65],
               start=first, stop=(n == 7), waits=wlk)
        return b.cnt["sPE"]

    def snapshot(c):
        bch, n = c // 8, c % 8
        pe_c = b.cnt["sPE"]
        for half in range(2):
            bank = P[4 + 2 * bch + half]
            src = bank[:, 0:512].rearrange("p (j x) -> p j x", x=128)[:, :, 0:65]
            if n < 7:
                dst = S_loc[:, c * 520 + half * 260: c * 520 + half * 260 + 260]
            else:
                dst = s_tot[:, bch * 520 + half * 260:
                            bch * 520 + half * 260 + 260]
            E_("vector", "tensor_copy",
               dst.rearrange("p (j e) -> p j e", e=65), src,
               waits=[("sPE", pe_c)] if half == 0 else (), inc="sV")
        snapshots[(bch, n)] = b.cnt["sV"]

    for c in range(NCH):
        for oh in range(2):
            bank = oh
            wl = [("sGB", wk_ld), ("sT1", xnT_h1 if c < 8 else xnT_ok)]
            if bank in evacV:
                wl += [evacV[bank], ("sA", evacA[bank])]
            for s in range(NSL):
                mm(P[bank][:, :],
                   xnT[:, s * TOK + 128 * c: s * TOK + 128 * c + 128],
                   WB[:, s * E + 512 * oh: s * E + 512 * oh + 512],
                   start=(s == 0), stop=(s == NSL - 1),
                   waits=wl if s == 0 else ())
            pe_c = b.cnt["sPE"]
            wl2 = [("sPE", pe_c), prev_tt]
            if c >= 3 and oh == 0:
                wl2.append(("sPE", ring_pe[c % 3]))
            E_("scalar", "activation", sc_a[:, 0:512], P[bank][:, :],
               AF.Relu, scale=-1.0, waits=wl2, inc="sA")
            E_("scalar", "activation", sc_b[:, 0:512], sc_a[:, 0:512],
               AF.Exp, scale=-1.0, inc="sA")
            prev_exp = b.cnt["sA"]
            E_("scalar", "activation", sc_b[:, 512:1024], P[bank][:, :],
               AF.Relu, inc="sA")
            evacA[bank] = b.cnt["sA"]
            wlr = [("sA", evacA[bank])]
            if c >= 3 and oh == 0:
                wlr.append(("sK%d" % (c % 3), phk_store[c - 3]))
            E_("vector", "tensor_tensor",
               ring[:, (c % 3) * E + 512 * oh:(c % 3) * E + 512 * oh + 512],
               sc_b[:, 0:512], sc_b[:, 512:1024], ALU.add,
               waits=wlr, inc="sV")
            evacV[bank] = ("sV", b.cnt["sV"])
            prev_tt = ("sV", b.cnt["sV"])
        kevacV[c] = b.cnt["sV"]
        b.dma("gpsimd", "sK%d" % (c % 3), phk_d.ap()[128 * c:128 * c + 128, :],
              ring[:, (c % 3) * E:(c % 3) * E + E],
              waits=[("sV", kevacV[c])])
        phk_store[c] = b.cnt["sK%d" % (c % 3)]
        for oh in range(2):
            bank = 2 + oh
            wl = [("sGA", wv_ld), ("sT1", xnT_h1 if c < 8 else xnT_ok),
                  ("sV", v_ones)]
            if bank in evacA:
                wl += [("sA", evacA[bank])]
            for s in range(NSL):
                mm(P[bank][:, :],
                   xnT[:, s * TOK + 128 * c: s * TOK + 128 * c + 128],
                   WA[:, s * E + 512 * oh: s * E + 512 * oh + 512],
                   start=(s == 0), stop=(s == NSL - 1),
                   waits=wl if s == 0 else ())
            pe_c = b.cnt["sPE"]
            dst = vbuf[:, c * 1040 + 520 * oh: c * 1040 + 520 * oh + 520]
            E_("scalar", "activation",
               dst.rearrange("p (h e) -> p h e", e=65)[:, :, 0:64],
               P[bank][:, :].rearrange("p (h e) -> p h e", e=64),
               AF.Copy, waits=[("sPE", pe_c)], inc="sA")
            evacA[bank] = b.cnt["sA"]
        vevacA[c] = b.cnt["sA"]
        if c >= 1:
            ring_pe[(c - 1) % 3] = kv_mms(c - 1)
            snapshot(c - 1)
        if c == 8:
            phk_half[0] = [b.cnt["sK%d" % i] for i in range(3)]
            s2_pe_half[0] = b.cnt["sPE"]
    ring_pe[(NCH - 1) % 3] = kv_mms(NCH - 1)
    snapshot(NCH - 1)
    s2_pe = b.cnt["sPE"]
    phk_half[1] = [b.cnt["sK%d" % i] for i in range(3)]
    s2_pe_half[1] = s2_pe
    phk_all_st = [b.cnt["sK%d" % i] for i in range(4)]
    b.end()
    if upto <= 2:
        return finish()

    b.begin()
    b.dma("scalar", "sY", cc_in0.ap(), s_tot[:, 0:520],
          waits=[("sV", snapshots[(0, 7)])])
    ccin0_st = b.cnt["sY"]
    b.dma("scalar", "sCI", cc_in1.ap(), s_tot[:, 520:1040],
          waits=[("sV", snapshots[(1, 7)])])
    ccin_st = b.cnt["sCI"]
    b.dma("gpsimd", "sGB", wslab(WB), wsrc(WpT),
          waits=[("sPE", s2_pe), ("sGB", b.cnt["sGB"])])
    wp_ld = b.cnt["sGB"]
    b.dma("gpsimd", "sGA", wslab(WA), wsrc(bWkT),
          waits=[("sPE", s2_pe), ("sGA", b.cnt["sGA"])])
    bk_ld = b.cnt["sGA"]
    E_("gpsimd", "collective_compute", "AllGather", ALU.bypass,
       replica_groups=[list(range(NCORE))],
       ins=[cc_in0.ap().opt()], outs=[cc_out0.ap().opt()],
       waits=[("sY", ccin0_st)], inc="sCC")
    E_("gpsimd", "collective_compute", "AllGather", ALU.bypass,
       replica_groups=[list(range(NCORE))],
       ins=[cc_in1.ap().opt()], outs=[cc_out1.ap().opt()],
       waits=[("sCI", ccin_st)], inc="sCC")
    E_("vector", "memset", s_base[:, :], 0.0, inc="sV")
    for cp in range(NCORE):
        b.dma("scalar", "sP0", s_gt[:, 0:520],
              cc_out0.ap()[128 * cp:128 * cp + 128, :],
              waits=([("sCC", 1)] if cp == 0 else []) + [("sV", b.cnt["sV"])])
        E_("vector", "scalar_tensor_tensor", s_base[:, 0:520],
           s_gt[:, 0:520],
           pm_sb[:, cp:cp + 1], s_base[:, 0:520], ALU.mult, ALU.add,
           waits=[("sP0", b.cnt["sP0"])], inc="sV")
    sbase_ok0 = b.cnt["sV"]

    Q4 = {0: (sc_a, sc_b, sc_c, qs_t, rs_t, "s4e"),
          1: (sc_d, sc_b, sc_c, stm[:, 0:32], stm[:, 32:64], "s4o")}
    xq_store[0] = b.cnt["sST0"]
    xq_store[1] = b.cnt["sST1"]
    l4 = {}

    def q4A(oc):
        p = oc % 2
        sa, sb_, scx, qs, rs = Q4[p][:5]
        gk = Q4[p][5]
        x3 = x_sb[p][:, :].rearrange("p (k m) -> p k m", m=32)
        E_("vector", "tensor_reduce", qs[:, :], x3, AX.X, ALU.max,
           apply_absolute_value=True, waits=[l4[oc]], inc="sV")
        E_("vector", "tensor_scalar", qs[:, :], qs[:, :], 1.0 / 7.5,
           1e-30, ALU.mult, ALU.max, inc="sV")
        E_("vector", "reciprocal", rs[:, :], qs[:, :], inc="sV")
        rdv = b.cnt["sV"]
        E_("gpsimd", "tensor_tensor",
           sa[:, :].rearrange("p (k m) -> p k m", m=32), x3,
           rs[:, :].unsqueeze(2).broadcast_to([128, 32, 32]), ALU.mult,
           waits=[("sV", rdv)], inc="sPL")
        yv = b.cnt["sPL"]
        s4y[p] = yv
        E_("scalar", "activation", sb_[:, :], sa[:, :], AF.Abs,
           waits=[("sPL", yv)], inc="sA")
        qst.setdefault(gk, {}).update(ya=b.cnt["sA"], yv=yv)

    s4y = {0: 0, 1: 0}
    for oc in range(NSL):
        p = oc % 2
        b.dma("scalar", "sP%d" % p, x_sb[p][:, :],
              ffnw.ap()[128 * oc:128 * oc + 128, :],
              waits=[("sPL", s4y[p]), ("sA", b.cnt["sA"])]
              if oc >= 2 else [("sV", b.cnt["sV"]), ("sA", b.cnt["sA"])])
        l4[oc] = ("sP%d" % p, b.cnt["sP%d" % p])
        if oc >= 3:
            ln_pC(oc - 3, (oc - 3) % 2, wd_d, Q4)
        if oc >= 2:
            ln_pB(oc - 2, (oc - 2) % 2, Q4)
        if oc >= 1:
            q4A(oc - 1)
    ln_pC(NSL - 3, (NSL - 3) % 2, wd_d, Q4)
    ln_pB(NSL - 2, (NSL - 2) % 2, Q4)
    q4A(NSL - 1)
    ln_pC(NSL - 2, (NSL - 2) % 2, wd_d, Q4)
    ln_pB(NSL - 1, (NSL - 1) % 2, Q4)
    ln_pC(NSL - 1, (NSL - 1) % 2, wd_d, Q4)
    E_("vector", "engine_nop", waits=[("sPL", b.cnt["sPL"])], inc="sV")
    wd_st = (b.cnt["sST0"], b.cnt["sST1"])
    for s in range(NSL):
        b.dma("scalar", "sTW", W3[:, s * E:(s + 1) * E],
              wd_d.ap()[:, 128 * s:128 * s + 128], transpose=True,
              waits=[("sST0", wd_st[0]), ("sST1", wd_st[1])]
              if s == 0 else ())
    w3_ok = b.cnt["sTW"]

    for cp in range(NCORE):
        b.dma("scalar", "sP0", s_gt[:, 0:520],
              cc_out1.ap()[128 * cp:128 * cp + 128, :],
              waits=([("sCC", 2)] if cp == 0 else []) + [("sV", b.cnt["sV"])])
        E_("vector", "scalar_tensor_tensor", s_base[:, 520:1040],
           s_gt[:, 0:520],
           pm_sb[:, cp:cp + 1], s_base[:, 520:1040], ALU.mult, ALU.add,
           waits=[("sP0", b.cnt["sP0"])], inc="sV")
    sbase_ok = b.cnt["sV"]
    b.end()
    if upto <= 3:
        return finish()

    b.begin()
    for h in range(2):
        for s in range(NSL):
            b.dma("sync", "sT2",
                  phikT[:, s * TOK + 1024 * h:s * TOK + 1024 * h + 1024],
                  phk_d.ap()[1024 * h:1024 * h + 1024,
                             128 * s:128 * s + 128], transpose=True,
                  waits=[("sK0", phk_half[h][0]), ("sK1", phk_half[h][1]),
                         ("sK2", phk_half[h][2]), ("sT2", b.cnt["sT2"]),
                         ("sPE", s2_pe_half[h])] if s == 0 else ())
    phikT_h1 = 8 * 16
    phikT_ok = b.cnt["sT2"]
    for _r in pq_ring:
        E_("vector", "memset", _r[:, :], 0.0, inc="sV")
    pqz_zero = b.cnt["sV"]

    aevac, nevac, aevac_pl, ydone_prev = {}, {}, {}, {}
    y_half = {}
    s5p1_pe_h1 = 0
    sop_pe = {}
    y_store = {}
    pq_ld = {}
    sop_okd = {}
    snap_last = snapshots[(1, 7)]
    chunks = [8 * bch + n for bch in range(2) for n in range(8)][:s5n]
    if "a" in s5parts:
        glist = [(c, g) for c in chunks for g in range(4)]
    else:
        glist = []

    def pq_load(ci):
        c = chunks[ci]
        slot = ci % 3
        wlq = [("sQ0", phq_all_st[0]), ("sQ1", phq_all_st[1]),
               ("sV", pqz_zero)]
        if ci >= 3:
            wlq.append(("sPE", sop_pe[chunks[ci - 3]]))
        pqv = pq_ring[slot][:, :].rearrange("p (s two x) -> p s two x",
                                            two=2, x=128)
        phv = phq_d.ap().rearrange("p (c s x) -> p c s x", s=NSL, x=128)[
            :, c, :, :]
        sem = "sP%d" % (ci % 3)
        b.dma("sync", sem, pqv[0:64, :, 0, :], phv[0:64],
              waits=wlq + [(sem, b.cnt[sem])])
        b.dma("sync", sem, pqv[64:128, :, 1, :], phv[64:128])
        pq_ld[c] = (sem, b.cnt[sem])

    def chunk_head(ci, c):
        bch, n = c // 8, c % 8
        if ci == 0:
            for cj in range(min(3, len(chunks))):
                pq_load(cj)
        elif ci + 2 < len(chunks):
            pq_load(ci + 2)
        wl = [("sV", sbase_ok if bch else sbase_ok0)]
        if ci >= 2:
            wl.append(("sPE", sop_pe[chunks[ci - 2]]))
        spar = c % 2
        if n == 0:
            E_("vector", "tensor_copy", S_op[spar][:, :],
               s_base[:, bch * 520:bch * 520 + 520], waits=wl, inc="sV")
        else:
            E_("vector", "tensor_tensor", S_op[spar][:, :],
               s_base[:, bch * 520:bch * 520 + 520],
               S_loc[:, (c - 1) * 520:(c - 1) * 520 + 520],
               ALU.add, waits=wl, inc="sV")
        sop_okd[c] = b.cnt["sV"]

    def emit_A(idx):
        c, g = glist[idx]
        par = chunks.index(c) % 3
        ab = 4 + (idx % 4)
        wlA = [pq_ld[c], ("sV", snap_last),
               ("sT2", phikT_h1 if c < 8 else phikT_ok)]
        if idx >= 4:
            wlA.append(aevac[idx - 4])
        for j in range(4):
            h = 4 * g + j
            s = h // 2
            blk = 2 * s + (h % 2)
            mm(P[ab][:, :][:, 128 * j:128 * j + 128],
               phikT[:, s * TOK + 128 * c: s * TOK + 128 * c + 128],
               pq_ring[par][:, 128 * blk:128 * blk + 128],
               start=(j == 0), stop=(j == 3),
               waits=wlA if j == 0 else ())
        peA = b.cnt["sPE"]
        E_("vector", "tensor_tensor",
           A_sb[idx % 2][:, :].rearrange("p (j x) -> p j x", x=128),
           P[ab][:, :][:, 0:512].rearrange("p (j x) -> p j x", x=128),
           mask_sb[:, :].unsqueeze(1).broadcast_to([128, 4, 128]),
           ALU.mult, waits=[("sPE", peA)], inc="sV")
        aevac[idx] = ("sV", b.cnt["sV"])
        aevac_pl[idx] = b.cnt["sV"]

    def emit_num(idx):
        c, g = glist[idx]
        par = chunks.index(c) % 3
        wlN = [("sV", aevac_pl[idx]), ("sV", sop_okd[c]),
               ("sA", vevacA[c])]
        if g == 0 and c in ydone_prev:
            wlN.append(("sV", ydone_prev[c]))
        for j in range(4):
            h = 4 * g + j
            s = h // 2
            blk = 2 * s + (h % 2)
            mm(PS[:, 128 * h:128 * h + 65],
               A_sb[idx % 2][:, 128 * j:128 * j + 128],
               vbuf[:, c * 1040 + 65 * h: c * 1040 + 65 * h + 65],
               start=(h % 4 == 0), stop=False,
               waits=wlN if j == 0 else ())
            mm(PS[:, 128 * h:128 * h + 65],
               pq_ring[par][:, 128 * blk:128 * blk + 128],
               S_op[c % 2][:, 65 * s:65 * s + 65],
               start=False, stop=(j == 3))
        if g == 3:
            peN = b.cnt["sPE"]
            sop_pe[c] = peN
            pv = PS[:, 0:2048].rearrange("p (h x) -> p h x", x=128)
            E_("vector", "tensor_scalar_add", dn_sb[:, :],
               pv[:, :, 64:65], float(EPS_ATT), waits=[("sPE", peN)],
               inc="sV")
            E_("vector", "reciprocal", rd_sb[:, :], dn_sb[:, :], inc="sV")
            wly = []
            ci = chunks.index(c)
            if ci >= 2:
                wly.append(y_store[chunks[ci - 2]])
            E_("vector", "tensor_tensor",
               y_sb[c % 2][:, :].rearrange("p (h x) -> p h x", x=64),
               pv[:, :, 0:64],
               rd_sb[:, :].unsqueeze(2).broadcast_to([128, 16, 64]),
               ALU.mult, waits=wly, inc="sV")
            if chunks.index(c) + 1 < len(chunks):
                ydone_prev[chunks[chunks.index(c) + 1]] = b.cnt["sV"]
            b.dma("scalar", "sST%d" % (c % 2),
                  y_d.ap()[128 * c:128 * c + 128, :],
                  y_sb[c % 2][:, :], waits=[("sV", b.cnt["sV"])])
            y_store[c] = ("sST%d" % (c % 2), b.cnt["sST%d" % (c % 2)])
            if c == 7:
                y_half[0] = (b.cnt["sST0"], b.cnt["sST1"])
                s5p1_pe_h1 = b.cnt["sPE"]

    for idx in range(len(glist)):
        c, g = glist[idx]
        if g == 0:
            if idx >= 1:
                emit_num(idx - 1)
            chunk_head(chunks.index(c), c)
        emit_A(idx)
        if g != 0 and idx >= 1:
            emit_num(idx - 1)
    if glist:
        emit_num(len(glist) - 1)
    for c in chunks:
        if c not in y_store:
            y_store[c] = ("sST0", 0)
    s5p1_pe = b.cnt["sPE"]
    y_half[1] = (b.cnt["sST0"], b.cnt["sST1"])
    b.end()
    if upto <= 4:
        return finish()
    st1.close()

    st24 = contextlib.ExitStack()
    sb24 = mk_sb(st24)
    out_sb = [sb24("out0", E, F32), sb24("out1", E, F32)]

    st2 = contextlib.ExitStack()
    sb2 = mk_sb(st2)
    A1b = sb2("A1b", NSL * TOK)
    q2a = sb2("q2a", E, F32)
    qs2 = sb2("qs2", 32, F32); rs2 = sb2("rs2", 32, F32)
    QSETS2 = {0: (sc_a, sc_b, sc_c, qs_t, rs_t, "g2e"),
              1: (q2a, sc_b, sc_c, qs2, rs2, "g2o")}
    R1 = sb2("R1", NCH * E, F32)
    bh_t = st2.enter_context(nc.sbuf_tensor("bh_t", [1, E], BF16))
    bl_t = st2.enter_context(nc.sbuf_tensor("bl_t", [1, E], BF16))

    def bias_prep(row_dram, ldw):
        b.dma("sync", "sCI", sc_b[0:1, 0:E], row_dram.ap(), waits=ldw)
        ld = b.cnt["sCI"]
        E_("vector", "tensor_copy", bh_t[0:1, :], sc_b[0:1, 0:E],
           waits=[("sCI", ld)], inc="sV")
        E_("vector", "tensor_copy", sc_c[0:1, 0:E], bh_t[0:1, :], inc="sV")
        E_("vector", "tensor_tensor", sc_a[0:1, 0:E], sc_b[0:1, 0:E],
           sc_c[0:1, 0:E], ALU.subtract, inc="sV")
        E_("vector", "tensor_copy", bl_t[0:1, :], sc_a[0:1, 0:E], inc="sV")
        return b.cnt["sV"]


    def ln_part1(sid, tc, p, res_ap, res_dram):
        col = 16 * sid + tc
        wv0 = b.cnt["sV"]
        E_("scalar", "activation", sc_j[:, :], res_ap, AF.Square,
           accum_out=stt_q[:, col % 64:col % 64 + 1], waits=[("sV", wv0)],
           inc="sA")
        wa = b.cnt["sA"]
        if res_dram is not None:
            b.dma("gpsimd", "sSW%d" % p,
                  res_dram.ap()[128 * tc:128 * tc + 128, :],
                  res_ap, waits=[("sV", wv0)])
            out_store[p] = b.cnt["sSW%d" % p]
        c0 = 32 * sid + 2 * tc
        cq = col % 64
        E_("vector", "tensor_scalar_mul", stm[:, cq:cq + 1],
           stt_s[:, c0:c0 + 1], 1.0 / E, inc="sV")
        E_("vector", "tensor_scalar_mul", str_[:, cq:cq + 1],
           stt_q[:, cq:cq + 1], 1.0 / E, waits=[("sA", wa)], inc="sV")
        E_("vector", "scalar_tensor_tensor", str_[:, cq:cq + 1],
           stm[:, cq:cq + 1], stm[:, cq:cq + 1], str_[:, cq:cq + 1],
           ALU.mult, ALU.subtract, inc="sV")
        E_("vector", "tensor_scalar", str_[:, cq:cq + 1],
           str_[:, cq:cq + 1], -1.0, float(EPS_LN), ALU.mult, ALU.add,
           inc="sV")
        E_("scalar", "activation", str_[:, cq:cq + 1], str_[:, cq:cq + 1],
           AF.Ln, waits=[("sV", b.cnt["sV"])], inc="sA")
        E_("scalar", "activation", str_[:, cq:cq + 1], str_[:, cq:cq + 1],
           AF.Exp, scale=-0.5, inc="sA")
        E_("vector", "tensor_scalar", stb[:, cq:cq + 1], stm[:, cq:cq + 1],
           str_[:, cq:cq + 1], -1.0, ALU.mult, ALU.mult,
           waits=[("sA", b.cnt["sA"])], inc="sV")
        dxn = sc_d if p == 0 else sc_d2
        E_("scalar", "activation", dxn[:, :], res_ap, AF.Identity,
           bias=stb[:, cq:cq + 1], scale=str_[:, cq:cq + 1],
           waits=[("sV", b.cnt["sV"]), ("sPL", b.cnt["sPL"]),
                  ("sV", lnq_guard[p])], inc="sA")
        lnq_xn[p] = b.cnt["sA"]

    def ln_part2(tc, p, out_dram, do_quant):
        dxn = sc_d if p == 0 else sc_d2
        if do_quant:
            _quant(dxn, xq_sb[p], xn_wait=lnq_xn[p],
                   st_wait=("sSW%d" % p, b.cnt["sSW%d" % p]), par=p)
            lnq_guard[p] = b.cnt["sV"]
            b.dma("gpsimd", "sSW%d" % p,
                  out_dram.ap()[128 * tc:128 * tc + 128, :],
                  xq_sb[p][:, :], waits=[("sV", b.cnt["sV"])])
        else:
            E_("vector", "tensor_copy", xq_sb[p][:, :], dxn[:, :],
               waits=[("sSW%d" % p, b.cnt["sSW%d" % p]), ("sA", lnq_xn[p])],
               inc="sV")
            lnq_guard[p] = b.cnt["sV"]
            b.dma("gpsimd", "sSW%d" % p,
                  out_dram.ap()[128 * tc:128 * tc + 128, :],
                  xq_sb[p][:, :], waits=[("sV", b.cnt["sV"])])
        xq_store[p] = b.cnt["sSW%d" % p]

    def epilogue_block(sid, lhsT_ar, Wt, extra_w, lhsT_ok, x_dram, bias_t,
                       res_to_R1, res_dram, out_dram, do_quant, qsets=None):
        prev = {}
        if qsets is not None:
            xq_store[0] = b.cnt["sST0"]
            xq_store[1] = b.cnt["sST1"]
        for tc in range(NCH):
            p = tc % 2
            lxw = []
            if x_dram is not None:
                b.dma("sync", "sX%d" % p, x_sb[p][:, :],
                      x_dram.ap()[128 * tc:128 * tc + 128, :],
                      waits=([("sV", prev[pair])] if tc >= 2 else []) +
                            [("sST0", b.cnt["sST0"]),
                             ("sST1", b.cnt["sST1"])])
                lxw = [("sX%d" % p, b.cnt["sX%d" % p])]
                xbp_t = x_sb[p]
            else:
                xbp_t = R1[:, 1024 * tc:1024 * tc + 1024]
            dst = (R1[:, 1024 * tc:1024 * tc + 1024] if res_to_R1
                   else out_sb[p][:, :])
            pair = tc % 2
            for oh in range(2):
                bank = 2 * pair + oh
                wl = [(lhsT_ok[0], lhsT_ok[1] if tc < 8 else lhsT_ok[2])] +                     list(extra_w)
                if tc < 2:
                    wl.append(("sV", bias_ld))
                if pair in prev:
                    wl.append(("sV", prev[pair]))
                for s in range(NSL):
                    mm(P[bank][:, :],
                       lhsT_ar[:, s * TOK + 128 * tc: s * TOK + 128 * tc + 128],
                       Wt[:, s * E + 512 * oh: s * E + 512 * oh + 512],
                       start=(s == 0), stop=False,
                       waits=wl if s == 0 else ())
                mm(P[bank][:, :], ones_t[0:1, :],
                   bh_t[0:1, 512 * oh:512 * oh + 512],
                   start=False, stop=False)
                mm(P[bank][:, :], ones_t[0:1, :],
                   bl_t[0:1, 512 * oh:512 * oh + 512],
                   start=False, stop=True)
            peO = b.cnt["sPE"]
            E_("vector", "scalar_tensor_tensor",
               dst[:, 0:1024], PS[:, 1024 * pair:1024 * pair + 1024], 1.0,
               xbp_t[:, 0:1024], ALU.mult, ALU.add,
               accum_out=stt_s[:, 32 * sid + 2 * tc:32 * sid + 2 * tc + 1],
               waits=[("sPE", peO), ("sSW%d" % p, b.cnt["sSW%d" % p])] + lxw,
               inc="sV")
            prev[pair] = b.cnt["sV"]
            ln_part1(sid, tc, p, dst, res_dram)
            if qsets is not None:
                if tc >= 3:
                    ln_pC(tc - 3, (tc - 3) % 2, out_dram, qsets)
                if tc >= 2:
                    ln_pB(tc - 2, (tc - 2) % 2, qsets)
                if tc >= 1:
                    ln_pA(tc - 1, (tc - 1) % 2, qsets)
            elif tc >= 1:
                ln_part2(tc - 1, (tc - 1) % 2, out_dram, do_quant)
            if tc == 8:
                sfam = "sST" if qsets is not None else "sSW"
                xq_half[0] = (b.cnt[sfam + "0"], b.cnt[sfam + "1"],
                              sfam)
                pe_half[0] = b.cnt["sPE"]
        if qsets is not None:
            ln_pC(NCH - 3, (NCH - 3) % 2, out_dram, qsets)
            ln_pB(NCH - 2, (NCH - 2) % 2, qsets)
            ln_pA(NCH - 1, (NCH - 1) % 2, qsets)
            ln_pC(NCH - 2, (NCH - 2) % 2, out_dram, qsets)
            ln_pB(NCH - 1, (NCH - 1) % 2, qsets)
            ln_pC(NCH - 1, (NCH - 1) % 2, out_dram, qsets)
        else:
            ln_part2(NCH - 1, (NCH - 1) % 2, out_dram, do_quant)
        sfam = "sST" if qsets is not None else "sSW"
        xq_half[1] = (b.cnt[sfam + "0"], b.cnt[sfam + "1"], sfam)
        pe_half[1] = b.cnt["sPE"]

    b.begin()
    g1_drain = [("sST0", b.cnt["sST0"]), ("sST1", b.cnt["sST1"]),
                ("sPE", s5p1_pe), ("sV", b.cnt["sV"]), ("sA", b.cnt["sA"]),
                ("sX0", b.cnt["sX0"]), ("sX1", b.cnt["sX1"])]
    bias_ld = bias_prep(bp_row, g1_drain)
    for h in range(2):
        for s in range(NSL):
            b.dma("sync", "sT3",
                  A1b[:, s * TOK + 1024 * h:s * TOK + 1024 * h + 1024],
                  y_d.ap()[1024 * h:1024 * h + 1024,
                           128 * s:128 * s + 128], transpose=True,
                  waits=[("sST0", y_half[h][0]), ("sST1", y_half[h][1]),
                         ("sPE", s5p1_pe if h else s5p1_pe_h1),
                         ("sT3", b.cnt["sT3"]),
                         ("sV", b.cnt["sV"]), ("sA", b.cnt["sA"])]
                  if s == 0 else ())
    yT_h1 = 8 * 16
    yT_ok = b.cnt["sT3"]
    epilogue_block(1, A1b, WB, [("sGB", wp_ld)], ("sT3", yT_h1, yT_ok), xs,
                   None,
                   res_to_R1=True, res_dram=None, out_dram=xd_d,
                   do_quant=True, qsets=QSETS2)
    s5p2_pe = b.cnt["sPE"]
    s5p2_pe_h1 = pe_half[0]
    xq_half_52 = dict(xq_half)
    b.end()
    if upto <= 5:
        return finish()

    b.begin()
    b.dma("gpsimd", "sGB", wslab(WB), wsrc(bWvT),
          waits=[("sPE", s5p2_pe), ("sGB", b.cnt["sGB"])])
    bv_ld = b.cnt["sGB"]
    for h in range(2):
        for s in range(NSL):
            b.dma("sync", "sT4",
                  A1b[:, s * TOK + 1024 * h:s * TOK + 1024 * h + 1024],
                  xd_d.ap()[1024 * h:1024 * h + 1024,
                            128 * s:128 * s + 128], transpose=True,
                  waits=[(xq_half_52[h][2] + "0", xq_half_52[h][0]),
                         (xq_half_52[h][2] + "1", xq_half_52[h][1]),
                         ("sT4", b.cnt["sT4"]),
                         ("sPE", s5p2_pe if h else s5p2_pe_h1)]
                  if s == 0 else ())
    xdT_h1 = 8 * 16
    xdT_ok = b.cnt["sT4"]
    bias_ld = bias_prep(fb_row, [("sV", b.cnt["sV"]), ("sA", b.cnt["sA"]),
                                 ("sPL", b.cnt["sPL"])])
    epilogue_block(2, A1b, W3, [("sTW", w3_ok)], ("sT4", xdT_h1, xdT_ok),
                   None, None,
                   res_to_R1=False, res_dram=r2_d, out_dram=xn3_d,
                   do_quant=False)
    s7_pe = b.cnt["sPE"]
    s7_v = b.cnt["sV"]
    s7_a = b.cnt["sA"]
    b.end()
    if upto <= 6:
        return finish()
    st2.close()

    st3 = contextlib.ExitStack()
    sb3 = mk_sb(st3)
    km_ct = sb3("km_ct", NSL * 1024)
    vm_ct = sb3("vm_ct", NSL * 1024)
    rm_ct = sb3("rm_ct", NSL * 1024)
    WrS = sb3("WrS", NSL * E)
    WoS = sb3("WoS", NSL * E)
    E0 = sb3("E0", 512, F32); E1 = sb3("E1", 512, F32)
    EK0 = sb3("EK0", 512, F32); EK1 = sb3("EK1", 512, F32)
    T01 = sb3("T01", 1024, F32)

    xn3T = [T01[:, :].bitcast(BF16)[:, 0:1024]]
    d_sb = EK1[:, :].bitcast(BF16)[:, 0:512]
    yT_ct = sb3("yT_ct", NSL * 1024)

    b.begin()
    s9_st0, s9_st1 = b.cnt["sSW0"], b.cnt["sSW1"]
    xn3_all_st = (b.cnt["sSW0"], b.cnt["sSW1"])
    b.dma("gpsimd", "sGA", wslab(WrS), wsrc(bWrT),
          waits=[("sV", s7_v), ("sGA", b.cnt["sGA"])])
    br_ld = b.cnt["sGA"]
    b.dma("gpsimd", "sGB", wslab(WoS), wsrc(bWoT),
          waits=[("sGB", b.cnt["sGB"])])
    bo_ld = b.cnt["sGB"]

    yct_pe = 0
    epi_prev = {}
    for a in range(2):
        mix_ok = {}
        for s in range(NSL):
            p = 0
            wtr = [("sSW0", xn3_all_st[0]), ("sSW1", xn3_all_st[1]),
                   ("sPE", s7_pe), ("sV", b.cnt["sV"]), ("sA", b.cnt["sA"]),
                   ("sPL", b.cnt["sPL"]), ("sST0", b.cnt["sST0"]),
                   ("sST1", b.cnt["sST1"])]
            if s >= 1:
                wtr = [("sV", mix_ok[s - 1])]
            b.dma("sync", "sQ%d" % p, xn3T[p][:, 0:512],
                  xn3_d.ap()[512 * a:512 * a + 512, 128 * s:128 * s + 128],
                  transpose=True, waits=wtr)
            b.dma("sync", "sQ%d" % p, xn3T[p][:, 512:1024],
                  xn3_d.ap()[1024 + 512 * a:1024 + 512 * a + 512,
                             128 * s:128 * s + 128], transpose=True)
            ltr = [("sQ%d" % p, b.cnt["sQ%d" % p])]
            if s == 0:
                ltr.append(("sPE", s7_pe if a == 0 else yct_pe))
            E_("vector", "tensor_tensor", d_sb, xn3T[p][:, 512:1024],
               xn3T[p][:, 0:512], ALU.subtract, waits=ltr, inc="sV")
            for (mu_t, dst_) in ((muk_sb, km_ct), (muv_sb, vm_ct),
                                 (mur_sb, rm_ct)):
                E_("vector", "tensor_scalar_mul",
                   dst_[:, s * 1024:s * 1024 + 512], xn3T[p][:, 0:512],
                   mu_t[:, s:s + 1], inc="sV")
                E_("vector", "scalar_tensor_tensor",
                   dst_[:, s * 1024 + 512:s * 1024 + 1024], d_sb,
                   mu_t[:, s:s + 1], xn3T[p][:, 0:512], ALU.mult, ALU.add,
                   inc="sV")
            mix_ok[s] = b.cnt["sV"]
        mixes_done = b.cnt["sV"]

        wkv_prev = None
        wkv_done = ("sV", 0, "sPL", 0)
        for oc in range(NSL):
            for gi, (Wt, Mt, wld, wsem) in enumerate(
                    ((WA, km_ct, bk_ld, "sGA"), (WB, vm_ct, bv_ld, "sGB"),
                     (WrS, rm_ct, br_ld, "sGA"))):
                for bb in range(2):
                    bank = 2 * gi + bb
                    wl = [(wsem, wld), ("sV", mixes_done)]
                    if wkv_prev:
                        wl.append(wkv_prev[bank])
                    if a == 0 and oc == 0:
                        wl.append(("sA", s7_a))
                    for s in range(NSL):
                        mm(P[bank][:, :],
                           Wt[:, s * E + 128 * oc: s * E + 128 * oc + 128],
                           Mt[:, s * 1024 + 512 * bb:
                              s * 1024 + 512 * bb + 512],
                           start=(s == 0), stop=(s == NSL - 1),
                           waits=wl if s == 0 else ())
            peKVR = b.cnt["sPE"]
            eu_c = eu_sb[:, oc:oc + 1]
            E_("scalar", "activation", E0[:, :], P[0][:, :], AF.Exp,
               waits=[("sPE", peKVR), ("sV", wkv_done[1]),
                      ("sPL", wkv_done[3])], inc="sA")
            E_("scalar", "activation", E1[:, :], P[1][:, :], AF.Exp,
               inc="sA")
            p01_free = b.cnt["sA"]
            E_("scalar", "activation", T01[:, 512:1024], P[4][:, :], AF.Exp,
               scale=-1.0, inc="sA")
            E_("scalar", "activation", T01[:, 0:512], P[5][:, :], AF.Exp,
               scale=-1.0, inc="sA")
            p45_free = b.cnt["sA"]
            wa = b.cnt["sA"]
            E_("vector", "tensor_tensor", EK1[:, :], E1[:, :], P[3][:, :],
               ALU.mult, waits=[("sA", p01_free)], inc="sV")
            p3_free = b.cnt["sV"]
            E_("vector", "tensor_tensor", EK0[:, :], E0[:, :], P[2][:, :],
               ALU.mult, inc="sV")
            E_("vector", "scalar_tensor_tensor", E1[:, :], E1[:, :], eu_c,
               E0[:, :], ALU.mult, ALU.add, inc="sV")
            E_("vector", "tensor_tensor", E0[:, :], E1[:, :], T01[:, 0:512],
               ALU.mult, waits=[("sA", wa)], inc="sV")
            E_("vector", "tensor_tensor", T01[:, 0:512], E1[:, :], E0[:, :],
               ALU.add, inc="sV")
            E_("vector", "tensor_scalar_add", T01[:, 512:1024],
               T01[:, 512:1024], 1.0, inc="sV")
            E_("vector", "reciprocal", T01[:, :], T01[:, :], inc="sV")
            E_("vector", "scalar_tensor_tensor", EK0[:, :], EK1[:, :], eu_c,
               EK0[:, :], ALU.mult, ALU.add, inc="sV")
            E_("vector", "tensor_tensor",
               yT_ct[:, oc * 1024 + 512:oc * 1024 + 1024], EK0[:, :],
               T01[:, 0:512], ALU.mult, inc="sV")
            E_("vector", "tensor_tensor",
               yT_ct[:, oc * 1024:oc * 1024 + 512],
               P[2][:, :], T01[:, 512:1024], ALU.mult, inc="sV")
            p2_free = b.cnt["sV"]
            bank_free = {0: ("sA", p01_free), 1: ("sA", p01_free),
                         2: ("sV", p2_free), 3: ("sV", p3_free),
                         4: ("sA", p45_free), 5: ("sA", p45_free)}
            wkv_prev = dict(bank_free)
            wkv_done = ("sV", b.cnt["sV"], "sPL", b.cnt["sPL"])

        tcs = [4 * a + i for i in range(4)] + [8 + 4 * a + i for i in range(4)]
        r2st = (s9_st0, s9_st1)
        s9stt = {}
        km_f = km_ct[:, :].bitcast(F32)
        QS9 = {0: (sc_a, sc_b, sc_c, qs_t, rs_t, "g3e"),
               1: (km_f[:, 0:1024], sc_b, sc_c, km_f[:, 1024:1056],
                   km_f[:, 1056:1088], "g3o")}
        km_pe = b.cnt["sPE"]
        eyw = [("sPE", km_pe)]
        xq_store[0] = b.cnt["sST0"]
        xq_store[1] = b.cnt["sST1"]
        for ti, tc in enumerate(tcs):
            p = tc % 2
            b.dma("sync", "sX%d" % p, x_sb[p][:, :],
                  r2_d.ap()[128 * tc:128 * tc + 128, :],
                  waits=([("sV", s9stt.get(p, 0))] if ti >= 2 else []) +
                        [("sSW0", r2st[0]), ("sSW1", r2st[1])])
            lxw = [("sX%d" % p, b.cnt["sX%d" % p])]
            bb = tc // 8
            lcol = 512 * bb + 128 * ((tc % 8) - 4 * a)
            for oh in range(2):
                bank = 6 + oh
                wl = [("sV", wkv_done[1]), ("sPL", wkv_done[3]),
                      ("sGB", bo_ld)]
                if 6 in epi_prev:
                    wl.append(("sV", epi_prev[6]))
                for s in range(NSL):
                    mm(P[bank][:, :],
                       yT_ct[:, s * 1024 + lcol: s * 1024 + lcol + 128],
                       WoS[:, s * E + 512 * oh: s * E + 512 * oh + 512],
                       start=(s == 0), stop=(s == NSL - 1),
                       waits=wl if s == 0 else ())
            peO = b.cnt["sPE"]
            E_("vector", "scalar_tensor_tensor",
               out_sb[p][:, 0:1024], PS[:, 3072:4096], 1.0,
               x_sb[p][:, 0:1024], ALU.mult, ALU.add,
               accum_out=stt_s[:, 96 + 2 * tc:96 + 2 * tc + 1],
               waits=[("sPE", peO), ("sSW%d" % p, b.cnt["sSW%d" % p])] + lxw,
               inc="sV")
            epi_prev[6] = b.cnt["sV"]
            s9stt[p] = b.cnt["sV"]
            ln_part1(3, tc, p, out_sb[p][:, :], r3_d)
            if ti >= 3:
                ln_pC(tcs[ti - 3], tcs[ti - 3] % 2, xd_d, QS9)
            if ti >= 2:
                ln_pB(tcs[ti - 2], tcs[ti - 2] % 2, QS9)
            if ti >= 1:
                ln_pA(tcs[ti - 1], tcs[ti - 1] % 2, QS9, ey=eyw)
        ln_pC(tcs[5], tcs[5] % 2, xd_d, QS9)
        ln_pB(tcs[6], tcs[6] % 2, QS9)
        ln_pA(tcs[7], tcs[7] % 2, QS9, ey=eyw)
        ln_pC(tcs[6], tcs[6] % 2, xd_d, QS9)
        ln_pB(tcs[7], tcs[7] % 2, QS9)
        ln_pC(tcs[7], tcs[7] % 2, xd_d, QS9)
        yct_pe = b.cnt["sPE"]
    s9_pe = b.cnt["sPE"]
    s9_v = b.cnt["sV"]
    s9_a = b.cnt["sA"]
    b.end()
    if upto <= 7:
        return finish()
    st3.close()

    st4 = contextlib.ExitStack()
    sb4 = mk_sb(st4)
    A1c = sb4("A1c", NSL * TOK)
    bh4 = st4.enter_context(nc.sbuf_tensor("bh4", [1, E], BF16))
    bl4 = st4.enter_context(nc.sbuf_tensor("bl4", [1, E], BF16))

    b.begin()
    b.dma("sync", "sCI", sc_b[0:1, 0:E], fb_row.ap(),
          waits=[("sV", s9_v), ("sA", s9_a), ("sPL", b.cnt["sPL"])])
    _l4 = b.cnt["sCI"]
    E_("vector", "tensor_copy", bh4[0:1, :], sc_b[0:1, 0:E],
       waits=[("sCI", _l4)], inc="sV")
    E_("vector", "tensor_copy", sc_c[0:1, 0:E], bh4[0:1, :], inc="sV")
    E_("vector", "tensor_tensor", sc_a[0:1, 0:E], sc_b[0:1, 0:E],
       sc_c[0:1, 0:E], ALU.subtract, inc="sV")
    E_("vector", "tensor_copy", bl4[0:1, :], sc_a[0:1, 0:E], inc="sV")
    fb2_ld = b.cnt["sV"]
    xd2_st = (b.cnt["sST0"], b.cnt["sST1"])
    for s in range(NSL):
        b.dma("sync", "sT5", A1c[:, s * TOK:(s + 1) * TOK],
              xd_d.ap()[:, 128 * s:128 * s + 128], transpose=True,
              waits=[("sST0", xd2_st[0]), ("sST1", xd2_st[1]),
                     ("sPE", s9_pe), ("sV", s9_v), ("sPL", b.cnt["sPL"]),
                     ("sA", s9_a)] if s == 0 else ())
    xdT2_ok = b.cnt["sT5"]
    f2_prev = {}
    s10_st = (b.cnt["sSW0"], b.cnt["sSW1"])
    for tc in range(NCH):
        p = tc % 2
        b.dma("sync", "sX%d" % p, x_sb[p][:, :],
              r3_d.ap()[128 * tc:128 * tc + 128, :],
              waits=([("sV", f2_prev[p])] if tc >= 2 else []) +
                    [("sSW0", s10_st[0]), ("sSW1", s10_st[1])])
        lxw = [("sX%d" % p, b.cnt["sX%d" % p])]
        pair = tc % 2
        for oh in range(2):
            bank = 2 * pair + oh
            wl = [("sT5", xdT2_ok)]
            if tc < 2:
                wl.append(("sV", fb2_ld))
            if pair in f2_prev:
                wl.append(("sV", f2_prev[pair]))
            for s in range(NSL):
                mm(P[bank][:, :],
                   A1c[:, s * TOK + 128 * tc: s * TOK + 128 * tc + 128],
                   W3[:, s * E + 512 * oh: s * E + 512 * oh + 512],
                   start=(s == 0), stop=False,
                   waits=wl if s == 0 else ())
            mm(P[bank][:, :], ones_t[0:1, :],
               bh4[0:1, 512 * oh:512 * oh + 512],
               start=False, stop=False)
            mm(P[bank][:, :], ones_t[0:1, :],
               bl4[0:1, 512 * oh:512 * oh + 512],
               start=False, stop=True)
        peF = b.cnt["sPE"]
        osem0 = "sSW0" if p == 0 else "sST1"
        E_("vector", "tensor_tensor",
           out_sb[p][:, 0:1024], PS[:, 1024 * pair:1024 * pair + 1024],
           x_sb[p][:, 0:1024], ALU.add,
           waits=[("sPE", peF), (osem0, b.cnt[osem0])] + lxw,
           inc="sV")
        f2_prev[pair] = b.cnt["sV"]
        oq, osem = (("gpsimd", "sSW0") if p == 0 else ("scalar", "sST1"))
        b.dma(oq, osem, yout.ap()[128 * tc:128 * tc + 128, :],
              out_sb[p][:, :], waits=[("sV", b.cnt["sV"])])
        out_store[p] = b.cnt[osem]
    for (dst, src) in dbg_list:
        b.dma("sync", "sY", dst.ap(), src.ap(),
              waits=[("sSW0", b.cnt["sSW0"]), ("sSW1", b.cnt["sSW1"]),
                     ("sST0", b.cnt["sST0"]), ("sST1", b.cnt["sST1"])])
    for _s in ("sT1", "sT2", "sT3", "sT4", "sT5", "sTW", "sQ0", "sQ1",
               "sK0", "sK1", "sK2", "sK3", "sCI", "sX0", "sX1", "sP0", "sP1", "sP2"):
        E_("sync", "wait_ge", _s, b.cnt[_s])
    E_("sync", "wait_ge", "sST0", b.cnt["sST0"])
    E_("sync", "wait_ge", "sST1", b.cnt["sST1"])
    E_("sync", "wait_ge", "sSW0", b.cnt["sSW0"])
    E_("sync", "wait_ge", "sSW1", b.cnt["sSW1"])
    E_("sync", "wait_ge", "sY", b.cnt["sY"])
    E_("sync", "wait_ge", "sGA", b.cnt["sGA"])
    E_("sync", "wait_ge", "sGB", b.cnt["sGB"])
    b.end()
    st4.close()
    st24.close()

    b.stack.close()
    return nc


_NC = {}


def _in_maps(ins):
    f32 = np.float32
    x = np.ascontiguousarray(ins["L0_attn_a_x"], dtype=f32)
    shared = {
        "WqT": np.ascontiguousarray(np.asarray(ins["L0_attn_a_Wq"], f32).T),
        "WkT": np.ascontiguousarray(np.asarray(ins["L0_attn_a_Wk"], f32).T),
        "WvT": np.ascontiguousarray(np.asarray(ins["L0_attn_a_Wv"], f32).T),
        "WpT": np.ascontiguousarray(np.asarray(ins["L0_attn_a_Wp"], f32).T),
        "bWkT": np.ascontiguousarray(np.asarray(ins["L0_attn_b_Wk"], f32).T),
        "bWvT": np.ascontiguousarray(np.asarray(ins["L0_attn_b_Wv"], f32).T),
        "bWrT": np.ascontiguousarray(np.asarray(ins["L0_attn_b_Wr"], f32).T),
        "bWoT": np.ascontiguousarray(np.asarray(ins["L0_attn_b_W_out"], f32).T),
        "ffnw": np.ascontiguousarray(np.asarray(ins["L0_ffn_weight"], f32)),
        "bp_row": np.asarray(ins["L0_attn_a_bp"], f32).reshape(1, E).copy(),
        "fb_row": np.asarray(ins["L0_ffn_bias"], f32).reshape(1, E).copy(),
        "mu_k_pf": np.ascontiguousarray(
            np.asarray(ins["L0_attn_b_mu_k"], f32).reshape(NSL, 128).T),
        "mu_v_pf": np.ascontiguousarray(
            np.asarray(ins["L0_attn_b_mu_v"], f32).reshape(NSL, 128).T),
        "mu_r_pf": np.ascontiguousarray(
            np.asarray(ins["L0_attn_b_mu_r"], f32).reshape(NSL, 128).T),
        "u_pf": np.ascontiguousarray(
            np.asarray(ins["L0_attn_b_u"], f32).reshape(NSL, 128).T),
        "mask_sc": np.ascontiguousarray(np.triu(np.ones((128, 128), f32))),
    }
    in_maps = []
    for c in range(NCORE):
        m = dict(shared)
        m["xs"] = np.ascontiguousarray(
            x[:, 1024 * c:1024 * (c + 1), :].reshape(TOK, E))
        pm = np.zeros((128, NCORE), f32)
        pm[:, :c] = 1.0
        m["prevm"] = pm
        in_maps.append(m)
    return in_maps, x.shape


def kernel(**ins):
    in_maps, (B_, T_, E_sz) = _in_maps(ins)
    if "k" not in _NC:
        _NC["k"] = build()
    res = run_bass_kernel_spmd(_NC["k"], in_maps, core_ids=list(range(NCORE)))
    out = np.empty((B_, T_, E_sz), np.float32)
    for c in range(NCORE):
        r = np.asarray(res.results[c]["yout"])
        out[0, 1024 * c:1024 * (c + 1), :] = r[0:1024]
        out[1, 1024 * c:1024 * (c + 1), :] = r[1024:2048]
    return out


# version 28
# speedup vs baseline: 1.0506x; 1.0506x over previous
import contextlib
import numpy as np
import ml_dtypes
import concourse.bass as bass
import concourse.mybir as mybir
from concourse.bass_utils import run_bass_kernel_spmd

F32 = mybir.dt.float32
BF16 = mybir.dt.bfloat16
AF = mybir.ActivationFunctionType
ALU = mybir.AluOpType
AX = mybir.AxisListType

NCORE = 8
TOK = 2048
NCH = 16
E = 1024
NSL = 8
NH = 16
EPS_LN = 1e-5
EPS_ATT = 1e-6
C1 = 1572864.0
ENGS = ("sync", "tensor", "vector", "scalar", "gpsimd")


class B:

    def __init__(self, nc):
        self.nc = nc
        self.cnt = {}
        self.sems = {}
        self.items = None
        self.stack = contextlib.ExitStack()

    def sem(self, name):
        self.sems[name] = self.stack.enter_context(self.nc.semaphore(name))
        self.cnt[name] = 0

    def begin(self):
        self.items = {e: [] for e in ENGS}

    _SELF = {"vector": "sV", "scalar": "sA", "gpsimd": "sPL"}

    def emit(self, eng, op, *args, waits=(), inc=None, amt=1, **kwargs):
        ssem = self._SELF.get(eng)
        if ssem is not None and inc == ssem and op != "dma_start":
            waits = tuple(waits) + ((ssem, self.cnt[ssem]),)
        if inc is not None:
            self.cnt[inc] += amt
        self.items[eng].append(
            (op, args, kwargs, tuple((s, v) for (s, v) in waits if v > 0),
             inc, amt))
        return self.cnt[inc] if inc is not None else None

    def dma(self, eng, sem, out, in_, waits=(), **kw):
        return self.emit(eng, "dma_start", waits=waits, inc=sem, amt=16,
                         out=out, in_=in_, **kw)

    def end(self):
        items, sems = self.items, self.sems
        with self.nc.Block() as blk:
            for eng in ENGS:
                if not items[eng]:
                    continue

                def body(e, _its=tuple(items[eng])):
                    for (op, args, kwargs, waits, inc, amt) in _its:
                        for (s, v) in waits:
                            e.wait_ge(sems[s], v)
                        if op == "wait_ge":
                            e.wait_ge(sems[args[0]], args[1])
                            continue
                        ins = getattr(e, op)(*args, **kwargs)
                        if inc is not None:
                            ins.then_inc(sems[inc], amt)
                getattr(blk, eng)(body)
        self.items = None


def build(dbg=False, upto=99, s5n=16, s5parts='asny'):
    nc = bass.Bass("TRN2", target_bir_lowering=False)

    def din(name, shape, dt=F32):
        return nc.dram_tensor(name, list(shape), dt, kind="ExternalInput")

    xs = din("xs", (TOK, E))
    WqT = din("WqT", (E, E)); WkT = din("WkT", (E, E))
    WvT = din("WvT", (E, E)); WpT = din("WpT", (E, E))
    bWkT = din("bWkT", (E, E)); bWvT = din("bWvT", (E, E))
    bWrT = din("bWrT", (E, E)); bWoT = din("bWoT", (E, E))
    ffnw = din("ffnw", (E, E))
    bp_row = din("bp_row", (1, E)); fb_row = din("fb_row", (1, E))
    mu_k = din("mu_k_pf", (128, NSL)); mu_v = din("mu_v_pf", (128, NSL))
    mu_r = din("mu_r_pf", (128, NSL)); u_pf = din("u_pf", (128, NSL))
    mask_in = din("mask_sc", (128, 128))
    prevm = din("prevm", (128, NCORE))

    yout = nc.dram_tensor("yout", [TOK, E], F32, kind="ExternalOutput")

    xn1_d = nc.dram_tensor("xn1_d", [TOK, E], BF16)
    phk_d = nc.dram_tensor("phk_d", [TOK, E], BF16)
    phq_d = nc.dram_tensor("phq_d", [128, NSL * TOK], BF16)
    y_d = nc.dram_tensor("y_d", [TOK, E], BF16)
    xd_d = nc.dram_tensor("xd_d", [TOK, E], BF16)
    xn3_d = nc.dram_tensor("xn3_d", [TOK, E], BF16)
    wd_d = nc.dram_tensor("wd_d", [E, E], BF16)
    r2_d = nc.dram_tensor("r2_d", [TOK, E], F32)
    r3_d = nc.dram_tensor("r3_d", [TOK, E], F32)
    cc_in = nc.dram_tensor("cc_in", [128, 1040], BF16)
    cc_out = nc.dram_tensor("cc_out", [128 * NCORE, 1040], BF16)

    dbg_list = []
    if dbg:
        for nm, src in (("d_xn1", xn1_d), ("d_phk", phk_d), ("d_phq", phq_d),
                        ("d_y", y_d), ("d_xd", xd_d), ("d_xn3", xn3_d),
                        ("d_wd", wd_d), ("d_r2", r2_d), ("d_r3", r3_d)):
            dbg_list.append((nc.dram_tensor(nm, list(src.shape), src.dtype,
                                            kind="ExternalOutput"), src))

    b = B(nc)
    for s in ("sY", "sX0", "sX1", "sP0", "sP1", "sP2", "sST0", "sST1",
              "sSW0", "sSW1", "sQ0", "sQ1",
              "sK0", "sK1", "sK2", "sK3", "sCI", "sGA", "sGB", "sCC",
              "sT1", "sT2", "sT3", "sT4", "sT5", "sTW",
              "sPE", "sV", "sA", "sPL"):
        b.sem(s)
    E_ = b.emit

    def mk_sb(stack):
        def sb(name, cols, dt=BF16):
            return stack.enter_context(nc.sbuf_tensor(name, [128, cols], dt))
        return sb


    def finish():
        b.begin()
        for _s in list(b.sems):
            if _s in ("sPE", "sV", "sA", "sPL", "sCC"):
                continue
            E_("sync", "wait_ge", _s, b.cnt[_s])
        b.end()
        return nc
    stS = b.stack
    sbS = mk_sb(stS)

    mask_sb = sbS("mask_sb", 128, F32)
    muk_sb = sbS("muk_sb", NSL, F32); muv_sb = sbS("muv_sb", NSL, F32)
    mur_sb = sbS("mur_sb", NSL, F32); u_sb = sbS("u_sb", NSL, F32)
    eu_sb = sbS("eu_sb", NSL, F32)
    pm_sb = sbS("pm_sb", NCORE, F32)
    ones_t = stS.enter_context(nc.sbuf_tensor("ones_t", [1, 128], BF16))
    stt_s = sbS("stt_s", 128, F32)
    stt_q = sbS("stt_q", 64, F32)
    stm = sbS("stm", 64, F32)
    str_ = sbS("str_", 64, F32)
    sc_a = sbS("sc_a", E, F32)
    sc_b = sbS("sc_b", E, F32)
    sc_c = sbS("sc_c", E, F32)
    sc_d = sbS("sc_d", E, F32)
    sc_d2 = sbS("sc_d2", E, F32)
    sc_j = sbS("sc_j", E, F32)
    qs_t = sbS("qs_t", 32, F32)
    rs_t = sbS("rs_t", 32, F32)
    stb = sbS("stb", 64, F32)
    x_sb = [sbS("x_sb0", E, F32), sbS("x_sb1", E, F32),
            sbS("x_sb2", E, F32)]
    xq_sb = [sbS("xq0", E), sbS("xq1", E)]
    WA = sbS("WA", NSL * E)
    WB = sbS("WB", NSL * E)
    W3 = sbS("W3", NSL * E)

    PS = stS.enter_context(nc.psum_tensor("PS", [128, 4096], F32))
    P = [PS[:, 512 * i:512 * (i + 1)] for i in range(8)]

    def mm(out, lhsT, rhs, start, stop, waits=()):
        E_("tensor", "matmul", out, lhsT, rhs, start=start, stop=stop,
           skip_group_check=True, waits=waits, inc="sPE")

    def wslab(Wt):
        return Wt[:, :].rearrange("p (s o) -> p s o", o=E)

    def wsrc(dram):
        return dram.ap().rearrange("(s p) o -> p s o", p=128)

    lnq_xn = {0: 0, 1: 0}
    lnq_guard = {0: 0, 1: 0}
    lnq_pl = {0: 0, 1: 0}
    xq_half = {}
    pe_half = {}
    xq_store = {0: 0, 1: 0}
    out_store = {0: 0, 1: 0}
    xbp_done = {0: 0, 1: 0, 2: 0}
    srcrd = {0: ("sV", 0), 1: ("sV", 0)}
    qguard_v, qguard_pl, qtail = {}, {}, {}
    qguard_sb = {}
    qguard_cx = [0]
    qst = {}

    def ln_pA(tc, p, qsets, ey=()):
        sa, sb_, scx, qs, rs = qsets[p][:5]
        gk = qsets[p][5]
        dxn = sc_d if p == 0 else sc_d2
        x3 = dxn[:, :].rearrange("p (k m) -> p k m", m=32)
        E_("vector", "tensor_reduce", qs[:, :], x3, AX.X, ALU.max,
           apply_absolute_value=True,
           waits=[("sA", lnq_xn[p])], inc="sV")
        E_("vector", "tensor_scalar", qs[:, :], qs[:, :], 1.0 / 7.5,
           1e-30, ALU.mult, ALU.max, inc="sV")
        E_("vector", "reciprocal", rs[:, :], qs[:, :], inc="sV")
        rdv = b.cnt["sV"]
        E_("gpsimd", "tensor_tensor",
           sa[:, :].rearrange("p (k m) -> p k m", m=32), x3,
           rs[:, :].unsqueeze(2).broadcast_to([128, 32, 32]), ALU.mult,
           waits=[("sV", rdv)] + list(ey),
           inc="sPL")
        yv = b.cnt["sPL"]
        lnq_guard[p] = rdv
        E_("scalar", "activation", sb_[:, :], sa[:, :], AF.Abs,
           waits=[("sPL", yv), ("sPL", qguard_sb.get(gk, 0))], inc="sA")
        qst.setdefault(gk, {}).update(ya=b.cnt["sA"], yv=yv)

    def ln_pB(tc, p, qsets):
        sa, sb_, scx, qs, rs = qsets[p][:5]
        gk = qsets[p][5]
        st = qst[gk]
        E_("vector", "tensor_scalar", scx[:, :], sb_[:, :], 2.0, C1,
           ALU.is_ge, ALU.mult,
           waits=[("sA", st["ya"]), ("sPL", qguard_cx[0])], inc="sV")
        E_("vector", "tensor_scalar", sb_[:, :], sb_[:, :], 4.0, 2.0 * C1,
           ALU.is_ge, ALU.mult, inc="sV")
        E_("vector", "scalar_tensor_tensor", scx[:, :], scx[:, :], C1,
           sb_[:, :], ALU.add, ALU.add, inc="sV")
        E_("gpsimd", "tensor_tensor", sb_[:, :], sa[:, :], scx[:, :],
           ALU.add, waits=[("sV", b.cnt["sV"])], inc="sPL")
        E_("gpsimd", "tensor_tensor", sa[:, :], sb_[:, :], scx[:, :],
           ALU.subtract, inc="sPL")
        qguard_sb[gk] = b.cnt["sPL"]
        qguard_cx[0] = b.cnt["sPL"]
        st["pl"] = b.cnt["sPL"]

    def ln_pC(tc, p, out_dram, qsets):
        sa, sb_, scx, qs, rs = qsets[p][:5]
        gk = qsets[p][5]
        st = qst[gk]
        E_("vector", "tensor_tensor",
           xq_sb[p][:, :].rearrange("p (k m) -> p k m", m=32),
           sa[:, :].rearrange("p (k m) -> p k m", m=32),
           qs[:, :].unsqueeze(2).broadcast_to([128, 32, 32]), ALU.mult,
           waits=[("sPL", st["pl"]),
                  ("sST%d" % p, xq_store.get(p, 0))], inc="sV")
        st["xdv"] = b.cnt["sV"]
        b.dma("scalar", "sST%d" % p,
              out_dram.ap()[128 * tc:128 * tc + 128, :],
              xq_sb[p][:, :], waits=[("sV", b.cnt["sV"])])
        xq_store[p] = b.cnt["sST%d" % p]


    def _quant(src_f32, dst_bf, xn_wait=None, st_wait=None, par=None,
               sset=None):
        sa, sb_, scx, qs, rs = (sc_a, sc_b, sc_c, qs_t, rs_t) if sset is None             else sset[:5]
        x3 = src_f32[:, :].rearrange("p (k m) -> p k m", m=32)
        if st_wait is None:
            entry = [("sST0", b.cnt["sST0"]), ("sST1", b.cnt["sST1"])]
        else:
            entry = [st_wait]
        gk = None if sset is None else sset[5]
        if gk is not None:
            entry += [("sV", qguard_v.get(gk, 0)),
                      ("sPL", qguard_pl.get(gk, 0))]
        else:
            entry += [("sPL", b.cnt["sPL"])]
        entry += [("sA", b.cnt["sA"] if xn_wait is None else xn_wait)]
        E_("vector", "tensor_reduce", qs[:, :], x3, AX.X, ALU.max,
           apply_absolute_value=True, waits=entry, inc="sV")
        E_("vector", "tensor_scalar", qs[:, :], qs[:, :], 1.0 / 7.5,
           1e-30, ALU.mult, ALU.max, inc="sV")
        E_("vector", "reciprocal", rs[:, :], qs[:, :], inc="sV")
        E_("vector", "tensor_tensor",
           sa[:, :].rearrange("p (k m) -> p k m", m=32), x3,
           rs[:, :].unsqueeze(2).broadcast_to([128, 32, 32]), ALU.mult,
           inc="sV")
        yv = ("sV", b.cnt["sV"])
        if par is not None:
            srcrd[par] = yv
        abs_w = [yv]
        if gk is not None:
            abs_w.append(("sPL", qguard_sb[0]))
        E_("scalar", "activation", sb_[:, :], sa[:, :], AF.Abs,
           waits=abs_w, inc="sA")
        ya = b.cnt["sA"]
        t1_w = [("sA", ya)]
        if gk is not None:
            t1_w.append(("sPL", abs_w[1][1]))
        E_("vector", "tensor_scalar", scx[:, :], sb_[:, :], 2.0, C1,
           ALU.is_ge, ALU.mult, waits=t1_w, inc="sV")
        E_("vector", "tensor_scalar", sb_[:, :], sb_[:, :], 4.0, 2.0 * C1,
           ALU.is_ge, ALU.mult, inc="sV")
        E_("vector", "scalar_tensor_tensor", scx[:, :], scx[:, :], C1,
           sb_[:, :], ALU.add, ALU.add, inc="sV")
        chv_ = b.cnt["sV"]
        if gk is not None:
            E_("gpsimd", "tensor_tensor", sb_[:, :], sa[:, :], scx[:, :],
               ALU.add, waits=[("sV", chv_)], inc="sPL")
            E_("gpsimd", "tensor_tensor", sa[:, :], sb_[:, :], scx[:, :],
               ALU.subtract, inc="sPL")
            qguard_sb[0] = b.cnt["sPL"]
            qtail[gk] = (b.cnt["sPL"], sa, qs, dst_bf)
            return
        E_("vector", "tensor_tensor", sb_[:, :], sa[:, :], scx[:, :],
           ALU.add, waits=[yv], inc="sV")
        E_("vector", "tensor_tensor", sa[:, :], sb_[:, :], scx[:, :],
           ALU.subtract, inc="sV")
        E_("vector", "tensor_tensor",
           dst_bf[:, :].rearrange("p (k m) -> p k m", m=32),
           sa[:, :].rearrange("p (k m) -> p k m", m=32),
           qs[:, :].unsqueeze(2).broadcast_to([128, 32, 32]), ALU.mult,
           inc="sV")

    def _quant_tail(gk):
        pl, sa, qs, dst_bf = qtail[gk]
        E_("vector", "tensor_tensor",
           dst_bf[:, :].rearrange("p (k m) -> p k m", m=32),
           sa[:, :].rearrange("p (k m) -> p k m", m=32),
           qs[:, :].unsqueeze(2).broadcast_to([128, 32, 32]), ALU.mult,
           waits=[("sPL", pl)], inc="sV")
        qguard_v[gk] = b.cnt["sV"]
        qguard_pl[gk] = qtail[gk][0]

    st1 = contextlib.ExitStack()
    sb1 = mk_sb(st1)
    A1 = sb1("A1", NSL * TOK)
    vbuf = sb1("vbuf", NCH * 1040)
    ring = sb1("ring", 3 * E)
    pq_out = [sb1("pq_out0", 512), sb1("pq_out1", 512)]
    pq_ring = [sb1("pq_ring%d" % _i, NSL * 256) for _i in range(3)]
    S_loc = sb1("S_loc", NCH * 520)
    s_base = sb1("s_base", 1040, F32)
    s_gt = sb1("s_gt", 1040)
    s_tot = sb1("s_tot", 1040)
    S_op = [sb1("S_op0", 520), sb1("S_op1", 520)]
    A_sb = [sb1("A_sb0", 512), sb1("A_sb1", 512)]

    y_sb = [sb1("y_sb0", E), sb1("y_sb1", E)]
    dn_sb = sb1("dn_sb", 16, F32)
    rd_sb = sb1("rd_sb", 16, F32)

    xnT = A1
    phikT = A1

    b.begin()
    for (dst, src) in ((mask_sb, mask_in), (muk_sb, mu_k), (muv_sb, mu_v),
                       (mur_sb, mu_r), (u_sb, u_pf), (pm_sb, prevm)):
        b.dma("sync", "sY", dst[:, :], src.ap())
    w0 = b.cnt["sY"]
    E_("scalar", "activation", eu_sb[:, :], u_sb[:, :], AF.Exp,
       waits=[("sY", w0)], inc="sA")

    E_("vector", "memset", vbuf[:, :], 1.0, inc="sV")
    v_ones = b.cnt["sV"]
    E_("vector", "memset", ones_t[0:1, :], 1.0, inc="sV")
    b.dma("gpsimd", "sGA", wslab(WA), wsrc(WqT))
    wq_ld = b.cnt["sGA"]
    b.dma("gpsimd", "sGB", wslab(WB), wsrc(WkT))
    wk_ld = b.cnt["sGB"]
    b.end()
    if upto <= 0:
        return finish()

    b.begin()
    xn1_st = {}
    s1_apply = {0: 0, 1: 0, 2: 0}
    s1sem = ["sX0", "sX1", "sP0"]
    for c in range(NCH):
        p = c % 3
        qn = "sync" if c % 2 == 0 else "scalar"
        b.dma(qn, s1sem[p], x_sb[p][:, :],
              xs.ap()[128 * c:128 * c + 128, :],
              waits=[("sA", s1_apply[p])] if c >= 3 else ())
        lw = [(s1sem[p], b.cnt[s1sem[p]])]
        E_("vector", "tensor_scalar", sc_a[:, :], x_sb[p][:, :], 1.0, None,
           ALU.mult, ALU.add, accum_out=stt_s[:, c:c + 1],
           waits=lw, inc="sV")
        E_("scalar", "activation", sc_j[:, :], x_sb[p][:, :], AF.Square,
           accum_out=stt_q[:, c:c + 1], waits=lw, inc="sA")
        wa = b.cnt["sA"]
        cc = slice(c, c + 1)
        E_("vector", "tensor_scalar_mul", stm[:, cc], stt_s[:, cc],
           1.0 / E, inc="sV")
        E_("vector", "tensor_scalar_mul", str_[:, cc], stt_q[:, cc],
           1.0 / E, waits=[("sA", wa)], inc="sV")
        E_("vector", "tensor_tensor", stb[:, cc], stm[:, cc],
           stm[:, cc], ALU.mult, inc="sV")
        E_("vector", "scalar_tensor_tensor", str_[:, cc], str_[:, cc],
           float(EPS_LN), stb[:, cc], ALU.add, ALU.subtract, inc="sV")
        E_("scalar", "activation", str_[:, cc], str_[:, cc],
           AF.Ln, waits=[("sV", b.cnt["sV"])], inc="sA")
        E_("scalar", "activation", str_[:, cc], str_[:, cc],
           AF.Exp, scale=-0.5, inc="sA")
        E_("vector", "scalar_tensor_tensor", stb[:, cc], stm[:, cc],
           -1.0, str_[:, cc], ALU.mult, ALU.mult,
           waits=[("sA", b.cnt["sA"])], inc="sV")
        pq = c % 2
        E_("scalar", "activation", xq_sb[pq][:, :], x_sb[p][:, :],
           AF.Identity,
           bias=stb[:, c:c + 1], scale=str_[:, c:c + 1],
           waits=[("sV", b.cnt["sV"]),
                  ("sSW%d" % pq, b.cnt["sSW%d" % pq])],
           inc="sA")
        s1_apply[p] = b.cnt["sA"]
        b.dma("gpsimd", "sSW%d" % pq, xn1_d.ap()[128 * c:128 * c + 128, :],
              xq_sb[pq][:, :], waits=[("sA", b.cnt["sA"])])
        xn1_st[c] = (b.cnt["sSW0"], b.cnt["sSW1"])
    b.end()
    if upto <= 1:
        return finish()

    b.begin()
    for h in range(2):
        for s in range(NSL):
            b.dma("sync", "sT1",
                  xnT[:, s * TOK + 1024 * h:s * TOK + 1024 * h + 1024],
                  xn1_d.ap()[1024 * h:1024 * h + 1024,
                             128 * s:128 * s + 128], transpose=True,
                  waits=[("sSW0", xn1_st[8 * h + 7][0]),
                         ("sSW1", xn1_st[8 * h + 7][1]),
                         ("sT1", b.cnt["sT1"])] if s == 0 else ())
    xnT_h1 = 8 * 16
    xnT_ok = b.cnt["sT1"]

    evacV, evacA = {}, {}
    prev_exp = 0
    prev_tt = ("sV", 0)
    for oc in range(NSL):
        for tt in range(4):
            bank = (oc * 4 + tt) % 4
            po = (oc * 4 + tt) % 2
            wl = [("sGA", wq_ld), ("sT1", xnT_h1 if tt < 2 else xnT_ok)]
            if bank in evacV:
                wl += [evacV[bank], ("sA", evacA[bank])]
            for s in range(NSL):
                mm(P[bank][:, :],
                   WA[:, s * E + 128 * oc: s * E + 128 * oc + 128],
                   xnT[:, s * TOK + 512 * tt: s * TOK + 512 * tt + 512],
                   start=(s == 0), stop=(s == NSL - 1),
                   waits=wl if s == 0 else ())
            pe_c = b.cnt["sPE"]
            E_("scalar", "activation", sc_a[:, 0:512], P[bank][:, :],
               AF.Relu, scale=-1.0,
               waits=[("sPE", pe_c), prev_tt], inc="sA")
            E_("scalar", "activation", sc_b[:, 0:512], sc_a[:, 0:512], AF.Exp,
               scale=-1.0, inc="sA")
            prev_exp = b.cnt["sA"]
            E_("scalar", "activation", sc_b[:, 512:1024], P[bank][:, :],
               AF.Relu, inc="sA")
            evacA[bank] = b.cnt["sA"]
            E_("vector", "tensor_tensor", pq_out[po][:, :], sc_b[:, 0:512],
               sc_b[:, 512:1024], ALU.add,
               waits=[("sA", evacA[bank]), ("sQ%d" % po, b.cnt["sQ%d" % po])],
               inc="sV")
            evacV[bank] = ("sV", b.cnt["sV"])
            prev_tt = ("sV", b.cnt["sV"])
            b.dma("sync", "sQ%d" % po,
                  phq_d.ap().rearrange("p (c s x) -> p c s x", s=NSL, x=128)[
                      :, 4 * tt:4 * tt + 4, oc, :],
                  pq_out[po][:, :].rearrange("p (c x) -> p c x", x=128),
                  waits=[prev_tt])
    q_pe_done = b.cnt["sPE"]
    phq_all_st = (b.cnt["sQ0"], b.cnt["sQ1"])

    b.dma("gpsimd", "sGA", wslab(WA), wsrc(WvT),
          waits=[("sPE", q_pe_done), ("sGA", b.cnt["sGA"])])
    wv_ld = b.cnt["sGA"]

    sS_first = set()
    phk_half = {}
    s2_pe_half = {}
    ring_pe = {}
    kevacV = {}
    vevacA = {}
    snapshots = {}
    phk_store = {}

    def kv_mms(c):
        bch, n = c // 8, c % 8
        for h in range(NH):
            j = 8 * bch + h // 2
            bank = 4 + j // 4
            col = 128 * (j % 4)
            half = 64 * (h % 2)
            first = (bank, half) not in sS_first
            if first:
                sS_first.add((bank, half))
            wlk = []
            if h == 0:
                wlk = [("sV", kevacV[c]), ("sA", vevacA[c])]
                if n > 0:
                    wlk.append(("sV", snapshots[(bch, n - 1)]))
            mm(P[4 + j // 4][half:half + 64, col:col + 65],
               ring[:, (c % 3) * E + 64 * h:(c % 3) * E + 64 * h + 64],
               vbuf[:, c * 1040 + 65 * h: c * 1040 + 65 * h + 65],
               start=first, stop=(n == 7), waits=wlk)
        return b.cnt["sPE"]

    def snapshot(c):
        bch, n = c // 8, c % 8
        pe_c = b.cnt["sPE"]
        for half in range(2):
            bank = P[4 + 2 * bch + half]
            src = bank[:, 0:512].rearrange("p (j x) -> p j x", x=128)[:, :, 0:65]
            if n < 7:
                dst = S_loc[:, c * 520 + half * 260: c * 520 + half * 260 + 260]
            else:
                dst = s_tot[:, bch * 520 + half * 260:
                            bch * 520 + half * 260 + 260]
            E_("vector", "tensor_copy",
               dst.rearrange("p (j e) -> p j e", e=65), src,
               waits=[("sPE", pe_c)] if half == 0 else (), inc="sV")
        snapshots[(bch, n)] = b.cnt["sV"]

    for c in range(NCH):
        for oh in range(2):
            bank = oh
            wl = [("sGB", wk_ld), ("sT1", xnT_h1 if c < 8 else xnT_ok)]
            if bank in evacV:
                wl += [evacV[bank], ("sA", evacA[bank])]
            for s in range(NSL):
                mm(P[bank][:, :],
                   xnT[:, s * TOK + 128 * c: s * TOK + 128 * c + 128],
                   WB[:, s * E + 512 * oh: s * E + 512 * oh + 512],
                   start=(s == 0), stop=(s == NSL - 1),
                   waits=wl if s == 0 else ())
            pe_c = b.cnt["sPE"]
            wl2 = [("sPE", pe_c), prev_tt]
            if c >= 3 and oh == 0:
                wl2.append(("sPE", ring_pe[c % 3]))
            E_("scalar", "activation", sc_a[:, 0:512], P[bank][:, :],
               AF.Relu, scale=-1.0, waits=wl2, inc="sA")
            E_("scalar", "activation", sc_b[:, 0:512], sc_a[:, 0:512],
               AF.Exp, scale=-1.0, inc="sA")
            prev_exp = b.cnt["sA"]
            E_("scalar", "activation", sc_b[:, 512:1024], P[bank][:, :],
               AF.Relu, inc="sA")
            evacA[bank] = b.cnt["sA"]
            wlr = [("sA", evacA[bank])]
            if c >= 3 and oh == 0:
                wlr.append(("sK%d" % (c % 3), phk_store[c - 3]))
            E_("vector", "tensor_tensor",
               ring[:, (c % 3) * E + 512 * oh:(c % 3) * E + 512 * oh + 512],
               sc_b[:, 0:512], sc_b[:, 512:1024], ALU.add,
               waits=wlr, inc="sV")
            evacV[bank] = ("sV", b.cnt["sV"])
            prev_tt = ("sV", b.cnt["sV"])
        kevacV[c] = b.cnt["sV"]
        b.dma("gpsimd", "sK%d" % (c % 3), phk_d.ap()[128 * c:128 * c + 128, :],
              ring[:, (c % 3) * E:(c % 3) * E + E],
              waits=[("sV", kevacV[c])])
        phk_store[c] = b.cnt["sK%d" % (c % 3)]
        for oh in range(2):
            bank = 2 + oh
            wl = [("sGA", wv_ld), ("sT1", xnT_h1 if c < 8 else xnT_ok),
                  ("sV", v_ones)]
            if bank in evacA:
                wl += [("sA", evacA[bank])]
            for s in range(NSL):
                mm(P[bank][:, :],
                   xnT[:, s * TOK + 128 * c: s * TOK + 128 * c + 128],
                   WA[:, s * E + 512 * oh: s * E + 512 * oh + 512],
                   start=(s == 0), stop=(s == NSL - 1),
                   waits=wl if s == 0 else ())
            pe_c = b.cnt["sPE"]
            dst = vbuf[:, c * 1040 + 520 * oh: c * 1040 + 520 * oh + 520]
            E_("scalar", "activation",
               dst.rearrange("p (h e) -> p h e", e=65)[:, :, 0:64],
               P[bank][:, :].rearrange("p (h e) -> p h e", e=64),
               AF.Copy, waits=[("sPE", pe_c)], inc="sA")
            evacA[bank] = b.cnt["sA"]
        vevacA[c] = b.cnt["sA"]
        if c >= 1:
            ring_pe[(c - 1) % 3] = kv_mms(c - 1)
            snapshot(c - 1)
        if c == 8:
            phk_half[0] = [b.cnt["sK%d" % i] for i in range(3)]
            s2_pe_half[0] = b.cnt["sPE"]
    ring_pe[(NCH - 1) % 3] = kv_mms(NCH - 1)
    snapshot(NCH - 1)
    s2_pe = b.cnt["sPE"]
    phk_half[1] = [b.cnt["sK%d" % i] for i in range(3)]
    s2_pe_half[1] = s2_pe
    phk_all_st = [b.cnt["sK%d" % i] for i in range(4)]
    b.end()
    if upto <= 2:
        return finish()

    b.begin()
    b.dma("scalar", "sCI", cc_in.ap(), s_tot[:, :],
          waits=[("sV", snapshots[(1, 7)])])
    ccin_st = b.cnt["sCI"]
    b.dma("gpsimd", "sGB", wslab(WB), wsrc(WpT),
          waits=[("sPE", s2_pe), ("sGB", b.cnt["sGB"])])
    wp_ld = b.cnt["sGB"]
    b.dma("gpsimd", "sGA", wslab(WA), wsrc(bWkT),
          waits=[("sPE", s2_pe), ("sGA", b.cnt["sGA"])])
    bk_ld = b.cnt["sGA"]
    E_("gpsimd", "collective_compute", "AllGather", ALU.bypass,
       replica_groups=[list(range(NCORE))],
       ins=[cc_in.ap().opt()], outs=[cc_out.ap().opt()],
       waits=[("sCI", ccin_st)], inc="sCC")

    Q4 = {0: (sc_a, sc_b, sc_c, qs_t, rs_t, "s4e"),
          1: (sc_d, sc_j, sc_c, stm[:, 0:32], stm[:, 32:64], "s4o")}
    xq_store[0] = b.cnt["sST0"]
    xq_store[1] = b.cnt["sST1"]
    l4 = {}

    def q4A(oc):
        p = oc % 2
        sa, sb_, scx, qs, rs = Q4[p][:5]
        gk = Q4[p][5]
        x3 = x_sb[p][:, :].rearrange("p (k m) -> p k m", m=32)
        E_("vector", "tensor_reduce", qs[:, :], x3, AX.X, ALU.max,
           apply_absolute_value=True, waits=[l4[oc]], inc="sV")
        E_("vector", "tensor_scalar", qs[:, :], qs[:, :], 1.0 / 7.5,
           1e-30, ALU.mult, ALU.max, inc="sV")
        E_("vector", "reciprocal", rs[:, :], qs[:, :], inc="sV")
        rdv = b.cnt["sV"]
        E_("gpsimd", "tensor_tensor",
           sa[:, :].rearrange("p (k m) -> p k m", m=32), x3,
           rs[:, :].unsqueeze(2).broadcast_to([128, 32, 32]), ALU.mult,
           waits=[("sV", rdv)], inc="sPL")
        yv = b.cnt["sPL"]
        s4y[p] = yv
        E_("scalar", "activation", sb_[:, :], sa[:, :], AF.Abs,
           waits=[("sPL", yv), ("sPL", qguard_sb.get(gk, 0))], inc="sA")
        qst.setdefault(gk, {}).update(ya=b.cnt["sA"], yv=yv)

    s4y = {0: 0, 1: 0}
    for oc in range(NSL):
        p = oc % 2
        b.dma("scalar", "sP%d" % p, x_sb[p][:, :],
              ffnw.ap()[128 * oc:128 * oc + 128, :],
              waits=[("sPL", s4y[p]), ("sA", b.cnt["sA"])]
              if oc >= 2 else [("sV", b.cnt["sV"]), ("sA", b.cnt["sA"])])
        l4[oc] = ("sP%d" % p, b.cnt["sP%d" % p])
        if oc >= 3:
            ln_pC(oc - 3, (oc - 3) % 2, wd_d, Q4)
        if oc >= 1:
            q4A(oc - 1)
        if oc >= 2:
            ln_pB(oc - 2, (oc - 2) % 2, Q4)
    ln_pC(NSL - 3, (NSL - 3) % 2, wd_d, Q4)
    q4A(NSL - 1)
    ln_pB(NSL - 2, (NSL - 2) % 2, Q4)
    ln_pC(NSL - 2, (NSL - 2) % 2, wd_d, Q4)
    ln_pB(NSL - 1, (NSL - 1) % 2, Q4)
    ln_pC(NSL - 1, (NSL - 1) % 2, wd_d, Q4)
    E_("vector", "engine_nop", waits=[("sPL", b.cnt["sPL"])], inc="sV")
    wd_st = (b.cnt["sST0"], b.cnt["sST1"])
    for s in range(NSL):
        b.dma("scalar", "sTW", W3[:, s * E:(s + 1) * E],
              wd_d.ap()[:, 128 * s:128 * s + 128], transpose=True,
              waits=[("sST0", wd_st[0]), ("sST1", wd_st[1])]
              if s == 0 else ())
    w3_ok = b.cnt["sTW"]

    E_("vector", "memset", s_base[:, :], 0.0, inc="sV")
    for cp in range(NCORE):
        b.dma("scalar", "sP0", s_gt[:, :],
              cc_out.ap()[128 * cp:128 * cp + 128, :],
              waits=([("sCC", 1)] if cp == 0 else []) + [("sV", b.cnt["sV"])])
        E_("vector", "scalar_tensor_tensor", s_base[:, :], s_gt[:, :],
           pm_sb[:, cp:cp + 1], s_base[:, :], ALU.mult, ALU.add,
           waits=[("sP0", b.cnt["sP0"])], inc="sV")
    sbase_ok = b.cnt["sV"]
    b.end()
    if upto <= 3:
        return finish()

    b.begin()
    for h in range(2):
        for s in range(NSL):
            b.dma("sync", "sT2",
                  phikT[:, s * TOK + 1024 * h:s * TOK + 1024 * h + 1024],
                  phk_d.ap()[1024 * h:1024 * h + 1024,
                             128 * s:128 * s + 128], transpose=True,
                  waits=[("sK0", phk_half[h][0]), ("sK1", phk_half[h][1]),
                         ("sK2", phk_half[h][2]), ("sT2", b.cnt["sT2"]),
                         ("sPE", s2_pe_half[h])] if s == 0 else ())
    phikT_h1 = 8 * 16
    phikT_ok = b.cnt["sT2"]
    for _r in pq_ring:
        E_("vector", "memset", _r[:, :], 0.0, inc="sV")
    pqz_zero = b.cnt["sV"]

    aevac, nevac, aevac_pl, ydone_prev = {}, {}, {}, {}
    y_half = {}
    s5p1_pe_h1 = 0
    sop_pe = {}
    y_store = {}
    pq_ld = {}
    sop_okd = {}
    snap_last = snapshots[(1, 7)]
    chunks = [8 * bch + n for bch in range(2) for n in range(8)][:s5n]
    if "a" in s5parts:
        glist = [(c, g) for c in chunks for g in range(4)]
    else:
        glist = []

    def pq_load(ci):
        c = chunks[ci]
        slot = ci % 3
        wlq = [("sQ0", phq_all_st[0]), ("sQ1", phq_all_st[1]),
               ("sV", pqz_zero)]
        if ci >= 3:
            wlq.append(("sPE", sop_pe[chunks[ci - 3]]))
        pqv = pq_ring[slot][:, :].rearrange("p (s two x) -> p s two x",
                                            two=2, x=128)
        phv = phq_d.ap().rearrange("p (c s x) -> p c s x", s=NSL, x=128)[
            :, c, :, :]
        sem = "sP%d" % (ci % 3)
        b.dma("sync", sem, pqv[0:64, :, 0, :], phv[0:64],
              waits=wlq + [(sem, b.cnt[sem])])
        b.dma("sync", sem, pqv[64:128, :, 1, :], phv[64:128])
        pq_ld[c] = (sem, b.cnt[sem])

    def chunk_head(ci, c):
        bch, n = c // 8, c % 8
        if ci == 0:
            for cj in range(min(3, len(chunks))):
                pq_load(cj)
        elif ci + 2 < len(chunks):
            pq_load(ci + 2)
        wl = [("sV", sbase_ok)]
        if ci >= 2:
            wl.append(("sPE", sop_pe[chunks[ci - 2]]))
        spar = c % 2
        if n == 0:
            E_("vector", "tensor_copy", S_op[spar][:, :],
               s_base[:, bch * 520:bch * 520 + 520], waits=wl, inc="sV")
        else:
            E_("vector", "tensor_tensor", S_op[spar][:, :],
               s_base[:, bch * 520:bch * 520 + 520],
               S_loc[:, (c - 1) * 520:(c - 1) * 520 + 520],
               ALU.add, waits=wl, inc="sV")
        sop_okd[c] = b.cnt["sV"]

    def emit_A(idx):
        c, g = glist[idx]
        par = chunks.index(c) % 3
        ab = 4 + (idx % 4)
        wlA = [pq_ld[c], ("sV", snap_last),
               ("sT2", phikT_h1 if c < 8 else phikT_ok)]
        if idx >= 4:
            wlA.append(aevac[idx - 4])
        for j in range(4):
            h = 4 * g + j
            s = h // 2
            blk = 2 * s + (h % 2)
            mm(P[ab][:, :][:, 128 * j:128 * j + 128],
               phikT[:, s * TOK + 128 * c: s * TOK + 128 * c + 128],
               pq_ring[par][:, 128 * blk:128 * blk + 128],
               start=(j == 0), stop=(j == 3),
               waits=wlA if j == 0 else ())
        peA = b.cnt["sPE"]
        E_("vector", "tensor_tensor",
           A_sb[idx % 2][:, :].rearrange("p (j x) -> p j x", x=128),
           P[ab][:, :][:, 0:512].rearrange("p (j x) -> p j x", x=128),
           mask_sb[:, :].unsqueeze(1).broadcast_to([128, 4, 128]),
           ALU.mult, waits=[("sPE", peA)], inc="sV")
        aevac[idx] = ("sV", b.cnt["sV"])
        aevac_pl[idx] = b.cnt["sV"]

    def emit_num(idx):
        c, g = glist[idx]
        par = chunks.index(c) % 3
        wlN = [("sV", aevac_pl[idx]), ("sV", sop_okd[c]),
               ("sA", vevacA[c])]
        if g == 0 and c in ydone_prev:
            wlN.append(("sV", ydone_prev[c]))
        for j in range(4):
            h = 4 * g + j
            s = h // 2
            blk = 2 * s + (h % 2)
            mm(PS[:, 128 * h:128 * h + 65],
               A_sb[idx % 2][:, 128 * j:128 * j + 128],
               vbuf[:, c * 1040 + 65 * h: c * 1040 + 65 * h + 65],
               start=(h % 4 == 0), stop=False,
               waits=wlN if j == 0 else ())
            mm(PS[:, 128 * h:128 * h + 65],
               pq_ring[par][:, 128 * blk:128 * blk + 128],
               S_op[c % 2][:, 65 * s:65 * s + 65],
               start=False, stop=(j == 3))
        if g == 3:
            peN = b.cnt["sPE"]
            sop_pe[c] = peN
            pv = PS[:, 0:2048].rearrange("p (h x) -> p h x", x=128)
            E_("vector", "tensor_scalar_add", dn_sb[:, :],
               pv[:, :, 64:65], float(EPS_ATT), waits=[("sPE", peN)],
               inc="sV")
            E_("vector", "reciprocal", rd_sb[:, :], dn_sb[:, :], inc="sV")
            wly = []
            ci = chunks.index(c)
            if ci >= 2:
                wly.append(y_store[chunks[ci - 2]])
            E_("vector", "tensor_tensor",
               y_sb[c % 2][:, :].rearrange("p (h x) -> p h x", x=64),
               pv[:, :, 0:64],
               rd_sb[:, :].unsqueeze(2).broadcast_to([128, 16, 64]),
               ALU.mult, waits=wly, inc="sV")
            if chunks.index(c) + 1 < len(chunks):
                ydone_prev[chunks[chunks.index(c) + 1]] = b.cnt["sV"]
            b.dma("scalar", "sST%d" % (c % 2),
                  y_d.ap()[128 * c:128 * c + 128, :],
                  y_sb[c % 2][:, :], waits=[("sV", b.cnt["sV"])])
            y_store[c] = ("sST%d" % (c % 2), b.cnt["sST%d" % (c % 2)])
            if c == 7:
                y_half[0] = (b.cnt["sST0"], b.cnt["sST1"])
                s5p1_pe_h1 = b.cnt["sPE"]

    for idx in range(len(glist)):
        c, g = glist[idx]
        if g == 0:
            if idx >= 1:
                emit_num(idx - 1)
            chunk_head(chunks.index(c), c)
        emit_A(idx)
        if g != 0 and idx >= 1:
            emit_num(idx - 1)
    if glist:
        emit_num(len(glist) - 1)
    for c in chunks:
        if c not in y_store:
            y_store[c] = ("sST0", 0)
    s5p1_pe = b.cnt["sPE"]
    y_half[1] = (b.cnt["sST0"], b.cnt["sST1"])
    b.end()
    if upto <= 4:
        return finish()
    st1.close()

    st24 = contextlib.ExitStack()
    sb24 = mk_sb(st24)
    out_sb = [sb24("out0", E, F32), sb24("out1", E, F32)]

    st2 = contextlib.ExitStack()
    sb2 = mk_sb(st2)
    A1b = sb2("A1b", NSL * TOK)
    q2a = sb2("q2a", E, F32)
    qs2 = sb2("qs2", 32, F32); rs2 = sb2("rs2", 32, F32)
    QSETS2 = {0: (sc_a, sc_b, sc_c, qs_t, rs_t, "g2e"),
              1: (q2a, out_sb[0], sc_c, qs2, rs2, "g2o")}
    R1 = sb2("R1", NCH * E, F32)
    bh_t = st2.enter_context(nc.sbuf_tensor("bh_t", [1, E], BF16))
    bl_t = st2.enter_context(nc.sbuf_tensor("bl_t", [1, E], BF16))

    def bias_prep(row_dram, ldw):
        b.dma("sync", "sCI", sc_b[0:1, 0:E], row_dram.ap(), waits=ldw)
        ld = b.cnt["sCI"]
        E_("vector", "tensor_copy", bh_t[0:1, :], sc_b[0:1, 0:E],
           waits=[("sCI", ld)], inc="sV")
        E_("vector", "tensor_copy", sc_c[0:1, 0:E], bh_t[0:1, :], inc="sV")
        E_("vector", "tensor_tensor", sc_a[0:1, 0:E], sc_b[0:1, 0:E],
           sc_c[0:1, 0:E], ALU.subtract, inc="sV")
        E_("vector", "tensor_copy", bl_t[0:1, :], sc_a[0:1, 0:E], inc="sV")
        return b.cnt["sV"]


    def ln_part1(sid, tc, p, res_ap, res_dram):
        col = 16 * sid + tc
        wv0 = b.cnt["sV"]
        E_("scalar", "activation", sc_j[:, :], res_ap, AF.Square,
           accum_out=stt_q[:, col % 64:col % 64 + 1], waits=[("sV", wv0)],
           inc="sA")
        wa = b.cnt["sA"]
        if res_dram is not None:
            b.dma("gpsimd", "sSW%d" % p,
                  res_dram.ap()[128 * tc:128 * tc + 128, :],
                  res_ap, waits=[("sV", wv0)])
            out_store[p] = b.cnt["sSW%d" % p]
        c0 = 32 * sid + 2 * tc
        cq = col % 64
        E_("vector", "tensor_scalar_mul", stm[:, cq:cq + 1],
           stt_s[:, c0:c0 + 1], 1.0 / E, inc="sV")
        E_("vector", "tensor_scalar_mul", str_[:, cq:cq + 1],
           stt_q[:, cq:cq + 1], 1.0 / E, waits=[("sA", wa)], inc="sV")
        E_("vector", "scalar_tensor_tensor", str_[:, cq:cq + 1],
           stm[:, cq:cq + 1], stm[:, cq:cq + 1], str_[:, cq:cq + 1],
           ALU.mult, ALU.subtract, inc="sV")
        E_("vector", "tensor_scalar", str_[:, cq:cq + 1],
           str_[:, cq:cq + 1], -1.0, float(EPS_LN), ALU.mult, ALU.add,
           inc="sV")
        E_("scalar", "activation", str_[:, cq:cq + 1], str_[:, cq:cq + 1],
           AF.Ln, waits=[("sV", b.cnt["sV"])], inc="sA")
        E_("scalar", "activation", str_[:, cq:cq + 1], str_[:, cq:cq + 1],
           AF.Exp, scale=-0.5, inc="sA")
        E_("vector", "tensor_scalar", stb[:, cq:cq + 1], stm[:, cq:cq + 1],
           str_[:, cq:cq + 1], -1.0, ALU.mult, ALU.mult,
           waits=[("sA", b.cnt["sA"])], inc="sV")
        dxn = sc_d if p == 0 else sc_d2
        E_("scalar", "activation", dxn[:, :], res_ap, AF.Identity,
           bias=stb[:, cq:cq + 1], scale=str_[:, cq:cq + 1],
           waits=[("sV", b.cnt["sV"]), ("sPL", b.cnt["sPL"]),
                  ("sV", lnq_guard[p])], inc="sA")
        lnq_xn[p] = b.cnt["sA"]

    def ln_part2(tc, p, out_dram, do_quant):
        dxn = sc_d if p == 0 else sc_d2
        if do_quant:
            _quant(dxn, xq_sb[p], xn_wait=lnq_xn[p],
                   st_wait=("sSW%d" % p, b.cnt["sSW%d" % p]), par=p)
            lnq_guard[p] = b.cnt["sV"]
            b.dma("gpsimd", "sSW%d" % p,
                  out_dram.ap()[128 * tc:128 * tc + 128, :],
                  xq_sb[p][:, :], waits=[("sV", b.cnt["sV"])])
        else:
            E_("vector", "tensor_copy", xq_sb[p][:, :], dxn[:, :],
               waits=[("sSW%d" % p, b.cnt["sSW%d" % p]), ("sA", lnq_xn[p])],
               inc="sV")
            lnq_guard[p] = b.cnt["sV"]
            b.dma("gpsimd", "sSW%d" % p,
                  out_dram.ap()[128 * tc:128 * tc + 128, :],
                  xq_sb[p][:, :], waits=[("sV", b.cnt["sV"])])
        xq_store[p] = b.cnt["sSW%d" % p]

    def epilogue_block(sid, lhsT_ar, Wt, extra_w, lhsT_ok, x_dram, bias_t,
                       res_to_R1, res_dram, out_dram, do_quant, qsets=None):
        prev = {}
        if qsets is not None:
            xq_store[0] = b.cnt["sST0"]
            xq_store[1] = b.cnt["sST1"]
        for tc in range(NCH):
            p = tc % 2
            lxw = []
            if x_dram is not None:
                b.dma("sync", "sX%d" % p, x_sb[p][:, :],
                      x_dram.ap()[128 * tc:128 * tc + 128, :],
                      waits=([("sV", prev[pair])] if tc >= 2 else []) +
                            [("sST0", b.cnt["sST0"]),
                             ("sST1", b.cnt["sST1"])])
                lxw = [("sX%d" % p, b.cnt["sX%d" % p])]
                xbp_t = x_sb[p]
            else:
                xbp_t = R1[:, 1024 * tc:1024 * tc + 1024]
            dst = (R1[:, 1024 * tc:1024 * tc + 1024] if res_to_R1
                   else out_sb[p][:, :])
            pair = tc % 2
            for oh in range(2):
                bank = 2 * pair + oh
                wl = [(lhsT_ok[0], lhsT_ok[1] if tc < 8 else lhsT_ok[2])] +                     list(extra_w)
                if tc < 2:
                    wl.append(("sV", bias_ld))
                if pair in prev:
                    wl.append(("sV", prev[pair]))
                for s in range(NSL):
                    mm(P[bank][:, :],
                       lhsT_ar[:, s * TOK + 128 * tc: s * TOK + 128 * tc + 128],
                       Wt[:, s * E + 512 * oh: s * E + 512 * oh + 512],
                       start=(s == 0), stop=False,
                       waits=wl if s == 0 else ())
                mm(P[bank][:, :], ones_t[0:1, :],
                   bh_t[0:1, 512 * oh:512 * oh + 512],
                   start=False, stop=False)
                mm(P[bank][:, :], ones_t[0:1, :],
                   bl_t[0:1, 512 * oh:512 * oh + 512],
                   start=False, stop=True)
            peO = b.cnt["sPE"]
            E_("vector", "scalar_tensor_tensor",
               dst[:, 0:1024], PS[:, 1024 * pair:1024 * pair + 1024], 1.0,
               xbp_t[:, 0:1024], ALU.mult, ALU.add,
               accum_out=stt_s[:, 32 * sid + 2 * tc:32 * sid + 2 * tc + 1],
               waits=[("sPE", peO), ("sSW%d" % p, b.cnt["sSW%d" % p])] + lxw,
               inc="sV")
            prev[pair] = b.cnt["sV"]
            ln_part1(sid, tc, p, dst, res_dram)
            if qsets is not None:
                if tc >= 3:
                    ln_pC(tc - 3, (tc - 3) % 2, out_dram, qsets)
                if tc >= 1:
                    ln_pA(tc - 1, (tc - 1) % 2, qsets)
                if tc >= 2:
                    ln_pB(tc - 2, (tc - 2) % 2, qsets)
            elif tc >= 1:
                ln_part2(tc - 1, (tc - 1) % 2, out_dram, do_quant)
            if tc == 8:
                sfam = "sST" if qsets is not None else "sSW"
                xq_half[0] = (b.cnt[sfam + "0"], b.cnt[sfam + "1"],
                              sfam)
                pe_half[0] = b.cnt["sPE"]
        if qsets is not None:
            ln_pC(NCH - 3, (NCH - 3) % 2, out_dram, qsets)
            ln_pA(NCH - 1, (NCH - 1) % 2, qsets)
            ln_pB(NCH - 2, (NCH - 2) % 2, qsets)
            ln_pC(NCH - 2, (NCH - 2) % 2, out_dram, qsets)
            ln_pB(NCH - 1, (NCH - 1) % 2, qsets)
            ln_pC(NCH - 1, (NCH - 1) % 2, out_dram, qsets)
        else:
            ln_part2(NCH - 1, (NCH - 1) % 2, out_dram, do_quant)
        sfam = "sST" if qsets is not None else "sSW"
        xq_half[1] = (b.cnt[sfam + "0"], b.cnt[sfam + "1"], sfam)
        pe_half[1] = b.cnt["sPE"]

    b.begin()
    g1_drain = [("sST0", b.cnt["sST0"]), ("sST1", b.cnt["sST1"]),
                ("sPE", s5p1_pe), ("sV", b.cnt["sV"]), ("sA", b.cnt["sA"]),
                ("sX0", b.cnt["sX0"]), ("sX1", b.cnt["sX1"])]
    bias_ld = bias_prep(bp_row, g1_drain)
    E_("scalar", "nop", waits=[("sPL", b.cnt["sPL"])], inc="sA")
    for h in range(2):
        for s in range(NSL):
            b.dma("sync", "sT3",
                  A1b[:, s * TOK + 1024 * h:s * TOK + 1024 * h + 1024],
                  y_d.ap()[1024 * h:1024 * h + 1024,
                           128 * s:128 * s + 128], transpose=True,
                  waits=[("sST0", y_half[h][0]), ("sST1", y_half[h][1]),
                         ("sPE", s5p1_pe if h else s5p1_pe_h1),
                         ("sT3", b.cnt["sT3"]),
                         ("sV", b.cnt["sV"]), ("sA", b.cnt["sA"])]
                  if s == 0 else ())
    yT_h1 = 8 * 16
    yT_ok = b.cnt["sT3"]
    epilogue_block(1, A1b, WB, [("sGB", wp_ld)], ("sT3", yT_h1, yT_ok), xs,
                   None,
                   res_to_R1=True, res_dram=None, out_dram=xd_d,
                   do_quant=True, qsets=QSETS2)
    s5p2_pe = b.cnt["sPE"]
    s5p2_pe_h1 = pe_half[0]
    xq_half_52 = dict(xq_half)
    b.end()
    if upto <= 5:
        return finish()

    b.begin()
    b.dma("gpsimd", "sGB", wslab(WB), wsrc(bWvT),
          waits=[("sPE", s5p2_pe), ("sGB", b.cnt["sGB"])])
    bv_ld = b.cnt["sGB"]
    for h in range(2):
        for s in range(NSL):
            b.dma("sync", "sT4",
                  A1b[:, s * TOK + 1024 * h:s * TOK + 1024 * h + 1024],
                  xd_d.ap()[1024 * h:1024 * h + 1024,
                            128 * s:128 * s + 128], transpose=True,
                  waits=[(xq_half_52[h][2] + "0", xq_half_52[h][0]),
                         (xq_half_52[h][2] + "1", xq_half_52[h][1]),
                         ("sT4", b.cnt["sT4"]),
                         ("sPE", s5p2_pe if h else s5p2_pe_h1)]
                  if s == 0 else ())
    xdT_h1 = 8 * 16
    xdT_ok = b.cnt["sT4"]
    bias_ld = bias_prep(fb_row, [("sV", b.cnt["sV"]), ("sA", b.cnt["sA"]),
                                 ("sPL", b.cnt["sPL"])])
    E_("vector", "engine_nop", waits=[("sPL", b.cnt["sPL"])], inc="sV")
    epilogue_block(2, A1b, W3, [("sTW", w3_ok)], ("sT4", xdT_h1, xdT_ok),
                   None, None,
                   res_to_R1=False, res_dram=r2_d, out_dram=xn3_d,
                   do_quant=False)
    s7_pe = b.cnt["sPE"]
    s7_v = b.cnt["sV"]
    s7_a = b.cnt["sA"]
    b.end()
    if upto <= 6:
        return finish()
    st2.close()

    st3 = contextlib.ExitStack()
    sb3 = mk_sb(st3)
    km_ct = sb3("km_ct", NSL * 1024)
    vm_ct = sb3("vm_ct", NSL * 1024)
    rm_ct = sb3("rm_ct", NSL * 1024)
    WrS = sb3("WrS", NSL * E)
    WoS = sb3("WoS", NSL * E)
    E0 = sb3("E0", 512, F32); E1 = sb3("E1", 512, F32)
    EK0 = sb3("EK0", 512, F32); EK1 = sb3("EK1", 512, F32)
    T01 = sb3("T01", 1024, F32)

    xn3T = [T01[:, :].bitcast(BF16)[:, 0:1024]]
    d_sb = EK1[:, :].bitcast(BF16)[:, 0:512]
    yT_ct = sb3("yT_ct", NSL * 1024)

    b.begin()
    s9_st0, s9_st1 = b.cnt["sSW0"], b.cnt["sSW1"]
    xn3_all_st = (b.cnt["sSW0"], b.cnt["sSW1"])
    b.dma("gpsimd", "sGA", wslab(WrS), wsrc(bWrT),
          waits=[("sV", s7_v), ("sGA", b.cnt["sGA"])])
    br_ld = b.cnt["sGA"]
    b.dma("gpsimd", "sGB", wslab(WoS), wsrc(bWoT),
          waits=[("sGB", b.cnt["sGB"])])
    bo_ld = b.cnt["sGB"]

    yct_pe = 0
    epi_prev = {}
    for a in range(2):
        mix_ok = {}
        for s in range(NSL):
            p = 0
            wtr = [("sSW0", xn3_all_st[0]), ("sSW1", xn3_all_st[1]),
                   ("sPE", s7_pe), ("sV", b.cnt["sV"]), ("sA", b.cnt["sA"]),
                   ("sPL", b.cnt["sPL"]), ("sST0", b.cnt["sST0"]),
                   ("sST1", b.cnt["sST1"])]
            if s >= 1:
                wtr = [("sV", mix_ok[s - 1])]
            b.dma("sync", "sQ%d" % p, xn3T[p][:, 0:512],
                  xn3_d.ap()[512 * a:512 * a + 512, 128 * s:128 * s + 128],
                  transpose=True, waits=wtr)
            b.dma("sync", "sQ%d" % p, xn3T[p][:, 512:1024],
                  xn3_d.ap()[1024 + 512 * a:1024 + 512 * a + 512,
                             128 * s:128 * s + 128], transpose=True)
            ltr = [("sQ%d" % p, b.cnt["sQ%d" % p])]
            if s == 0:
                ltr.append(("sPE", s7_pe if a == 0 else yct_pe))
            E_("vector", "tensor_tensor", d_sb, xn3T[p][:, 512:1024],
               xn3T[p][:, 0:512], ALU.subtract, waits=ltr, inc="sV")
            for (mu_t, dst_) in ((muk_sb, km_ct), (muv_sb, vm_ct),
                                 (mur_sb, rm_ct)):
                E_("vector", "tensor_scalar_mul",
                   dst_[:, s * 1024:s * 1024 + 512], xn3T[p][:, 0:512],
                   mu_t[:, s:s + 1], inc="sV")
                E_("vector", "scalar_tensor_tensor",
                   dst_[:, s * 1024 + 512:s * 1024 + 1024], d_sb,
                   mu_t[:, s:s + 1], xn3T[p][:, 0:512], ALU.mult, ALU.add,
                   inc="sV")
            mix_ok[s] = b.cnt["sV"]
        mixes_done = b.cnt["sV"]

        wkv_prev = None
        wkv_done = ("sV", 0, "sPL", 0)
        for oc in range(NSL):
            for gi, (Wt, Mt, wld, wsem) in enumerate(
                    ((WA, km_ct, bk_ld, "sGA"), (WB, vm_ct, bv_ld, "sGB"),
                     (WrS, rm_ct, br_ld, "sGA"))):
                for bb in range(2):
                    bank = 2 * gi + bb
                    wl = [(wsem, wld), ("sV", mixes_done)]
                    if wkv_prev:
                        wl.append(wkv_prev[bank])
                    if a == 0 and oc == 0:
                        wl.append(("sA", s7_a))
                    for s in range(NSL):
                        mm(P[bank][:, :],
                           Wt[:, s * E + 128 * oc: s * E + 128 * oc + 128],
                           Mt[:, s * 1024 + 512 * bb:
                              s * 1024 + 512 * bb + 512],
                           start=(s == 0), stop=(s == NSL - 1),
                           waits=wl if s == 0 else ())
            peKVR = b.cnt["sPE"]
            eu_c = eu_sb[:, oc:oc + 1]
            E_("scalar", "activation", E0[:, :], P[0][:, :], AF.Exp,
               waits=[("sPE", peKVR), ("sV", wkv_done[1]),
                      ("sPL", wkv_done[3])], inc="sA")
            E_("scalar", "activation", E1[:, :], P[1][:, :], AF.Exp,
               inc="sA")
            p01_free = b.cnt["sA"]
            E_("scalar", "activation", T01[:, 512:1024], P[4][:, :], AF.Exp,
               scale=-1.0, inc="sA")
            E_("scalar", "activation", T01[:, 0:512], P[5][:, :], AF.Exp,
               scale=-1.0, inc="sA")
            p45_free = b.cnt["sA"]
            wa = b.cnt["sA"]
            E_("vector", "tensor_tensor", EK1[:, :], E1[:, :], P[3][:, :],
               ALU.mult, waits=[("sA", p01_free)], inc="sV")
            p3_free = b.cnt["sV"]
            E_("vector", "tensor_tensor", EK0[:, :], E0[:, :], P[2][:, :],
               ALU.mult, inc="sV")
            E_("vector", "scalar_tensor_tensor", E1[:, :], E1[:, :], eu_c,
               E0[:, :], ALU.mult, ALU.add, inc="sV")
            E_("vector", "tensor_tensor", E0[:, :], E1[:, :], T01[:, 0:512],
               ALU.mult, waits=[("sA", wa)], inc="sV")
            E_("vector", "tensor_tensor", T01[:, 0:512], E1[:, :], E0[:, :],
               ALU.add, inc="sV")
            E_("vector", "tensor_scalar_add", T01[:, 512:1024],
               T01[:, 512:1024], 1.0, inc="sV")
            E_("vector", "reciprocal", T01[:, :], T01[:, :], inc="sV")
            E_("vector", "scalar_tensor_tensor", EK0[:, :], EK1[:, :], eu_c,
               EK0[:, :], ALU.mult, ALU.add, inc="sV")
            E_("vector", "tensor_tensor",
               yT_ct[:, oc * 1024 + 512:oc * 1024 + 1024], EK0[:, :],
               T01[:, 0:512], ALU.mult, inc="sV")
            E_("vector", "tensor_tensor",
               yT_ct[:, oc * 1024:oc * 1024 + 512],
               P[2][:, :], T01[:, 512:1024], ALU.mult, inc="sV")
            p2_free = b.cnt["sV"]
            bank_free = {0: ("sA", p01_free), 1: ("sA", p01_free),
                         2: ("sV", p2_free), 3: ("sV", p3_free),
                         4: ("sA", p45_free), 5: ("sA", p45_free)}
            wkv_prev = dict(bank_free)
            wkv_done = ("sV", b.cnt["sV"], "sPL", b.cnt["sPL"])

        tcs = [4 * a + i for i in range(4)] + [8 + 4 * a + i for i in range(4)]
        r2st = (s9_st0, s9_st1)
        s9stt = {}
        km_f = km_ct[:, :].bitcast(F32)
        QS9 = {0: (sc_a, sc_b, sc_c, qs_t, rs_t, "g3e"),
               1: (km_f[:, 0:1024], km_f[:, 1088:2112], sc_c,
                   km_f[:, 1024:1056],
                   km_f[:, 1056:1088], "g3o")}
        km_pe = b.cnt["sPE"]
        eyw = [("sPE", km_pe)]
        xq_store[0] = b.cnt["sST0"]
        xq_store[1] = b.cnt["sST1"]
        for ti, tc in enumerate(tcs):
            p = tc % 2
            b.dma("sync", "sX%d" % p, x_sb[p][:, :],
                  r2_d.ap()[128 * tc:128 * tc + 128, :],
                  waits=([("sV", s9stt.get(p, 0))] if ti >= 2 else []) +
                        [("sSW0", r2st[0]), ("sSW1", r2st[1])])
            lxw = [("sX%d" % p, b.cnt["sX%d" % p])]
            bb = tc // 8
            lcol = 512 * bb + 128 * ((tc % 8) - 4 * a)
            for oh in range(2):
                bank = 6 + oh
                wl = [("sV", wkv_done[1]), ("sPL", wkv_done[3]),
                      ("sGB", bo_ld)]
                if 6 in epi_prev:
                    wl.append(("sV", epi_prev[6]))
                for s in range(NSL):
                    mm(P[bank][:, :],
                       yT_ct[:, s * 1024 + lcol: s * 1024 + lcol + 128],
                       WoS[:, s * E + 512 * oh: s * E + 512 * oh + 512],
                       start=(s == 0), stop=(s == NSL - 1),
                       waits=wl if s == 0 else ())
            peO = b.cnt["sPE"]
            E_("vector", "scalar_tensor_tensor",
               out_sb[p][:, 0:1024], PS[:, 3072:4096], 1.0,
               x_sb[p][:, 0:1024], ALU.mult, ALU.add,
               accum_out=stt_s[:, 96 + 2 * tc:96 + 2 * tc + 1],
               waits=[("sPE", peO), ("sSW%d" % p, b.cnt["sSW%d" % p])] + lxw,
               inc="sV")
            epi_prev[6] = b.cnt["sV"]
            s9stt[p] = b.cnt["sV"]
            ln_part1(3, tc, p, out_sb[p][:, :], r3_d)
            if ti >= 3:
                ln_pC(tcs[ti - 3], tcs[ti - 3] % 2, xd_d, QS9)
            if ti >= 1:
                ln_pA(tcs[ti - 1], tcs[ti - 1] % 2, QS9, ey=eyw)
            if ti >= 2:
                ln_pB(tcs[ti - 2], tcs[ti - 2] % 2, QS9)
        ln_pC(tcs[5], tcs[5] % 2, xd_d, QS9)
        ln_pA(tcs[7], tcs[7] % 2, QS9, ey=eyw)
        ln_pB(tcs[6], tcs[6] % 2, QS9)
        ln_pC(tcs[6], tcs[6] % 2, xd_d, QS9)
        ln_pB(tcs[7], tcs[7] % 2, QS9)
        ln_pC(tcs[7], tcs[7] % 2, xd_d, QS9)
        yct_pe = b.cnt["sPE"]
    s9_pe = b.cnt["sPE"]
    s9_v = b.cnt["sV"]
    s9_a = b.cnt["sA"]
    b.end()
    if upto <= 7:
        return finish()
    st3.close()

    st4 = contextlib.ExitStack()
    sb4 = mk_sb(st4)
    A1c = sb4("A1c", NSL * TOK)
    bh4 = st4.enter_context(nc.sbuf_tensor("bh4", [1, E], BF16))
    bl4 = st4.enter_context(nc.sbuf_tensor("bl4", [1, E], BF16))

    b.begin()
    b.dma("sync", "sCI", sc_b[0:1, 0:E], fb_row.ap(),
          waits=[("sV", s9_v), ("sA", s9_a), ("sPL", b.cnt["sPL"])])
    _l4 = b.cnt["sCI"]
    E_("vector", "tensor_copy", bh4[0:1, :], sc_b[0:1, 0:E],
       waits=[("sCI", _l4)], inc="sV")
    E_("vector", "tensor_copy", sc_c[0:1, 0:E], bh4[0:1, :], inc="sV")
    E_("vector", "tensor_tensor", sc_a[0:1, 0:E], sc_b[0:1, 0:E],
       sc_c[0:1, 0:E], ALU.subtract, inc="sV")
    E_("vector", "tensor_copy", bl4[0:1, :], sc_a[0:1, 0:E], inc="sV")
    fb2_ld = b.cnt["sV"]
    xd2_st = (b.cnt["sST0"], b.cnt["sST1"])
    for s in range(NSL):
        b.dma("sync", "sT5", A1c[:, s * TOK:(s + 1) * TOK],
              xd_d.ap()[:, 128 * s:128 * s + 128], transpose=True,
              waits=[("sST0", xd2_st[0]), ("sST1", xd2_st[1]),
                     ("sPE", s9_pe), ("sV", s9_v), ("sPL", b.cnt["sPL"]),
                     ("sA", s9_a)] if s == 0 else ())
    xdT2_ok = b.cnt["sT5"]
    f2_prev = {}
    s10_st = (b.cnt["sSW0"], b.cnt["sSW1"])
    for tc in range(NCH):
        p = tc % 2
        b.dma("sync", "sX%d" % p, x_sb[p][:, :],
              r3_d.ap()[128 * tc:128 * tc + 128, :],
              waits=([("sV", f2_prev[p])] if tc >= 2 else []) +
                    [("sSW0", s10_st[0]), ("sSW1", s10_st[1])])
        lxw = [("sX%d" % p, b.cnt["sX%d" % p])]
        pair = tc % 2
        for oh in range(2):
            bank = 2 * pair + oh
            wl = [("sT5", xdT2_ok)]
            if tc < 2:
                wl.append(("sV", fb2_ld))
            if pair in f2_prev:
                wl.append(("sV", f2_prev[pair]))
            for s in range(NSL):
                mm(P[bank][:, :],
                   A1c[:, s * TOK + 128 * tc: s * TOK + 128 * tc + 128],
                   W3[:, s * E + 512 * oh: s * E + 512 * oh + 512],
                   start=(s == 0), stop=False,
                   waits=wl if s == 0 else ())
            mm(P[bank][:, :], ones_t[0:1, :],
               bh4[0:1, 512 * oh:512 * oh + 512],
               start=False, stop=False)
            mm(P[bank][:, :], ones_t[0:1, :],
               bl4[0:1, 512 * oh:512 * oh + 512],
               start=False, stop=True)
        peF = b.cnt["sPE"]
        osem0 = "sSW0" if p == 0 else "sST1"
        E_("vector", "tensor_tensor",
           out_sb[p][:, 0:1024], PS[:, 1024 * pair:1024 * pair + 1024],
           x_sb[p][:, 0:1024], ALU.add,
           waits=[("sPE", peF), (osem0, b.cnt[osem0])] + lxw,
           inc="sV")
        f2_prev[pair] = b.cnt["sV"]
        oq, osem = (("gpsimd", "sSW0") if p == 0 else ("scalar", "sST1"))
        b.dma(oq, osem, yout.ap()[128 * tc:128 * tc + 128, :],
              out_sb[p][:, :], waits=[("sV", b.cnt["sV"])])
        out_store[p] = b.cnt[osem]
    for (dst, src) in dbg_list:
        b.dma("sync", "sY", dst.ap(), src.ap(),
              waits=[("sSW0", b.cnt["sSW0"]), ("sSW1", b.cnt["sSW1"]),
                     ("sST0", b.cnt["sST0"]), ("sST1", b.cnt["sST1"])])
    for _s in ("sT1", "sT2", "sT3", "sT4", "sT5", "sTW", "sQ0", "sQ1",
               "sK0", "sK1", "sK2", "sK3", "sCI", "sX0", "sX1", "sP0", "sP1", "sP2"):
        E_("sync", "wait_ge", _s, b.cnt[_s])
    E_("sync", "wait_ge", "sST0", b.cnt["sST0"])
    E_("sync", "wait_ge", "sST1", b.cnt["sST1"])
    E_("sync", "wait_ge", "sSW0", b.cnt["sSW0"])
    E_("sync", "wait_ge", "sSW1", b.cnt["sSW1"])
    E_("sync", "wait_ge", "sY", b.cnt["sY"])
    E_("sync", "wait_ge", "sGA", b.cnt["sGA"])
    E_("sync", "wait_ge", "sGB", b.cnt["sGB"])
    b.end()
    st4.close()
    st24.close()

    b.stack.close()
    return nc


_NC = {}


def _in_maps(ins):
    f32 = np.float32
    x = np.ascontiguousarray(ins["L0_attn_a_x"], dtype=f32)
    shared = {
        "WqT": np.ascontiguousarray(np.asarray(ins["L0_attn_a_Wq"], f32).T),
        "WkT": np.ascontiguousarray(np.asarray(ins["L0_attn_a_Wk"], f32).T),
        "WvT": np.ascontiguousarray(np.asarray(ins["L0_attn_a_Wv"], f32).T),
        "WpT": np.ascontiguousarray(np.asarray(ins["L0_attn_a_Wp"], f32).T),
        "bWkT": np.ascontiguousarray(np.asarray(ins["L0_attn_b_Wk"], f32).T),
        "bWvT": np.ascontiguousarray(np.asarray(ins["L0_attn_b_Wv"], f32).T),
        "bWrT": np.ascontiguousarray(np.asarray(ins["L0_attn_b_Wr"], f32).T),
        "bWoT": np.ascontiguousarray(np.asarray(ins["L0_attn_b_W_out"], f32).T),
        "ffnw": np.ascontiguousarray(np.asarray(ins["L0_ffn_weight"], f32)),
        "bp_row": np.asarray(ins["L0_attn_a_bp"], f32).reshape(1, E).copy(),
        "fb_row": np.asarray(ins["L0_ffn_bias"], f32).reshape(1, E).copy(),
        "mu_k_pf": np.ascontiguousarray(
            np.asarray(ins["L0_attn_b_mu_k"], f32).reshape(NSL, 128).T),
        "mu_v_pf": np.ascontiguousarray(
            np.asarray(ins["L0_attn_b_mu_v"], f32).reshape(NSL, 128).T),
        "mu_r_pf": np.ascontiguousarray(
            np.asarray(ins["L0_attn_b_mu_r"], f32).reshape(NSL, 128).T),
        "u_pf": np.ascontiguousarray(
            np.asarray(ins["L0_attn_b_u"], f32).reshape(NSL, 128).T),
        "mask_sc": np.ascontiguousarray(np.triu(np.ones((128, 128), f32))),
    }
    in_maps = []
    for c in range(NCORE):
        m = dict(shared)
        m["xs"] = np.ascontiguousarray(
            x[:, 1024 * c:1024 * (c + 1), :].reshape(TOK, E))
        pm = np.zeros((128, NCORE), f32)
        pm[:, :c] = 1.0
        m["prevm"] = pm
        in_maps.append(m)
    return in_maps, x.shape


def kernel(**ins):
    in_maps, (B_, T_, E_sz) = _in_maps(ins)
    if "k" not in _NC:
        _NC["k"] = build()
    res = run_bass_kernel_spmd(_NC["k"], in_maps, core_ids=list(range(NCORE)))
    out = np.empty((B_, T_, E_sz), np.float32)
    for c in range(NCORE):
        r = np.asarray(res.results[c]["yout"])
        out[0, 1024 * c:1024 * (c + 1), :] = r[0:1024]
        out[1, 1024 * c:1024 * (c + 1), :] = r[1024:2048]
    return out


# version 29
# speedup vs baseline: 1.0652x; 1.0139x over previous
import contextlib
import numpy as np
import ml_dtypes
import concourse.bass as bass
import concourse.mybir as mybir
from concourse.bass_utils import run_bass_kernel_spmd

F32 = mybir.dt.float32
BF16 = mybir.dt.bfloat16
AF = mybir.ActivationFunctionType
ALU = mybir.AluOpType
AX = mybir.AxisListType

NCORE = 8
TOK = 2048
NCH = 16
E = 1024
NSL = 8
NH = 16
EPS_LN = 1e-5
EPS_ATT = 1e-6
C1 = 1572864.0
ENGS = ("sync", "tensor", "vector", "scalar", "gpsimd")


class B:

    def __init__(self, nc):
        self.nc = nc
        self.cnt = {}
        self.sems = {}
        self.items = None
        self.stack = contextlib.ExitStack()

    def sem(self, name):
        self.sems[name] = self.stack.enter_context(self.nc.semaphore(name))
        self.cnt[name] = 0

    def begin(self):
        self.items = {e: [] for e in ENGS}

    _SELF = {"vector": "sV", "scalar": "sA", "gpsimd": "sPL"}

    def emit(self, eng, op, *args, waits=(), inc=None, amt=1, **kwargs):
        ssem = self._SELF.get(eng)
        if ssem is not None and inc == ssem and op != "dma_start":
            waits = tuple(waits) + ((ssem, self.cnt[ssem]),)
        if inc is not None:
            self.cnt[inc] += amt
        self.items[eng].append(
            (op, args, kwargs, tuple((s, v) for (s, v) in waits if v > 0),
             inc, amt))
        return self.cnt[inc] if inc is not None else None

    def dma(self, eng, sem, out, in_, waits=(), **kw):
        return self.emit(eng, "dma_start", waits=waits, inc=sem, amt=16,
                         out=out, in_=in_, **kw)

    def end(self):
        items, sems = self.items, self.sems
        with self.nc.Block() as blk:
            for eng in ENGS:
                if not items[eng]:
                    continue

                def body(e, _its=tuple(items[eng])):
                    for (op, args, kwargs, waits, inc, amt) in _its:
                        for (s, v) in waits:
                            e.wait_ge(sems[s], v)
                        if op == "wait_ge":
                            e.wait_ge(sems[args[0]], args[1])
                            continue
                        ins = getattr(e, op)(*args, **kwargs)
                        if inc is not None:
                            ins.then_inc(sems[inc], amt)
                getattr(blk, eng)(body)
        self.items = None


def build(dbg=False, upto=99, s5n=16, s5parts='asny'):
    nc = bass.Bass("TRN2", target_bir_lowering=False)

    def din(name, shape, dt=F32):
        return nc.dram_tensor(name, list(shape), dt, kind="ExternalInput")

    xs = din("xs", (TOK, E))
    WqT = din("WqT", (E, E)); WkT = din("WkT", (E, E))
    WvT = din("WvT", (E, E)); WpT = din("WpT", (E, E))
    bWkT = din("bWkT", (E, E)); bWvT = din("bWvT", (E, E))
    bWrT = din("bWrT", (E, E)); bWoT = din("bWoT", (E, E))
    ffnw = din("ffnw", (E, E))
    bp_row = din("bp_row", (1, E)); fb_row = din("fb_row", (1, E))
    mu_k = din("mu_k_pf", (128, NSL)); mu_v = din("mu_v_pf", (128, NSL))
    mu_r = din("mu_r_pf", (128, NSL)); u_pf = din("u_pf", (128, NSL))
    mask_in = din("mask_sc", (128, 128))
    prevm = din("prevm", (128, NCORE))

    yout = nc.dram_tensor("yout", [TOK, E], F32, kind="ExternalOutput")

    xn1_d = nc.dram_tensor("xn1_d", [TOK, E], BF16)
    phk_d = nc.dram_tensor("phk_d", [TOK, E], BF16)
    phq_d = nc.dram_tensor("phq_d", [128, NSL * TOK], BF16)
    y_d = nc.dram_tensor("y_d", [TOK, E], BF16)
    xd_d = nc.dram_tensor("xd_d", [TOK, E], BF16)
    xn3_d = nc.dram_tensor("xn3_d", [TOK, E], BF16)
    wd_d = nc.dram_tensor("wd_d", [E, E], BF16)
    r2_d = nc.dram_tensor("r2_d", [TOK, E], F32)
    r3_d = nc.dram_tensor("r3_d", [TOK, E], F32)
    cc_in = nc.dram_tensor("cc_in", [128, 1040], BF16)
    cc_out = nc.dram_tensor("cc_out", [128 * NCORE, 1040], BF16)

    dbg_list = []
    if dbg:
        for nm, src in (("d_xn1", xn1_d), ("d_phk", phk_d), ("d_phq", phq_d),
                        ("d_y", y_d), ("d_xd", xd_d), ("d_xn3", xn3_d),
                        ("d_wd", wd_d), ("d_r2", r2_d), ("d_r3", r3_d)):
            dbg_list.append((nc.dram_tensor(nm, list(src.shape), src.dtype,
                                            kind="ExternalOutput"), src))

    b = B(nc)
    for s in ("sY", "sX0", "sX1", "sP0", "sP1", "sP2", "sST0", "sST1",
              "sSW0", "sSW1", "sQ0", "sQ1",
              "sK0", "sK1", "sK2", "sK3", "sCI", "sGA", "sGB", "sCC",
              "sT1", "sT2", "sT3", "sT4", "sT5", "sTW",
              "sPE", "sV", "sA", "sPL"):
        b.sem(s)
    E_ = b.emit

    def mk_sb(stack):
        def sb(name, cols, dt=BF16):
            return stack.enter_context(nc.sbuf_tensor(name, [128, cols], dt))
        return sb


    def finish():
        b.begin()
        for _s in list(b.sems):
            if _s in ("sPE", "sV", "sA", "sPL", "sCC"):
                continue
            E_("sync", "wait_ge", _s, b.cnt[_s])
        b.end()
        return nc
    stS = b.stack
    sbS = mk_sb(stS)

    mask_sb = sbS("mask_sb", 128, F32)
    muk_sb = sbS("muk_sb", NSL, F32); muv_sb = sbS("muv_sb", NSL, F32)
    mur_sb = sbS("mur_sb", NSL, F32); u_sb = sbS("u_sb", NSL, F32)
    eu_sb = sbS("eu_sb", NSL, F32)
    pm_sb = sbS("pm_sb", NCORE, F32)
    ones_t = stS.enter_context(nc.sbuf_tensor("ones_t", [1, 128], BF16))
    stt_s = sbS("stt_s", 128, F32)
    stt_q = sbS("stt_q", 64, F32)
    stm = sbS("stm", 64, F32)
    str_ = sbS("str_", 64, F32)
    sc_a = sbS("sc_a", E, F32)
    sc_b = sbS("sc_b", E, F32)
    sc_c = sbS("sc_c", E, F32)
    sc_d = sbS("sc_d", E, F32)
    sc_d2 = sbS("sc_d2", E, F32)
    sc_j = sbS("sc_j", E, F32)
    qs_t = sbS("qs_t", 32, F32)
    rs_t = sbS("rs_t", 32, F32)
    stb = sbS("stb", 64, F32)
    x_sb = [sbS("x_sb0", E, F32), sbS("x_sb1", E, F32),
            sbS("x_sb2", E, F32)]
    xq_sb = [sbS("xq0", E), sbS("xq1", E)]
    WA = sbS("WA", NSL * E)
    WB = sbS("WB", NSL * E)
    W3 = sbS("W3", NSL * E)

    PS = stS.enter_context(nc.psum_tensor("PS", [128, 4096], F32))
    P = [PS[:, 512 * i:512 * (i + 1)] for i in range(8)]

    def mm(out, lhsT, rhs, start, stop, waits=()):
        E_("tensor", "matmul", out, lhsT, rhs, start=start, stop=stop,
           skip_group_check=True, waits=waits, inc="sPE")

    def wslab(Wt):
        return Wt[:, :].rearrange("p (s o) -> p s o", o=E)

    def wsrc(dram):
        return dram.ap().rearrange("(s p) o -> p s o", p=128)

    lnq_xn = {0: 0, 1: 0}
    lnq_guard = {0: 0, 1: 0}
    lnq_pl = {0: 0, 1: 0}
    xq_half = {}
    pe_half = {}
    xq_store = {0: 0, 1: 0}
    out_store = {0: 0, 1: 0}
    xbp_done = {0: 0, 1: 0, 2: 0}
    srcrd = {0: ("sV", 0), 1: ("sV", 0)}
    qguard_v, qguard_pl, qtail = {}, {}, {}
    qguard_sb = {}
    qguard_cx = [0]
    qst = {}

    def ln_pA(tc, p, qsets, ey=()):
        sa, sb_, scx, qs, rs = qsets[p][:5]
        gk = qsets[p][5]
        dxn = sc_d if p == 0 else sc_d2
        x3 = dxn[:, :].rearrange("p (k m) -> p k m", m=32)
        E_("vector", "tensor_reduce", qs[:, :], x3, AX.X, ALU.max,
           apply_absolute_value=True,
           waits=[("sA", lnq_xn[p])], inc="sV")
        E_("vector", "tensor_scalar", qs[:, :], qs[:, :], 1.0 / 7.5,
           1e-30, ALU.mult, ALU.max, inc="sV")
        E_("vector", "reciprocal", rs[:, :], qs[:, :], inc="sV")
        rdv = b.cnt["sV"]
        E_("gpsimd", "tensor_tensor",
           sa[:, :].rearrange("p (k m) -> p k m", m=32), x3,
           rs[:, :].unsqueeze(2).broadcast_to([128, 32, 32]), ALU.mult,
           waits=[("sV", rdv)] + list(ey),
           inc="sPL")
        yv = b.cnt["sPL"]
        lnq_guard[p] = rdv
        E_("scalar", "activation", sb_[:, :], sa[:, :], AF.Abs,
           waits=[("sPL", yv), ("sPL", qguard_sb.get(gk, 0))], inc="sA")
        qst.setdefault(gk, {}).update(ya=b.cnt["sA"], yv=yv)

    def ln_pB(tc, p, qsets):
        sa, sb_, scx, qs, rs = qsets[p][:5]
        gk = qsets[p][5]
        st = qst[gk]
        E_("vector", "tensor_scalar", scx[:, :], sb_[:, :], 2.0, C1,
           ALU.is_ge, ALU.mult,
           waits=[("sA", st["ya"]), ("sPL", qguard_cx[0])], inc="sV")
        E_("vector", "tensor_scalar", sb_[:, :], sb_[:, :], 4.0, 2.0 * C1,
           ALU.is_ge, ALU.mult, inc="sV")
        E_("vector", "scalar_tensor_tensor", scx[:, :], scx[:, :], C1,
           sb_[:, :], ALU.add, ALU.add, inc="sV")
        E_("gpsimd", "tensor_tensor", sb_[:, :], sa[:, :], scx[:, :],
           ALU.add, waits=[("sV", b.cnt["sV"])], inc="sPL")
        E_("gpsimd", "tensor_tensor", sa[:, :], sb_[:, :], scx[:, :],
           ALU.subtract, inc="sPL")
        qguard_sb[gk] = b.cnt["sPL"]
        qguard_cx[0] = b.cnt["sPL"]
        st["pl"] = b.cnt["sPL"]

    def ln_pC(tc, p, out_dram, qsets):
        sa, sb_, scx, qs, rs = qsets[p][:5]
        gk = qsets[p][5]
        st = qst[gk]
        E_("vector", "tensor_tensor",
           xq_sb[p][:, :].rearrange("p (k m) -> p k m", m=32),
           sa[:, :].rearrange("p (k m) -> p k m", m=32),
           qs[:, :].unsqueeze(2).broadcast_to([128, 32, 32]), ALU.mult,
           waits=[("sPL", st["pl"]),
                  ("sST%d" % p, xq_store.get(p, 0))], inc="sV")
        st["xdv"] = b.cnt["sV"]
        b.dma("scalar", "sST%d" % p,
              out_dram.ap()[128 * tc:128 * tc + 128, :],
              xq_sb[p][:, :], waits=[("sV", b.cnt["sV"])])
        xq_store[p] = b.cnt["sST%d" % p]


    def _quant(src_f32, dst_bf, xn_wait=None, st_wait=None, par=None,
               sset=None):
        sa, sb_, scx, qs, rs = (sc_a, sc_b, sc_c, qs_t, rs_t) if sset is None             else sset[:5]
        x3 = src_f32[:, :].rearrange("p (k m) -> p k m", m=32)
        if st_wait is None:
            entry = [("sST0", b.cnt["sST0"]), ("sST1", b.cnt["sST1"])]
        else:
            entry = [st_wait]
        gk = None if sset is None else sset[5]
        if gk is not None:
            entry += [("sV", qguard_v.get(gk, 0)),
                      ("sPL", qguard_pl.get(gk, 0))]
        else:
            entry += [("sPL", b.cnt["sPL"])]
        entry += [("sA", b.cnt["sA"] if xn_wait is None else xn_wait)]
        E_("vector", "tensor_reduce", qs[:, :], x3, AX.X, ALU.max,
           apply_absolute_value=True, waits=entry, inc="sV")
        E_("vector", "tensor_scalar", qs[:, :], qs[:, :], 1.0 / 7.5,
           1e-30, ALU.mult, ALU.max, inc="sV")
        E_("vector", "reciprocal", rs[:, :], qs[:, :], inc="sV")
        E_("vector", "tensor_tensor",
           sa[:, :].rearrange("p (k m) -> p k m", m=32), x3,
           rs[:, :].unsqueeze(2).broadcast_to([128, 32, 32]), ALU.mult,
           inc="sV")
        yv = ("sV", b.cnt["sV"])
        if par is not None:
            srcrd[par] = yv
        abs_w = [yv]
        if gk is not None:
            abs_w.append(("sPL", qguard_sb[0]))
        E_("scalar", "activation", sb_[:, :], sa[:, :], AF.Abs,
           waits=abs_w, inc="sA")
        ya = b.cnt["sA"]
        t1_w = [("sA", ya)]
        if gk is not None:
            t1_w.append(("sPL", abs_w[1][1]))
        E_("vector", "tensor_scalar", scx[:, :], sb_[:, :], 2.0, C1,
           ALU.is_ge, ALU.mult, waits=t1_w, inc="sV")
        E_("vector", "tensor_scalar", sb_[:, :], sb_[:, :], 4.0, 2.0 * C1,
           ALU.is_ge, ALU.mult, inc="sV")
        E_("vector", "scalar_tensor_tensor", scx[:, :], scx[:, :], C1,
           sb_[:, :], ALU.add, ALU.add, inc="sV")
        chv_ = b.cnt["sV"]
        if gk is not None:
            E_("gpsimd", "tensor_tensor", sb_[:, :], sa[:, :], scx[:, :],
               ALU.add, waits=[("sV", chv_)], inc="sPL")
            E_("gpsimd", "tensor_tensor", sa[:, :], sb_[:, :], scx[:, :],
               ALU.subtract, inc="sPL")
            qguard_sb[0] = b.cnt["sPL"]
            qtail[gk] = (b.cnt["sPL"], sa, qs, dst_bf)
            return
        E_("vector", "tensor_tensor", sb_[:, :], sa[:, :], scx[:, :],
           ALU.add, waits=[yv], inc="sV")
        E_("vector", "tensor_tensor", sa[:, :], sb_[:, :], scx[:, :],
           ALU.subtract, inc="sV")
        E_("vector", "tensor_tensor",
           dst_bf[:, :].rearrange("p (k m) -> p k m", m=32),
           sa[:, :].rearrange("p (k m) -> p k m", m=32),
           qs[:, :].unsqueeze(2).broadcast_to([128, 32, 32]), ALU.mult,
           inc="sV")

    def _quant_tail(gk):
        pl, sa, qs, dst_bf = qtail[gk]
        E_("vector", "tensor_tensor",
           dst_bf[:, :].rearrange("p (k m) -> p k m", m=32),
           sa[:, :].rearrange("p (k m) -> p k m", m=32),
           qs[:, :].unsqueeze(2).broadcast_to([128, 32, 32]), ALU.mult,
           waits=[("sPL", pl)], inc="sV")
        qguard_v[gk] = b.cnt["sV"]
        qguard_pl[gk] = qtail[gk][0]

    st1 = contextlib.ExitStack()
    sb1 = mk_sb(st1)
    A1 = sb1("A1", NSL * TOK)
    vbuf = sb1("vbuf", NCH * 1040)
    ring = sb1("ring", 3 * E)
    pq_out = [sb1("pq_out0", 512), sb1("pq_out1", 512)]
    pq_ring = [sb1("pq_ring%d" % _i, NSL * 256) for _i in range(3)]
    S_loc = sb1("S_loc", NCH * 520)
    s_base = sb1("s_base", 1040, F32)
    s_gt = sb1("s_gt", 1040)
    s_tot = sb1("s_tot", 1040)
    S_op = [sb1("S_op0", 520), sb1("S_op1", 520)]
    A_sb = [sb1("A_sb0", 512), sb1("A_sb1", 512)]

    y_sb = [sb1("y_sb0", E), sb1("y_sb1", E)]
    dn_sb = sb1("dn_sb", 16, F32)
    rd_sb = sb1("rd_sb", 16, F32)

    xnT = A1
    phikT = A1

    b.begin()
    for (dst, src) in ((mask_sb, mask_in), (muk_sb, mu_k), (muv_sb, mu_v),
                       (mur_sb, mu_r), (u_sb, u_pf), (pm_sb, prevm)):
        b.dma("sync", "sY", dst[:, :], src.ap())
    w0 = b.cnt["sY"]
    E_("scalar", "activation", eu_sb[:, :], u_sb[:, :], AF.Exp,
       waits=[("sY", w0)], inc="sA")

    E_("vector", "memset", vbuf[:, :], 1.0, inc="sV")
    v_ones = b.cnt["sV"]
    E_("vector", "memset", ones_t[0:1, :], 1.0, inc="sV")
    b.dma("gpsimd", "sGA", wslab(WA), wsrc(WqT))
    wq_ld = b.cnt["sGA"]
    b.dma("gpsimd", "sGB", wslab(WB), wsrc(WkT))
    wk_ld = b.cnt["sGB"]
    b.end()
    if upto <= 0:
        return finish()

    b.begin()
    xn1_st = {}
    s1_apply = {0: 0, 1: 0, 2: 0}
    s1sem = ["sX0", "sX1", "sP0"]
    for c in range(NCH):
        p = c % 3
        qn = "sync" if c % 2 == 0 else "scalar"
        b.dma(qn, s1sem[p], x_sb[p][:, :],
              xs.ap()[128 * c:128 * c + 128, :],
              waits=[("sA", s1_apply[p])] if c >= 3 else ())
        lw = [(s1sem[p], b.cnt[s1sem[p]])]
        E_("vector", "tensor_scalar", sc_a[:, :], x_sb[p][:, :], 1.0, None,
           ALU.mult, ALU.add, accum_out=stt_s[:, c:c + 1],
           waits=lw, inc="sV")
        E_("scalar", "activation", sc_j[:, :], x_sb[p][:, :], AF.Square,
           accum_out=stt_q[:, c:c + 1], waits=lw, inc="sA")
        wa = b.cnt["sA"]
        cc = slice(c, c + 1)
        E_("vector", "tensor_scalar_mul", stm[:, cc], stt_s[:, cc],
           1.0 / E, inc="sV")
        E_("vector", "tensor_scalar_mul", str_[:, cc], stt_q[:, cc],
           1.0 / E, waits=[("sA", wa)], inc="sV")
        E_("vector", "tensor_tensor", stb[:, cc], stm[:, cc],
           stm[:, cc], ALU.mult, inc="sV")
        E_("vector", "scalar_tensor_tensor", str_[:, cc], str_[:, cc],
           float(EPS_LN), stb[:, cc], ALU.add, ALU.subtract, inc="sV")
        E_("scalar", "activation", str_[:, cc], str_[:, cc],
           AF.Ln, waits=[("sV", b.cnt["sV"])], inc="sA")
        E_("scalar", "activation", str_[:, cc], str_[:, cc],
           AF.Exp, scale=-0.5, inc="sA")
        E_("vector", "scalar_tensor_tensor", stb[:, cc], stm[:, cc],
           -1.0, str_[:, cc], ALU.mult, ALU.mult,
           waits=[("sA", b.cnt["sA"])], inc="sV")
        pq = c % 2
        E_("scalar", "activation", xq_sb[pq][:, :], x_sb[p][:, :],
           AF.Identity,
           bias=stb[:, c:c + 1], scale=str_[:, c:c + 1],
           waits=[("sV", b.cnt["sV"]),
                  ("sSW%d" % pq, b.cnt["sSW%d" % pq])],
           inc="sA")
        s1_apply[p] = b.cnt["sA"]
        b.dma("gpsimd", "sSW%d" % pq, xn1_d.ap()[128 * c:128 * c + 128, :],
              xq_sb[pq][:, :], waits=[("sA", b.cnt["sA"])])
        xn1_st[c] = (b.cnt["sSW0"], b.cnt["sSW1"])
    b.end()
    if upto <= 1:
        return finish()

    b.begin()
    for h in range(2):
        for s in range(NSL):
            b.dma("sync", "sT1",
                  xnT[:, s * TOK + 1024 * h:s * TOK + 1024 * h + 1024],
                  xn1_d.ap()[1024 * h:1024 * h + 1024,
                             128 * s:128 * s + 128], transpose=True,
                  waits=[("sSW0", xn1_st[8 * h + 7][0]),
                         ("sSW1", xn1_st[8 * h + 7][1]),
                         ("sT1", b.cnt["sT1"])] if s == 0 else ())
    xnT_h1 = 8 * 16
    xnT_ok = b.cnt["sT1"]

    evacV, evacA = {}, {}
    prev_exp = 0
    prev_tt = ("sV", 0)
    for oc in range(NSL):
        for tt in range(4):
            bank = (oc * 4 + tt) % 4
            po = (oc * 4 + tt) % 2
            wl = [("sGA", wq_ld), ("sT1", xnT_h1 if tt < 2 else xnT_ok)]
            if bank in evacV:
                wl += [evacV[bank], ("sA", evacA[bank])]
            for s in range(NSL):
                mm(P[bank][:, :],
                   WA[:, s * E + 128 * oc: s * E + 128 * oc + 128],
                   xnT[:, s * TOK + 512 * tt: s * TOK + 512 * tt + 512],
                   start=(s == 0), stop=(s == NSL - 1),
                   waits=wl if s == 0 else ())
            pe_c = b.cnt["sPE"]
            E_("scalar", "activation", sc_a[:, 0:512], P[bank][:, :],
               AF.Relu, scale=-1.0,
               waits=[("sPE", pe_c), prev_tt], inc="sA")
            E_("scalar", "activation", sc_b[:, 0:512], sc_a[:, 0:512], AF.Exp,
               scale=-1.0, inc="sA")
            prev_exp = b.cnt["sA"]
            E_("scalar", "activation", sc_b[:, 512:1024], P[bank][:, :],
               AF.Relu, inc="sA")
            evacA[bank] = b.cnt["sA"]
            E_("vector", "tensor_tensor", pq_out[po][:, :], sc_b[:, 0:512],
               sc_b[:, 512:1024], ALU.add,
               waits=[("sA", evacA[bank]), ("sQ%d" % po, b.cnt["sQ%d" % po])],
               inc="sV")
            evacV[bank] = ("sV", b.cnt["sV"])
            prev_tt = ("sV", b.cnt["sV"])
            b.dma("sync", "sQ%d" % po,
                  phq_d.ap().rearrange("p (c s x) -> p c s x", s=NSL, x=128)[
                      :, 4 * tt:4 * tt + 4, oc, :],
                  pq_out[po][:, :].rearrange("p (c x) -> p c x", x=128),
                  waits=[prev_tt])
    q_pe_done = b.cnt["sPE"]
    phq_all_st = (b.cnt["sQ0"], b.cnt["sQ1"])

    b.dma("gpsimd", "sGA", wslab(WA), wsrc(WvT),
          waits=[("sPE", q_pe_done), ("sGA", b.cnt["sGA"])])
    wv_ld = b.cnt["sGA"]

    sS_first = set()
    phk_half = {}
    s2_pe_half = {}
    ring_pe = {}
    kevacV = {}
    vevacA = {}
    snapshots = {}
    phk_store = {}

    def kv_mms(c):
        bch, n = c // 8, c % 8
        for h in range(NH):
            j = 8 * bch + h // 2
            bank = 4 + j // 4
            col = 128 * (j % 4)
            half = 64 * (h % 2)
            first = (bank, half) not in sS_first
            if first:
                sS_first.add((bank, half))
            wlk = []
            if h == 0:
                wlk = [("sV", kevacV[c]), ("sA", vevacA[c])]
                if n > 0:
                    wlk.append(("sV", snapshots[(bch, n - 1)]))
            mm(P[4 + j // 4][half:half + 64, col:col + 65],
               ring[:, (c % 3) * E + 64 * h:(c % 3) * E + 64 * h + 64],
               vbuf[:, c * 1040 + 65 * h: c * 1040 + 65 * h + 65],
               start=first, stop=(n == 7), waits=wlk)
        return b.cnt["sPE"]

    def snapshot(c):
        bch, n = c // 8, c % 8
        pe_c = b.cnt["sPE"]
        for half in range(2):
            bank = P[4 + 2 * bch + half]
            src = bank[:, 0:512].rearrange("p (j x) -> p j x", x=128)[:, :, 0:65]
            if n < 7:
                dst = S_loc[:, c * 520 + half * 260: c * 520 + half * 260 + 260]
            else:
                dst = s_tot[:, bch * 520 + half * 260:
                            bch * 520 + half * 260 + 260]
            E_("vector", "tensor_copy",
               dst.rearrange("p (j e) -> p j e", e=65), src,
               waits=[("sPE", pe_c)] if half == 0 else (), inc="sV")
        snapshots[(bch, n)] = b.cnt["sV"]

    for c in range(NCH):
        for oh in range(2):
            bank = oh
            wl = [("sGB", wk_ld), ("sT1", xnT_h1 if c < 8 else xnT_ok)]
            if bank in evacV:
                wl += [evacV[bank], ("sA", evacA[bank])]
            for s in range(NSL):
                mm(P[bank][:, :],
                   xnT[:, s * TOK + 128 * c: s * TOK + 128 * c + 128],
                   WB[:, s * E + 512 * oh: s * E + 512 * oh + 512],
                   start=(s == 0), stop=(s == NSL - 1),
                   waits=wl if s == 0 else ())
            pe_c = b.cnt["sPE"]
            wl2 = [("sPE", pe_c), prev_tt]
            if c >= 3 and oh == 0:
                wl2.append(("sPE", ring_pe[c % 3]))
            E_("scalar", "activation", sc_a[:, 0:512], P[bank][:, :],
               AF.Relu, scale=-1.0, waits=wl2, inc="sA")
            E_("scalar", "activation", sc_b[:, 0:512], sc_a[:, 0:512],
               AF.Exp, scale=-1.0, inc="sA")
            prev_exp = b.cnt["sA"]
            E_("scalar", "activation", sc_b[:, 512:1024], P[bank][:, :],
               AF.Relu, inc="sA")
            evacA[bank] = b.cnt["sA"]
            wlr = [("sA", evacA[bank])]
            if c >= 3 and oh == 0:
                wlr.append(("sK%d" % (c % 3), phk_store[c - 3]))
            E_("vector", "tensor_tensor",
               ring[:, (c % 3) * E + 512 * oh:(c % 3) * E + 512 * oh + 512],
               sc_b[:, 0:512], sc_b[:, 512:1024], ALU.add,
               waits=wlr, inc="sV")
            evacV[bank] = ("sV", b.cnt["sV"])
            prev_tt = ("sV", b.cnt["sV"])
        kevacV[c] = b.cnt["sV"]
        b.dma("gpsimd", "sK%d" % (c % 3), phk_d.ap()[128 * c:128 * c + 128, :],
              ring[:, (c % 3) * E:(c % 3) * E + E],
              waits=[("sV", kevacV[c])])
        phk_store[c] = b.cnt["sK%d" % (c % 3)]
        for oh in range(2):
            bank = 2 + oh
            wl = [("sGA", wv_ld), ("sT1", xnT_h1 if c < 8 else xnT_ok),
                  ("sV", v_ones)]
            if bank in evacA:
                wl += [("sA", evacA[bank])]
            for s in range(NSL):
                mm(P[bank][:, :],
                   xnT[:, s * TOK + 128 * c: s * TOK + 128 * c + 128],
                   WA[:, s * E + 512 * oh: s * E + 512 * oh + 512],
                   start=(s == 0), stop=(s == NSL - 1),
                   waits=wl if s == 0 else ())
            pe_c = b.cnt["sPE"]
            dst = vbuf[:, c * 1040 + 520 * oh: c * 1040 + 520 * oh + 520]
            E_("scalar", "activation",
               dst.rearrange("p (h e) -> p h e", e=65)[:, :, 0:64],
               P[bank][:, :].rearrange("p (h e) -> p h e", e=64),
               AF.Copy, waits=[("sPE", pe_c)], inc="sA")
            evacA[bank] = b.cnt["sA"]
        vevacA[c] = b.cnt["sA"]
        if c >= 1:
            ring_pe[(c - 1) % 3] = kv_mms(c - 1)
            snapshot(c - 1)
        if c == 8:
            phk_half[0] = [b.cnt["sK%d" % i] for i in range(3)]
            s2_pe_half[0] = b.cnt["sPE"]
    ring_pe[(NCH - 1) % 3] = kv_mms(NCH - 1)
    snapshot(NCH - 1)
    s2_pe = b.cnt["sPE"]
    phk_half[1] = [b.cnt["sK%d" % i] for i in range(3)]
    s2_pe_half[1] = s2_pe
    phk_all_st = [b.cnt["sK%d" % i] for i in range(4)]
    b.end()
    if upto <= 2:
        return finish()

    b.begin()
    b.dma("scalar", "sCI", cc_in.ap(), s_tot[:, :],
          waits=[("sV", snapshots[(1, 7)])])
    ccin_st = b.cnt["sCI"]
    b.dma("gpsimd", "sGB", wslab(WB), wsrc(WpT),
          waits=[("sPE", s2_pe), ("sGB", b.cnt["sGB"])])
    wp_ld = b.cnt["sGB"]
    b.dma("gpsimd", "sGA", wslab(WA), wsrc(bWkT),
          waits=[("sPE", s2_pe), ("sGA", b.cnt["sGA"])])
    bk_ld = b.cnt["sGA"]
    E_("gpsimd", "collective_compute", "AllGather", ALU.bypass,
       replica_groups=[list(range(NCORE))],
       ins=[cc_in.ap().opt()], outs=[cc_out.ap().opt()],
       waits=[("sCI", ccin_st)], inc="sCC")

    Q4 = {0: (sc_a, sc_b, sc_c, qs_t, rs_t, "s4e"),
          1: (sc_d, sc_j, sc_c, stm[:, 0:32], stm[:, 32:64], "s4o")}
    xq_store[0] = b.cnt["sST0"]
    xq_store[1] = b.cnt["sST1"]
    l4 = {}

    def q4A(oc):
        p = oc % 2
        sa, sb_, scx, qs, rs = Q4[p][:5]
        gk = Q4[p][5]
        x3 = x_sb[p][:, :].rearrange("p (k m) -> p k m", m=32)
        E_("vector", "tensor_reduce", qs[:, :], x3, AX.X, ALU.max,
           apply_absolute_value=True, waits=[l4[oc]], inc="sV")
        E_("vector", "tensor_scalar", qs[:, :], qs[:, :], 1.0 / 7.5,
           1e-30, ALU.mult, ALU.max, inc="sV")
        E_("vector", "reciprocal", rs[:, :], qs[:, :], inc="sV")
        rdv = b.cnt["sV"]
        E_("gpsimd", "tensor_tensor",
           sa[:, :].rearrange("p (k m) -> p k m", m=32), x3,
           rs[:, :].unsqueeze(2).broadcast_to([128, 32, 32]), ALU.mult,
           waits=[("sV", rdv)], inc="sPL")
        yv = b.cnt["sPL"]
        s4y[p] = yv
        E_("scalar", "activation", sb_[:, :], sa[:, :], AF.Abs,
           waits=[("sPL", yv), ("sPL", qguard_sb.get(gk, 0))], inc="sA")
        qst.setdefault(gk, {}).update(ya=b.cnt["sA"], yv=yv)

    s4y = {0: 0, 1: 0}
    for oc in range(NSL):
        p = oc % 2
        b.dma("scalar", "sP%d" % p, x_sb[p][:, :],
              ffnw.ap()[128 * oc:128 * oc + 128, :],
              waits=[("sPL", s4y[p]), ("sA", b.cnt["sA"])]
              if oc >= 2 else [("sV", b.cnt["sV"]), ("sA", b.cnt["sA"])])
        l4[oc] = ("sP%d" % p, b.cnt["sP%d" % p])
        if oc >= 3:
            ln_pC(oc - 3, (oc - 3) % 2, wd_d, Q4)
        if oc >= 1:
            q4A(oc - 1)
        if oc >= 2:
            ln_pB(oc - 2, (oc - 2) % 2, Q4)
    ln_pC(NSL - 3, (NSL - 3) % 2, wd_d, Q4)
    q4A(NSL - 1)
    ln_pB(NSL - 2, (NSL - 2) % 2, Q4)
    ln_pC(NSL - 2, (NSL - 2) % 2, wd_d, Q4)
    ln_pB(NSL - 1, (NSL - 1) % 2, Q4)
    ln_pC(NSL - 1, (NSL - 1) % 2, wd_d, Q4)
    E_("vector", "engine_nop", waits=[("sPL", b.cnt["sPL"])], inc="sV")
    wd_st = (b.cnt["sST0"], b.cnt["sST1"])
    for s in range(NSL):
        b.dma("scalar", "sTW", W3[:, s * E:(s + 1) * E],
              wd_d.ap()[:, 128 * s:128 * s + 128], transpose=True,
              waits=[("sST0", wd_st[0]), ("sST1", wd_st[1])]
              if s == 0 else ())
    w3_ok = b.cnt["sTW"]

    E_("vector", "memset", s_base[:, :], 0.0, inc="sV")
    for cp in range(NCORE):
        b.dma("scalar", "sP0", s_gt[:, :],
              cc_out.ap()[128 * cp:128 * cp + 128, :],
              waits=([("sCC", 1)] if cp == 0 else []) + [("sV", b.cnt["sV"])])
        E_("vector", "scalar_tensor_tensor", s_base[:, :], s_gt[:, :],
           pm_sb[:, cp:cp + 1], s_base[:, :], ALU.mult, ALU.add,
           waits=[("sP0", b.cnt["sP0"])], inc="sV")
    sbase_ok = b.cnt["sV"]
    b.end()
    if upto <= 3:
        return finish()

    b.begin()
    for h in range(2):
        for s in range(NSL):
            b.dma("sync", "sT2",
                  phikT[:, s * TOK + 1024 * h:s * TOK + 1024 * h + 1024],
                  phk_d.ap()[1024 * h:1024 * h + 1024,
                             128 * s:128 * s + 128], transpose=True,
                  waits=[("sK0", phk_half[h][0]), ("sK1", phk_half[h][1]),
                         ("sK2", phk_half[h][2]), ("sT2", b.cnt["sT2"]),
                         ("sPE", s2_pe_half[h])] if s == 0 else ())
    phikT_h1 = 8 * 16
    phikT_ok = b.cnt["sT2"]
    for _r in pq_ring:
        E_("vector", "memset", _r[:, :], 0.0, inc="sV")
    pqz_zero = b.cnt["sV"]

    aevac, nevac, aevac_pl, ydone_prev = {}, {}, {}, {}
    y_half = {}
    s5p1_pe_h1 = 0
    sop_pe = {}
    y_store = {}
    pq_ld = {}
    sop_okd = {}
    snap_last = snapshots[(1, 7)]
    chunks = [8 * bch + n for bch in range(2) for n in range(8)][:s5n]
    if "a" in s5parts:
        glist = [(c, g) for c in chunks for g in range(4)]
    else:
        glist = []

    def pq_load(ci):
        c = chunks[ci]
        slot = ci % 3
        wlq = [("sQ0", phq_all_st[0]), ("sQ1", phq_all_st[1]),
               ("sV", pqz_zero)]
        if ci >= 3:
            wlq.append(("sPE", sop_pe[chunks[ci - 3]]))
        pqv = pq_ring[slot][:, :].rearrange("p (s two x) -> p s two x",
                                            two=2, x=128)
        phv = phq_d.ap().rearrange("p (c s x) -> p c s x", s=NSL, x=128)[
            :, c, :, :]
        sem = "sP%d" % (ci % 3)
        b.dma("sync", sem, pqv[0:64, :, 0, :], phv[0:64],
              waits=wlq + [(sem, b.cnt[sem])])
        b.dma("sync", sem, pqv[64:128, :, 1, :], phv[64:128])
        pq_ld[c] = (sem, b.cnt[sem])

    def chunk_head(ci, c):
        bch, n = c // 8, c % 8
        if ci == 0:
            for cj in range(min(3, len(chunks))):
                pq_load(cj)
        elif ci + 2 < len(chunks):
            pq_load(ci + 2)
        wl = [("sV", sbase_ok)]
        if ci >= 2:
            wl.append(("sPE", sop_pe[chunks[ci - 2]]))
        spar = c % 2
        if n == 0:
            E_("vector", "tensor_copy", S_op[spar][:, :],
               s_base[:, bch * 520:bch * 520 + 520], waits=wl, inc="sV")
        else:
            E_("vector", "tensor_tensor", S_op[spar][:, :],
               s_base[:, bch * 520:bch * 520 + 520],
               S_loc[:, (c - 1) * 520:(c - 1) * 520 + 520],
               ALU.add, waits=wl, inc="sV")
        sop_okd[c] = b.cnt["sV"]

    def emit_A(idx):
        c, g = glist[idx]
        par = chunks.index(c) % 3
        ab = 4 + (idx % 4)
        wlA = [pq_ld[c], ("sV", snap_last),
               ("sT2", phikT_h1 if c < 8 else phikT_ok)]
        if idx >= 4:
            wlA.append(aevac[idx - 4])
        for j in range(4):
            h = 4 * g + j
            s = h // 2
            blk = 2 * s + (h % 2)
            mm(P[ab][:, :][:, 128 * j:128 * j + 128],
               phikT[:, s * TOK + 128 * c: s * TOK + 128 * c + 128],
               pq_ring[par][:, 128 * blk:128 * blk + 128],
               start=(j == 0), stop=(j == 3),
               waits=wlA if j == 0 else ())
        peA = b.cnt["sPE"]
        E_("vector", "tensor_tensor",
           A_sb[idx % 2][:, :].rearrange("p (j x) -> p j x", x=128),
           P[ab][:, :][:, 0:512].rearrange("p (j x) -> p j x", x=128),
           mask_sb[:, :].unsqueeze(1).broadcast_to([128, 4, 128]),
           ALU.mult, waits=[("sPE", peA)], inc="sV")
        aevac[idx] = ("sV", b.cnt["sV"])
        aevac_pl[idx] = b.cnt["sV"]

    def emit_num(idx):
        c, g = glist[idx]
        par = chunks.index(c) % 3
        wlN = [("sV", aevac_pl[idx]), ("sV", sop_okd[c]),
               ("sA", vevacA[c])]
        if g == 0 and c in ydone_prev:
            wlN.append(("sV", ydone_prev[c]))
        for j in range(4):
            h = 4 * g + j
            s = h // 2
            blk = 2 * s + (h % 2)
            mm(PS[:, 128 * h:128 * h + 65],
               A_sb[idx % 2][:, 128 * j:128 * j + 128],
               vbuf[:, c * 1040 + 65 * h: c * 1040 + 65 * h + 65],
               start=(h % 4 == 0), stop=False,
               waits=wlN if j == 0 else ())
            mm(PS[:, 128 * h:128 * h + 65],
               pq_ring[par][:, 128 * blk:128 * blk + 128],
               S_op[c % 2][:, 65 * s:65 * s + 65],
               start=False, stop=(j == 3))
        if g == 3:
            peN = b.cnt["sPE"]
            sop_pe[c] = peN
            pv = PS[:, 0:2048].rearrange("p (h x) -> p h x", x=128)
            E_("vector", "tensor_scalar_add", dn_sb[:, :],
               pv[:, :, 64:65], float(EPS_ATT), waits=[("sPE", peN)],
               inc="sV")
            E_("vector", "reciprocal", rd_sb[:, :], dn_sb[:, :], inc="sV")
            wly = []
            ci = chunks.index(c)
            if ci >= 2:
                wly.append(y_store[chunks[ci - 2]])
            E_("vector", "tensor_tensor",
               y_sb[c % 2][:, :].rearrange("p (h x) -> p h x", x=64),
               pv[:, :, 0:64],
               rd_sb[:, :].unsqueeze(2).broadcast_to([128, 16, 64]),
               ALU.mult, waits=wly, inc="sV")
            if chunks.index(c) + 1 < len(chunks):
                ydone_prev[chunks[chunks.index(c) + 1]] = b.cnt["sV"]
            b.dma("scalar", "sST%d" % (c % 2),
                  y_d.ap()[128 * c:128 * c + 128, :],
                  y_sb[c % 2][:, :], waits=[("sV", b.cnt["sV"])])
            y_store[c] = ("sST%d" % (c % 2), b.cnt["sST%d" % (c % 2)])
            if c == 7:
                y_half[0] = (b.cnt["sST0"], b.cnt["sST1"])
                s5p1_pe_h1 = b.cnt["sPE"]

    for idx in range(len(glist)):
        c, g = glist[idx]
        if g == 0:
            if idx >= 1:
                emit_num(idx - 1)
            chunk_head(chunks.index(c), c)
        emit_A(idx)
        if g != 0 and idx >= 1:
            emit_num(idx - 1)
    if glist:
        emit_num(len(glist) - 1)
    for c in chunks:
        if c not in y_store:
            y_store[c] = ("sST0", 0)
    s5p1_pe = b.cnt["sPE"]
    y_half[1] = (b.cnt["sST0"], b.cnt["sST1"])
    b.end()
    if upto <= 4:
        return finish()
    st1.close()

    st24 = contextlib.ExitStack()
    sb24 = mk_sb(st24)
    out_sb = [sb24("out0", E, F32), sb24("out1", E, F32)]

    st2 = contextlib.ExitStack()
    sb2 = mk_sb(st2)
    A1b = sb2("A1b", NSL * TOK)
    q2a = sb2("q2a", E, F32)
    qs2 = sb2("qs2", 32, F32); rs2 = sb2("rs2", 32, F32)
    QSETS2 = {0: (sc_a, sc_b, sc_c, qs_t, rs_t, "g2e"),
              1: (q2a, out_sb[0], sc_c, qs2, rs2, "g2o")}
    R1 = sb2("R1", NCH * E, F32)
    bh_t = st2.enter_context(nc.sbuf_tensor("bh_t", [1, E], BF16))
    bl_t = st2.enter_context(nc.sbuf_tensor("bl_t", [1, E], BF16))

    def bias_prep(row_dram, ldw):
        b.dma("sync", "sCI", sc_b[0:1, 0:E], row_dram.ap(), waits=ldw)
        ld = b.cnt["sCI"]
        E_("vector", "tensor_copy", bh_t[0:1, :], sc_b[0:1, 0:E],
           waits=[("sCI", ld)], inc="sV")
        E_("vector", "tensor_copy", sc_c[0:1, 0:E], bh_t[0:1, :], inc="sV")
        E_("vector", "tensor_tensor", sc_a[0:1, 0:E], sc_b[0:1, 0:E],
           sc_c[0:1, 0:E], ALU.subtract, inc="sV")
        E_("vector", "tensor_copy", bl_t[0:1, :], sc_a[0:1, 0:E], inc="sV")
        return b.cnt["sV"]


    def ln_part1(sid, tc, p, res_ap, res_dram):
        col = 16 * sid + tc
        wv0 = b.cnt["sV"]
        E_("scalar", "activation", sc_j[:, :], res_ap, AF.Square,
           accum_out=stt_q[:, col % 64:col % 64 + 1], waits=[("sV", wv0)],
           inc="sA")
        wa = b.cnt["sA"]
        if res_dram is not None:
            b.dma("gpsimd", "sSW%d" % p,
                  res_dram.ap()[128 * tc:128 * tc + 128, :],
                  res_ap, waits=[("sV", wv0)])
            out_store[p] = b.cnt["sSW%d" % p]
        c0 = 32 * sid + 2 * tc
        cq = col % 64
        E_("vector", "tensor_scalar_mul", stm[:, cq:cq + 1],
           stt_s[:, c0:c0 + 1], 1.0 / E, inc="sV")
        E_("vector", "tensor_scalar_mul", str_[:, cq:cq + 1],
           stt_q[:, cq:cq + 1], 1.0 / E, waits=[("sA", wa)], inc="sV")
        E_("vector", "scalar_tensor_tensor", str_[:, cq:cq + 1],
           stm[:, cq:cq + 1], stm[:, cq:cq + 1], str_[:, cq:cq + 1],
           ALU.mult, ALU.subtract, inc="sV")
        E_("vector", "tensor_scalar", str_[:, cq:cq + 1],
           str_[:, cq:cq + 1], -1.0, float(EPS_LN), ALU.mult, ALU.add,
           inc="sV")
        E_("scalar", "activation", str_[:, cq:cq + 1], str_[:, cq:cq + 1],
           AF.Ln, waits=[("sV", b.cnt["sV"])], inc="sA")
        E_("scalar", "activation", str_[:, cq:cq + 1], str_[:, cq:cq + 1],
           AF.Exp, scale=-0.5, inc="sA")
        E_("vector", "tensor_scalar", stb[:, cq:cq + 1], stm[:, cq:cq + 1],
           str_[:, cq:cq + 1], -1.0, ALU.mult, ALU.mult,
           waits=[("sA", b.cnt["sA"])], inc="sV")
        dxn = sc_d if p == 0 else sc_d2
        E_("scalar", "activation", dxn[:, :], res_ap, AF.Identity,
           bias=stb[:, cq:cq + 1], scale=str_[:, cq:cq + 1],
           waits=[("sV", b.cnt["sV"]), ("sPL", b.cnt["sPL"]),
                  ("sV", lnq_guard[p])], inc="sA")
        lnq_xn[p] = b.cnt["sA"]

    def ln_part2(tc, p, out_dram, do_quant):
        dxn = sc_d if p == 0 else sc_d2
        if do_quant:
            _quant(dxn, xq_sb[p], xn_wait=lnq_xn[p],
                   st_wait=("sSW%d" % p, b.cnt["sSW%d" % p]), par=p)
            lnq_guard[p] = b.cnt["sV"]
            b.dma("gpsimd", "sSW%d" % p,
                  out_dram.ap()[128 * tc:128 * tc + 128, :],
                  xq_sb[p][:, :], waits=[("sV", b.cnt["sV"])])
        else:
            E_("vector", "tensor_copy", xq_sb[p][:, :], dxn[:, :],
               waits=[("sSW%d" % p, b.cnt["sSW%d" % p]), ("sA", lnq_xn[p])],
               inc="sV")
            lnq_guard[p] = b.cnt["sV"]
            b.dma("gpsimd", "sSW%d" % p,
                  out_dram.ap()[128 * tc:128 * tc + 128, :],
                  xq_sb[p][:, :], waits=[("sV", b.cnt["sV"])])
        xq_store[p] = b.cnt["sSW%d" % p]

    def epilogue_block(sid, lhsT_ar, Wt, extra_w, lhsT_ok, x_dram, bias_t,
                       res_to_R1, res_dram, out_dram, do_quant, qsets=None):
        prev = {}
        if qsets is not None:
            xq_store[0] = b.cnt["sST0"]
            xq_store[1] = b.cnt["sST1"]
        for tc in range(NCH):
            p = tc % 2
            lxw = []
            if x_dram is not None:
                b.dma("sync", "sX%d" % p, x_sb[p][:, :],
                      x_dram.ap()[128 * tc:128 * tc + 128, :],
                      waits=([("sV", prev[pair])] if tc >= 2 else []) +
                            [("sST0", b.cnt["sST0"]),
                             ("sST1", b.cnt["sST1"])])
                lxw = [("sX%d" % p, b.cnt["sX%d" % p])]
                xbp_t = x_sb[p]
            else:
                xbp_t = R1[:, 1024 * tc:1024 * tc + 1024]
            dst = (R1[:, 1024 * tc:1024 * tc + 1024] if res_to_R1
                   else out_sb[p][:, :])
            pair = tc % 2
            for oh in range(2):
                bank = 2 * pair + oh
                wl = [(lhsT_ok[0], lhsT_ok[1] if tc < 8 else lhsT_ok[2])] +                     list(extra_w)
                if tc < 2:
                    wl.append(("sV", bias_ld))
                if pair in prev:
                    wl.append(("sV", prev[pair]))
                for s in range(NSL):
                    mm(P[bank][:, :],
                       lhsT_ar[:, s * TOK + 128 * tc: s * TOK + 128 * tc + 128],
                       Wt[:, s * E + 512 * oh: s * E + 512 * oh + 512],
                       start=(s == 0), stop=False,
                       waits=wl if s == 0 else ())
                mm(P[bank][:, :], ones_t[0:1, :],
                   bh_t[0:1, 512 * oh:512 * oh + 512],
                   start=False, stop=False)
                mm(P[bank][:, :], ones_t[0:1, :],
                   bl_t[0:1, 512 * oh:512 * oh + 512],
                   start=False, stop=True)
            peO = b.cnt["sPE"]
            E_("vector", "scalar_tensor_tensor",
               dst[:, 0:1024], PS[:, 1024 * pair:1024 * pair + 1024], 1.0,
               xbp_t[:, 0:1024], ALU.mult, ALU.add,
               accum_out=stt_s[:, 32 * sid + 2 * tc:32 * sid + 2 * tc + 1],
               waits=[("sPE", peO), ("sSW%d" % p, b.cnt["sSW%d" % p])] + lxw,
               inc="sV")
            prev[pair] = b.cnt["sV"]
            ln_part1(sid, tc, p, dst, res_dram)
            if qsets is not None:
                if tc >= 3:
                    ln_pC(tc - 3, (tc - 3) % 2, out_dram, qsets)
                if tc >= 1:
                    ln_pA(tc - 1, (tc - 1) % 2, qsets)
                if tc >= 2:
                    ln_pB(tc - 2, (tc - 2) % 2, qsets)
            elif tc >= 1:
                ln_part2(tc - 1, (tc - 1) % 2, out_dram, do_quant)
            if tc == 8:
                sfam = "sST" if qsets is not None else "sSW"
                xq_half[0] = (b.cnt[sfam + "0"], b.cnt[sfam + "1"],
                              sfam)
                pe_half[0] = b.cnt["sPE"]
        if qsets is not None:
            ln_pC(NCH - 3, (NCH - 3) % 2, out_dram, qsets)
            ln_pA(NCH - 1, (NCH - 1) % 2, qsets)
            ln_pB(NCH - 2, (NCH - 2) % 2, qsets)
            ln_pC(NCH - 2, (NCH - 2) % 2, out_dram, qsets)
            ln_pB(NCH - 1, (NCH - 1) % 2, qsets)
            ln_pC(NCH - 1, (NCH - 1) % 2, out_dram, qsets)
        else:
            ln_part2(NCH - 1, (NCH - 1) % 2, out_dram, do_quant)
        sfam = "sST" if qsets is not None else "sSW"
        xq_half[1] = (b.cnt[sfam + "0"], b.cnt[sfam + "1"], sfam)
        pe_half[1] = b.cnt["sPE"]

    b.begin()
    g1_drain = [("sST0", b.cnt["sST0"]), ("sST1", b.cnt["sST1"]),
                ("sPE", s5p1_pe), ("sV", b.cnt["sV"]), ("sA", b.cnt["sA"]),
                ("sX0", b.cnt["sX0"]), ("sX1", b.cnt["sX1"])]
    bias_ld = bias_prep(bp_row, g1_drain)
    E_("scalar", "nop", waits=[("sPL", b.cnt["sPL"])], inc="sA")
    for h in range(2):
        for s in range(NSL):
            b.dma("sync", "sT3",
                  A1b[:, s * TOK + 1024 * h:s * TOK + 1024 * h + 1024],
                  y_d.ap()[1024 * h:1024 * h + 1024,
                           128 * s:128 * s + 128], transpose=True,
                  waits=[("sST0", y_half[h][0]), ("sST1", y_half[h][1]),
                         ("sPE", s5p1_pe if h else s5p1_pe_h1),
                         ("sT3", b.cnt["sT3"]),
                         ("sV", b.cnt["sV"]), ("sA", b.cnt["sA"])]
                  if s == 0 else ())
    yT_h1 = 8 * 16
    yT_ok = b.cnt["sT3"]
    epilogue_block(1, A1b, WB, [("sGB", wp_ld)], ("sT3", yT_h1, yT_ok), xs,
                   None,
                   res_to_R1=True, res_dram=None, out_dram=xd_d,
                   do_quant=True, qsets=QSETS2)
    s5p2_pe = b.cnt["sPE"]
    s5p2_pe_h1 = pe_half[0]
    xq_half_52 = dict(xq_half)
    b.end()
    if upto <= 5:
        return finish()

    b.begin()
    b.dma("gpsimd", "sGB", wslab(WB), wsrc(bWvT),
          waits=[("sPE", s5p2_pe), ("sGB", b.cnt["sGB"])])
    bv_ld = b.cnt["sGB"]
    for h in range(2):
        for s in range(NSL):
            b.dma("sync", "sT4",
                  A1b[:, s * TOK + 1024 * h:s * TOK + 1024 * h + 1024],
                  xd_d.ap()[1024 * h:1024 * h + 1024,
                            128 * s:128 * s + 128], transpose=True,
                  waits=[(xq_half_52[h][2] + "0", xq_half_52[h][0]),
                         (xq_half_52[h][2] + "1", xq_half_52[h][1]),
                         ("sT4", b.cnt["sT4"]),
                         ("sPE", s5p2_pe if h else s5p2_pe_h1)]
                  if s == 0 else ())
    xdT_h1 = 8 * 16
    xdT_ok = b.cnt["sT4"]
    bias_ld = bias_prep(fb_row, [("sV", b.cnt["sV"]), ("sA", b.cnt["sA"]),
                                 ("sPL", b.cnt["sPL"])])
    E_("vector", "engine_nop", waits=[("sPL", b.cnt["sPL"])], inc="sV")
    epilogue_block(2, A1b, W3, [("sTW", w3_ok)], ("sT4", xdT_h1, xdT_ok),
                   None, None,
                   res_to_R1=False, res_dram=r2_d, out_dram=xn3_d,
                   do_quant=False)
    s7_pe = b.cnt["sPE"]
    s7_v = b.cnt["sV"]
    s7_a = b.cnt["sA"]
    b.end()
    if upto <= 6:
        return finish()
    st2.close()

    st3 = contextlib.ExitStack()
    sb3 = mk_sb(st3)
    km_ct = sb3("km_ct", NSL * 1024)
    vm_ct = sb3("vm_ct", NSL * 1024)
    rm_ct = sb3("rm_ct", NSL * 1024)
    WrS = sb3("WrS", NSL * E)
    WoS = sb3("WoS", NSL * E)
    E0 = sb3("E0", 512, F32); E1 = sb3("E1", 512, F32)
    EK0 = sb3("EK0", 512, F32); EK1 = sb3("EK1", 512, F32)
    T01 = sb3("T01", 1024, F32)

    xn3T = [T01[:, :].bitcast(BF16)[:, 0:1024]]
    d_sb = EK1[:, :].bitcast(BF16)[:, 0:512]
    yT_ct = sb3("yT_ct", NSL * 1024)

    b.begin()
    s9_st0, s9_st1 = b.cnt["sSW0"], b.cnt["sSW1"]
    xn3_all_st = (b.cnt["sSW0"], b.cnt["sSW1"])
    b.dma("gpsimd", "sGA", wslab(WrS), wsrc(bWrT),
          waits=[("sV", s7_v), ("sGA", b.cnt["sGA"])])
    br_ld = b.cnt["sGA"]
    b.dma("gpsimd", "sGB", wslab(WoS), wsrc(bWoT),
          waits=[("sGB", b.cnt["sGB"])])
    bo_ld = b.cnt["sGB"]

    yct_pe = 0
    epi_prev = {}
    for a in range(2):
        mix_ok = {}
        for s in range(NSL):
            p = 0
            wtr = [("sSW0", xn3_all_st[0]), ("sSW1", xn3_all_st[1]),
                   ("sPE", s7_pe), ("sV", b.cnt["sV"]), ("sA", b.cnt["sA"]),
                   ("sPL", b.cnt["sPL"]), ("sST0", b.cnt["sST0"]),
                   ("sST1", b.cnt["sST1"])]
            if s >= 1:
                wtr = [("sV", mix_ok[s - 1])]
            b.dma("sync", "sQ%d" % p, xn3T[p][:, 0:512],
                  xn3_d.ap()[512 * a:512 * a + 512, 128 * s:128 * s + 128],
                  transpose=True, waits=wtr)
            b.dma("sync", "sQ%d" % p, xn3T[p][:, 512:1024],
                  xn3_d.ap()[1024 + 512 * a:1024 + 512 * a + 512,
                             128 * s:128 * s + 128], transpose=True)
            ltr = [("sQ%d" % p, b.cnt["sQ%d" % p])]
            if s == 0:
                ltr.append(("sPE", s7_pe if a == 0 else yct_pe))
            E_("vector", "tensor_tensor", d_sb, xn3T[p][:, 512:1024],
               xn3T[p][:, 0:512], ALU.subtract, waits=ltr, inc="sV")
            for (mu_t, dst_) in ((muk_sb, km_ct), (muv_sb, vm_ct),
                                 (mur_sb, rm_ct)):
                E_("vector", "tensor_scalar_mul",
                   dst_[:, s * 1024:s * 1024 + 512], xn3T[p][:, 0:512],
                   mu_t[:, s:s + 1], inc="sV")
                E_("vector", "scalar_tensor_tensor",
                   dst_[:, s * 1024 + 512:s * 1024 + 1024], d_sb,
                   mu_t[:, s:s + 1], xn3T[p][:, 0:512], ALU.mult, ALU.add,
                   inc="sV")
            mix_ok[s] = b.cnt["sV"]
        mixes_done = b.cnt["sV"]

        wkv_prev = None
        wkv_done = ("sV", 0, "sPL", 0)
        for oc in range(NSL):
            for gi, (Wt, Mt, wld, wsem) in enumerate(
                    ((WA, km_ct, bk_ld, "sGA"), (WB, vm_ct, bv_ld, "sGB"),
                     (WrS, rm_ct, br_ld, "sGA"))):
                for bb in range(2):
                    bank = 2 * gi + bb
                    wl = [(wsem, wld), ("sV", mixes_done)]
                    if wkv_prev:
                        wl.append(wkv_prev[bank])
                    if a == 0 and oc == 0:
                        wl.append(("sA", s7_a))
                    for s in range(NSL):
                        mm(P[bank][:, :],
                           Wt[:, s * E + 128 * oc: s * E + 128 * oc + 128],
                           Mt[:, s * 1024 + 512 * bb:
                              s * 1024 + 512 * bb + 512],
                           start=(s == 0), stop=(s == NSL - 1),
                           waits=wl if s == 0 else ())
            peKVR = b.cnt["sPE"]
            eu_c = eu_sb[:, oc:oc + 1]
            E_("scalar", "activation", E0[:, :], P[0][:, :], AF.Exp,
               waits=[("sPE", peKVR), ("sV", wkv_done[1]),
                      ("sPL", wkv_done[3])], inc="sA")
            E_("scalar", "activation", E1[:, :], P[1][:, :], AF.Exp,
               inc="sA")
            p01_free = b.cnt["sA"]
            E_("scalar", "activation", T01[:, 512:1024], P[4][:, :], AF.Exp,
               scale=-1.0, inc="sA")
            E_("scalar", "activation", T01[:, 0:512], P[5][:, :], AF.Exp,
               scale=-1.0, inc="sA")
            p45_free = b.cnt["sA"]
            wa = b.cnt["sA"]
            E_("vector", "tensor_tensor", EK1[:, :], E1[:, :], P[3][:, :],
               ALU.mult, waits=[("sA", p01_free)], inc="sV")
            p3_free = b.cnt["sV"]
            E_("vector", "tensor_tensor", EK0[:, :], E0[:, :], P[2][:, :],
               ALU.mult, inc="sV")
            E_("vector", "tensor_scalar_add", T01[:, 512:1024],
               T01[:, 512:1024], 1.0, waits=[("sA", wa)], inc="sV")
            E_("vector", "scalar_tensor_tensor", E1[:, :], E1[:, :], eu_c,
               E0[:, :], ALU.mult, ALU.add, inc="sV")
            e1_ok = b.cnt["sV"]
            E_("gpsimd", "tensor_tensor", E0[:, :], E1[:, :], T01[:, 0:512],
               ALU.mult, waits=[("sV", e1_ok), ("sA", wa)], inc="sPL")
            E_("gpsimd", "tensor_tensor", T01[:, 0:512], E1[:, :], E0[:, :],
               ALU.add, inc="sPL")
            t01lo_pl = b.cnt["sPL"]
            E_("vector", "reciprocal", T01[:, 512:1024], T01[:, 512:1024],
               inc="sV")
            E_("vector", "scalar_tensor_tensor", EK0[:, :], EK1[:, :], eu_c,
               EK0[:, :], ALU.mult, ALU.add, inc="sV")
            E_("vector", "tensor_tensor",
               yT_ct[:, oc * 1024:oc * 1024 + 512],
               P[2][:, :], T01[:, 512:1024], ALU.mult, inc="sV")
            E_("vector", "reciprocal", T01[:, 0:512], T01[:, 0:512],
               waits=[("sPL", t01lo_pl)], inc="sV")
            rlo_ok = b.cnt["sV"]
            E_("gpsimd", "tensor_tensor",
               yT_ct[:, oc * 1024 + 512:oc * 1024 + 1024], EK0[:, :],
               T01[:, 0:512], ALU.mult, waits=[("sV", rlo_ok)], inc="sPL")
            p2_free = b.cnt["sV"]
            bank_free = {0: ("sA", p01_free), 1: ("sA", p01_free),
                         2: ("sV", p2_free), 3: ("sV", p3_free),
                         4: ("sA", p45_free), 5: ("sA", p45_free)}
            wkv_prev = dict(bank_free)
            wkv_done = ("sV", b.cnt["sV"], "sPL", b.cnt["sPL"])

        tcs = [4 * a + i for i in range(4)] + [8 + 4 * a + i for i in range(4)]
        r2st = (s9_st0, s9_st1)
        s9stt = {}
        km_f = km_ct[:, :].bitcast(F32)
        QS9 = {0: (sc_a, sc_b, sc_c, qs_t, rs_t, "g3e"),
               1: (km_f[:, 0:1024], km_f[:, 1088:2112], sc_c,
                   km_f[:, 1024:1056],
                   km_f[:, 1056:1088], "g3o")}
        km_pe = b.cnt["sPE"]
        eyw = [("sPE", km_pe)]
        xq_store[0] = b.cnt["sST0"]
        xq_store[1] = b.cnt["sST1"]
        for ti, tc in enumerate(tcs):
            p = tc % 2
            b.dma("sync", "sX%d" % p, x_sb[p][:, :],
                  r2_d.ap()[128 * tc:128 * tc + 128, :],
                  waits=([("sV", s9stt.get(p, 0))] if ti >= 2 else []) +
                        [("sSW0", r2st[0]), ("sSW1", r2st[1])])
            lxw = [("sX%d" % p, b.cnt["sX%d" % p])]
            bb = tc // 8
            lcol = 512 * bb + 128 * ((tc % 8) - 4 * a)
            for oh in range(2):
                bank = 6 + oh
                wl = [("sV", wkv_done[1]), ("sPL", wkv_done[3]),
                      ("sGB", bo_ld)]
                if 6 in epi_prev:
                    wl.append(("sV", epi_prev[6]))
                for s in range(NSL):
                    mm(P[bank][:, :],
                       yT_ct[:, s * 1024 + lcol: s * 1024 + lcol + 128],
                       WoS[:, s * E + 512 * oh: s * E + 512 * oh + 512],
                       start=(s == 0), stop=(s == NSL - 1),
                       waits=wl if s == 0 else ())
            peO = b.cnt["sPE"]
            E_("vector", "scalar_tensor_tensor",
               out_sb[p][:, 0:1024], PS[:, 3072:4096], 1.0,
               x_sb[p][:, 0:1024], ALU.mult, ALU.add,
               accum_out=stt_s[:, 96 + 2 * tc:96 + 2 * tc + 1],
               waits=[("sPE", peO), ("sSW%d" % p, b.cnt["sSW%d" % p])] + lxw,
               inc="sV")
            epi_prev[6] = b.cnt["sV"]
            s9stt[p] = b.cnt["sV"]
            ln_part1(3, tc, p, out_sb[p][:, :], r3_d)
            if ti >= 3:
                ln_pC(tcs[ti - 3], tcs[ti - 3] % 2, xd_d, QS9)
            if ti >= 1:
                ln_pA(tcs[ti - 1], tcs[ti - 1] % 2, QS9, ey=eyw)
            if ti >= 2:
                ln_pB(tcs[ti - 2], tcs[ti - 2] % 2, QS9)
        ln_pC(tcs[5], tcs[5] % 2, xd_d, QS9)
        ln_pA(tcs[7], tcs[7] % 2, QS9, ey=eyw)
        ln_pB(tcs[6], tcs[6] % 2, QS9)
        ln_pC(tcs[6], tcs[6] % 2, xd_d, QS9)
        ln_pB(tcs[7], tcs[7] % 2, QS9)
        ln_pC(tcs[7], tcs[7] % 2, xd_d, QS9)
        yct_pe = b.cnt["sPE"]
    s9_pe = b.cnt["sPE"]
    s9_v = b.cnt["sV"]
    s9_a = b.cnt["sA"]
    b.end()
    if upto <= 7:
        return finish()
    st3.close()

    st4 = contextlib.ExitStack()
    sb4 = mk_sb(st4)
    A1c = sb4("A1c", NSL * TOK)
    bh4 = st4.enter_context(nc.sbuf_tensor("bh4", [1, E], BF16))
    bl4 = st4.enter_context(nc.sbuf_tensor("bl4", [1, E], BF16))

    b.begin()
    b.dma("sync", "sCI", sc_b[0:1, 0:E], fb_row.ap(),
          waits=[("sV", s9_v), ("sA", s9_a), ("sPL", b.cnt["sPL"])])
    _l4 = b.cnt["sCI"]
    E_("vector", "tensor_copy", bh4[0:1, :], sc_b[0:1, 0:E],
       waits=[("sCI", _l4)], inc="sV")
    E_("vector", "tensor_copy", sc_c[0:1, 0:E], bh4[0:1, :], inc="sV")
    E_("vector", "tensor_tensor", sc_a[0:1, 0:E], sc_b[0:1, 0:E],
       sc_c[0:1, 0:E], ALU.subtract, inc="sV")
    E_("vector", "tensor_copy", bl4[0:1, :], sc_a[0:1, 0:E], inc="sV")
    fb2_ld = b.cnt["sV"]
    xd2_st = (b.cnt["sST0"], b.cnt["sST1"])
    for s in range(NSL):
        b.dma("sync", "sT5", A1c[:, s * TOK:(s + 1) * TOK],
              xd_d.ap()[:, 128 * s:128 * s + 128], transpose=True,
              waits=[("sST0", xd2_st[0]), ("sST1", xd2_st[1]),
                     ("sPE", s9_pe), ("sV", s9_v), ("sPL", b.cnt["sPL"]),
                     ("sA", s9_a)] if s == 0 else ())
    xdT2_ok = b.cnt["sT5"]
    f2_prev = {}
    s10_st = (b.cnt["sSW0"], b.cnt["sSW1"])
    for tc in range(NCH):
        p = tc % 2
        b.dma("sync", "sX%d" % p, x_sb[p][:, :],
              r3_d.ap()[128 * tc:128 * tc + 128, :],
              waits=([("sV", f2_prev[p])] if tc >= 2 else []) +
                    [("sSW0", s10_st[0]), ("sSW1", s10_st[1])])
        lxw = [("sX%d" % p, b.cnt["sX%d" % p])]
        pair = tc % 2
        for oh in range(2):
            bank = 2 * pair + oh
            wl = [("sT5", xdT2_ok)]
            if tc < 2:
                wl.append(("sV", fb2_ld))
            if pair in f2_prev:
                wl.append(("sV", f2_prev[pair]))
            for s in range(NSL):
                mm(P[bank][:, :],
                   A1c[:, s * TOK + 128 * tc: s * TOK + 128 * tc + 128],
                   W3[:, s * E + 512 * oh: s * E + 512 * oh + 512],
                   start=(s == 0), stop=False,
                   waits=wl if s == 0 else ())
            mm(P[bank][:, :], ones_t[0:1, :],
               bh4[0:1, 512 * oh:512 * oh + 512],
               start=False, stop=False)
            mm(P[bank][:, :], ones_t[0:1, :],
               bl4[0:1, 512 * oh:512 * oh + 512],
               start=False, stop=True)
        peF = b.cnt["sPE"]
        osem0 = "sSW0" if p == 0 else "sST1"
        E_("vector", "tensor_tensor",
           out_sb[p][:, 0:1024], PS[:, 1024 * pair:1024 * pair + 1024],
           x_sb[p][:, 0:1024], ALU.add,
           waits=[("sPE", peF), (osem0, b.cnt[osem0])] + lxw,
           inc="sV")
        f2_prev[pair] = b.cnt["sV"]
        oq, osem = (("gpsimd", "sSW0") if p == 0 else ("scalar", "sST1"))
        b.dma(oq, osem, yout.ap()[128 * tc:128 * tc + 128, :],
              out_sb[p][:, :], waits=[("sV", b.cnt["sV"])])
        out_store[p] = b.cnt[osem]
    for (dst, src) in dbg_list:
        b.dma("sync", "sY", dst.ap(), src.ap(),
              waits=[("sSW0", b.cnt["sSW0"]), ("sSW1", b.cnt["sSW1"]),
                     ("sST0", b.cnt["sST0"]), ("sST1", b.cnt["sST1"])])
    for _s in ("sT1", "sT2", "sT3", "sT4", "sT5", "sTW", "sQ0", "sQ1",
               "sK0", "sK1", "sK2", "sK3", "sCI", "sX0", "sX1", "sP0", "sP1", "sP2"):
        E_("sync", "wait_ge", _s, b.cnt[_s])
    E_("sync", "wait_ge", "sST0", b.cnt["sST0"])
    E_("sync", "wait_ge", "sST1", b.cnt["sST1"])
    E_("sync", "wait_ge", "sSW0", b.cnt["sSW0"])
    E_("sync", "wait_ge", "sSW1", b.cnt["sSW1"])
    E_("sync", "wait_ge", "sY", b.cnt["sY"])
    E_("sync", "wait_ge", "sGA", b.cnt["sGA"])
    E_("sync", "wait_ge", "sGB", b.cnt["sGB"])
    b.end()
    st4.close()
    st24.close()

    b.stack.close()
    return nc


_NC = {}


def _in_maps(ins):
    f32 = np.float32
    x = np.ascontiguousarray(ins["L0_attn_a_x"], dtype=f32)
    shared = {
        "WqT": np.ascontiguousarray(np.asarray(ins["L0_attn_a_Wq"], f32).T),
        "WkT": np.ascontiguousarray(np.asarray(ins["L0_attn_a_Wk"], f32).T),
        "WvT": np.ascontiguousarray(np.asarray(ins["L0_attn_a_Wv"], f32).T),
        "WpT": np.ascontiguousarray(np.asarray(ins["L0_attn_a_Wp"], f32).T),
        "bWkT": np.ascontiguousarray(np.asarray(ins["L0_attn_b_Wk"], f32).T),
        "bWvT": np.ascontiguousarray(np.asarray(ins["L0_attn_b_Wv"], f32).T),
        "bWrT": np.ascontiguousarray(np.asarray(ins["L0_attn_b_Wr"], f32).T),
        "bWoT": np.ascontiguousarray(np.asarray(ins["L0_attn_b_W_out"], f32).T),
        "ffnw": np.ascontiguousarray(np.asarray(ins["L0_ffn_weight"], f32)),
        "bp_row": np.asarray(ins["L0_attn_a_bp"], f32).reshape(1, E).copy(),
        "fb_row": np.asarray(ins["L0_ffn_bias"], f32).reshape(1, E).copy(),
        "mu_k_pf": np.ascontiguousarray(
            np.asarray(ins["L0_attn_b_mu_k"], f32).reshape(NSL, 128).T),
        "mu_v_pf": np.ascontiguousarray(
            np.asarray(ins["L0_attn_b_mu_v"], f32).reshape(NSL, 128).T),
        "mu_r_pf": np.ascontiguousarray(
            np.asarray(ins["L0_attn_b_mu_r"], f32).reshape(NSL, 128).T),
        "u_pf": np.ascontiguousarray(
            np.asarray(ins["L0_attn_b_u"], f32).reshape(NSL, 128).T),
        "mask_sc": np.ascontiguousarray(np.triu(np.ones((128, 128), f32))),
    }
    in_maps = []
    for c in range(NCORE):
        m = dict(shared)
        m["xs"] = np.ascontiguousarray(
            x[:, 1024 * c:1024 * (c + 1), :].reshape(TOK, E))
        pm = np.zeros((128, NCORE), f32)
        pm[:, :c] = 1.0
        m["prevm"] = pm
        in_maps.append(m)
    return in_maps, x.shape


def kernel(**ins):
    in_maps, (B_, T_, E_sz) = _in_maps(ins)
    if "k" not in _NC:
        _NC["k"] = build()
    res = run_bass_kernel_spmd(_NC["k"], in_maps, core_ids=list(range(NCORE)))
    out = np.empty((B_, T_, E_sz), np.float32)
    for c in range(NCORE):
        r = np.asarray(res.results[c]["yout"])
        out[0, 1024 * c:1024 * (c + 1), :] = r[0:1024]
        out[1, 1024 * c:1024 * (c + 1), :] = r[1024:2048]
    return out
